# version 06
# baseline (speedup 1.0000x reference)
import numpy as np
import ml_dtypes
from contextlib import ExitStack

import concourse.bass as bass
import concourse.mybir as mybir
from concourse.bass_utils import run_bass_kernel_spmd

F32 = mybir.dt.float32
BF16 = mybir.dt.bfloat16
I32 = mybir.dt.int32
AF = mybir.ActivationFunctionType
ALU = mybir.AluOpType

B, L, V = 4, 4096, 4096
TOK = 2048
DIN, DFF, ITR = 512, 2048, 11008
NLAY = 4
Q, NB = 128, 16
C4 = DIN // 128
FC16 = DFF // 128
VC32 = V // 128
IC = ITR // 128
HG = 4
EPS = 1e-5
NCORES = 8


class Sem:
    def __init__(self, nc, ctx, name):
        self.h = ctx.enter_context(nc.semaphore(name))
        self.n = 0

    def inc(self, instr, k=1):
        instr.then_inc(self.h, k)
        self.n += k
        return self.n


def _w(eng, sem, val):
    if val > 0:
        eng.wait_ge(sem.h, val)


def build(dbg=()):
    nc = bass.Bass("TRN2", target_bir_lowering=False, debug=False)
    ap = lambda t: t.ap()

    x_in = ap(nc.dram_tensor("x", [TOK, V], F32, kind="ExternalInput"))
    winT = ap(nc.dram_tensor("winT", [128, VC32 * DIN], F32, kind="ExternalInput"))
    w1T = ap(nc.dram_tensor("w1T", [NLAY, 128, C4 * DFF], F32, kind="ExternalInput"))
    w2T = ap(nc.dram_tensor("w2T", [NLAY, 128, FC16 * DIN], F32, kind="ExternalInput"))
    woT = ap(nc.dram_tensor("woT", [HG, C4, 128, 1024], F32, kind="ExternalInput"))
    wg_i = ap(nc.dram_tensor("wg_i", [IC, 128, 4096], I32, kind="ExternalInput"))
    wu_i = ap(nc.dram_tensor("wu_i", [IC, 128, 4096], I32, kind="ExternalInput"))
    wd_i = ap(nc.dram_tensor("wd_i", [HG, IC, 128, 1024], I32, kind="ExternalInput"))
    pv = {}
    for name, k in [("bin", C4), ("l1w", 16), ("l1b", 16), ("l2w", 16), ("l2b", 16),
                    ("scl", 16), ("phr", 16), ("phi", 16), ("lcr", 16), ("lci", 16),
                    ("b2", 16), ("lnlw", C4), ("lnlb", C4), ("b1", NLAY * FC16),
                    ("sg", IC), ("su", IC), ("sd", 32), ("mv", 1), ("mb", 1), ("eps", 1)]:
        pv[name] = ap(nc.dram_tensor("p_" + name, [128, k], F32, kind="ExternalInput"))
    ident_b = ap(nc.dram_tensor("ident_b", [128, 128], BF16, kind="ExternalInput"))
    ident_f = ap(nc.dram_tensor("ident_f", [128, 128], F32, kind="ExternalInput"))
    om_in = ap(nc.dram_tensor("om", [128, 128], F32, kind="ExternalInput"))
    sel_in = ap(nc.dram_tensor("sel", [8, 1], F32, kind="ExternalInput"))
    bout_in = ap(nc.dram_tensor("bout", [1, V], F32, kind="ExternalInput"))
    ones_in = ap(nc.dram_tensor("onesr", [1, 512], BF16, kind="ExternalInput"))
    out_t = ap(nc.dram_tensor("out", [TOK, V], F32, kind="ExternalOutput"))

    wg_bf = ap(nc.dram_tensor("wg_bf", [IC, 128, 4096], BF16))
    wu_bf = ap(nc.dram_tensor("wu_bf", [IC, 128, 4096], BF16))
    wd_bf = ap(nc.dram_tensor("wd_bf", [HG, IC, 128, 1024], BF16))
    xnt = nc.dram_tensor("xnt", [VC32, 128, TOK], BF16).ap().rearrange("v p t -> p v t")
    cc_in = ap(nc.dram_tensor("cc_in", [1, 1024], F32))
    cc_out = ap(nc.dram_tensor("cc_out", [8, 1024], F32))

    dbg_aps = {}
    for name in dbg:
        shp = {"h_a": [128, C4 * TOK], "h_f": [128, C4 * TOK], "xn0": [128, V],
               "y0": [128, TOK], "cv0": [128, TOK], "pc": [128, 32],
               "st0": [128, 16]}[name]
        dt = {"xn0": BF16}.get(name, F32)
        dbg_aps[name] = ap(nc.dram_tensor("dbg_" + name, shp, dt, kind="ExternalOutput"))

    with ExitStack() as ctx:
        sCC = Sem(nc, ctx, "sCC")
        sMM = Sem(nc, ctx, "sMM")
        sDV = Sem(nc, ctx, "sDV")
        sSC = Sem(nc, ctx, "sSC")
        sPRM = Sem(nc, ctx, "sPRM")
        sCVg = Sem(nc, ctx, "sCVg")
        sCVu = Sem(nc, ctx, "sCVu")
        sCVd = Sem(nc, ctx, "sCVd")
        sONE = Sem(nc, ctx, "sONE")
        sW1 = Sem(nc, ctx, "sW1")
        sW2 = Sem(nc, ctx, "sW2")
        sCCD = Sem(nc, ctx, "sCCD")
        sX = [Sem(nc, ctx, "sX0"), Sem(nc, ctx, "sX1")]
        sSL = [Sem(nc, ctx, "sSL0"), Sem(nc, ctx, "sSL1")]
        sXNT = Sem(nc, ctx, "sXNT")
        sWG = [Sem(nc, ctx, "sWG0"), Sem(nc, ctx, "sWG1")]
        sWU = [Sem(nc, ctx, "sWU0"), Sem(nc, ctx, "sWU1")]
        sWD = [Sem(nc, ctx, "sWDa"), Sem(nc, ctx, "sWDb")]
        sWO = [Sem(nc, ctx, "sWO0"), Sem(nc, ctx, "sWO1")]
        sXS = Sem(nc, ctx, "sXS")
        sOS = [Sem(nc, ctx, "sOS0"), Sem(nc, ctx, "sOS1")]
        sDBG = Sem(nc, ctx, "sDBG")
        sGP = Sem(nc, ctx, "sGP")
        sy, sc, ve, te, gp = nc.sync, nc.scalar, nc.vector, nc.tensor, nc.gpsimd

        sb = lambda n, shp, dt=F32: ctx.enter_context(nc.sbuf_tensor(n, shp, dt))
        P = {k: sb("sp_" + k, [128, v.shape[-1]]) for k, v in pv.items()}
        identb = sb("identb", [128, 128], BF16)
        identf = sb("identf", [128, 128], F32)
        omt = sb("omt", [128, 128], F32)
        selsb = sb("selsb", [8, 1], F32)
        onesb = sb("onesb", [1, 512], BF16)
        hf = sb("hf", [128, C4 * TOK], BF16)
        lcsr = sb("lcsr", [128, 16], F32)
        lcsi = sb("lcsi", [128, 16], F32)
        pcr = sb("pcr", [128, 16], F32)
        pci = sb("pci", [128, 16], F32)
        tmp16a = sb("tmp16a", [128, 16], F32)
        tmp16b = sb("tmp16b", [128, 16], F32)
        tmp16c = sb("tmp16c", [128, 16], F32)
        tmp16d = sb("tmp16d", [128, 16], F32)
        tmp16e = sb("tmp16e", [128, 16], F32)

        ps = ctx.enter_context(nc.psum_tensor("ps", [128, 8 * 512], F32))
        pbank = lambda b: ps[:, b * 512:(b + 1) * 512]
        pbank_b16 = lambda b: ps[:, :].bitcast(BF16)[:, b * 1024:(b + 1) * 1024]
        bank_free = [(sDV, 0)] * 8

        def bank_wait(b):
            s_, v_ = bank_free[b]
            _w(te, s_, v_)

        for k in P:
            sPRM.inc(sy.dma_start(out=P[k][:, :], in_=pv[k]), 16)
        sPRM.inc(sy.dma_start(out=identb[:, :], in_=ident_b), 16)
        sPRM.inc(sy.dma_start(out=identf[:, :], in_=ident_f), 16)
        sPRM.inc(sy.dma_start(out=omt[:, :], in_=om_in), 16)
        sPRM.inc(sy.dma_start(out=selsb[:, :], in_=sel_in), 16)
        params_ready = sPRM.inc(sy.dma_start(out=onesb[:, :], in_=ones_in), 16)

        CHUNK = 4 * 1024 * 1024

        def conv_chunks(sem, dst, src, n_elem, c0=0, c1=10**9):
            dims = len(src.shape)
            names = " ".join(f"a{i}" for i in range(dims))
            sflat = src.rearrange(f"{names} -> ({names})")
            dflat = dst.rearrange(f"{names} -> ({names})")
            v = 0
            ncall = (n_elem + CHUNK - 1) // CHUNK
            for ci in range(max(0, c0), min(ncall, c1)):
                o = ci * CHUNK
                e = min(n_elem, o + CHUNK)
                v = sem.inc(gp.dma_start(out=dflat[o:e], in_=sflat[o:e],
                                         max_dma_last_dim=8192), 16)
            return v

        NGE = IC * 128 * 4096
        wg_ready = wu_ready = wd_ready = 0


        def vfence(instr):
            v = sDV.inc(instr)
            ve.wait_ge(sDV.h, v)
            return v

        def sfence(instr):
            v = sSC.inc(instr)
            sc.wait_ge(sSC.h, v)
            return v

        _w(ve, sPRM, params_ready)
        ve.tensor_tensor(out=tmp16a[:, :], in0=P["phr"][:, :], in1=P["phr"][:, :], op=ALU.mult)
        vfence(ve.tensor_tensor(out=tmp16b[:, :], in0=P["phi"][:, :], in1=P["phi"][:, :], op=ALU.mult))
        r2_done = sDV.inc(ve.tensor_tensor(out=tmp16c[:, :], in0=tmp16a[:, :],
                                           in1=tmp16b[:, :], op=ALU.add))
        _w(sc, sDV, r2_done)
        sfence(sc.activation(out=tmp16a[:, :], in_=tmp16c[:, :], func=AF.Ln))
        sfence(sc.activation(out=tmp16b[:, :], in_=tmp16a[:, :], func=AF.Exp, scale=0.5))
        sc.activation(out=tmp16d[:, :], in_=tmp16a[:, :], func=AF.Exp, scale=-0.5)
        mag_done = sSC.inc(sc.activation(out=tmp16e[:, :], in_=tmp16b[:, :], func=AF.Tanh))
        _w(ve, sSC, mag_done)
        vfence(ve.tensor_tensor(out=tmp16c[:, :], in0=tmp16e[:, :], in1=tmp16d[:, :], op=ALU.mult))
        ve.tensor_tensor(out=pcr[:, :], in0=P["phr"][:, :], in1=tmp16c[:, :], op=ALU.mult)
        ve.tensor_tensor(out=pci[:, :], in0=P["phi"][:, :], in1=tmp16c[:, :], op=ALU.mult)
        ve.tensor_tensor(out=lcsr[:, :], in0=P["lcr"][:, :], in1=P["scl"][:, :], op=ALU.mult)
        pcv = sDV.inc(ve.tensor_tensor(out=lcsi[:, :], in0=P["lci"][:, :], in1=P["scl"][:, :], op=ALU.mult))
        if "pc" in dbg_aps:
            _w(sy, sDV, pcv)
            sDBG.inc(sy.dma_start(out=dbg_aps["pc"][:, 0:16], in_=pcr[:, :]), 16)
            sDBG.inc(sy.dma_start(out=dbg_aps["pc"][:, 16:32], in_=pci[:, :]), 16)

        with ExitStack() as abctx:
            absb = lambda n, shp, dt=F32: abctx.enter_context(nc.sbuf_tensor(n, shp, dt))
            h = absb("h", [128, C4 * TOK], F32)
            hsl = lambda c, a, b_: h[:, c * TOK + a:c * TOK + b_]

            with ExitStack() as actx:
                actx.enter_context(nc.named_scope("stageA"))
                asb = lambda n, shp, dt=F32: actx.enter_context(nc.sbuf_tensor(n, shp, dt))
                xbuf = [asb("xa", [128, V]), asb("xb", [128, V])]
                xnb = [asb("xna", [128, V], BF16), asb("xnb2", [128, V], BF16)]
                stg = [asb("stga", [128, VC32 * 512], BF16),
                       asb("stgb", [128, VC32 * 512], BF16)]
                winTb = asb("winTb", [128, VC32 * DIN], BF16)
                stat = asb("stat", [128, 16], F32)
                junkf = asb("junkf", [128, V // 2], F32)
                winT_ready = sONE.inc(gp.dma_start(out=winTb[:, :], in_=winT,
                                                   max_dma_last_dim=8192), 16)
                wg_ready = conv_chunks(sCVg, wg_bf, wg_i, NGE)
                _w(te, sPRM, params_ready)

                xbuf_free = [(sDV, 0), (sDV, 0)]
                xnb_free = [(sMM, 0), (sMM, 0)]
                stg_free = [(sMM, 0), (sMM, 0)]
                stg_wr = [0, 0]
                tp_bank_free = [(sDV, 0), (sDV, 0)]
                stg_evac = [0, 0]
                h_a_done = 0

                for tc in range(16):
                    bi = tc % 2
                    sl = tc // 4
                    sbl = sl % 2
                    st_off = (tc % 4) * 128
                    s_, v_ = xbuf_free[bi]
                    _w(sy, s_, v_)
                    x_ld = sX[bi].inc(sy.dma_start(out=xbuf[bi][:, :],
                                                   in_=x_in[tc * 128:(tc + 1) * 128, :]), 16)
                    def vs_(instr):
                        v = sDV.inc(instr)
                        ve.wait_ge(sDV.h, v)
                        return v

                    _w(ve, sX[bi], x_ld)
                    _w(ve, sPRM, params_ready)
                    vs_(ve.tensor_reduce(out=stat[:, 8 * bi + 0:8 * bi + 1],
                                         in_=xbuf[bi][:, :], axis=mybir.AxisListType.X,
                                         op=ALU.add))
                    _w(sc, sX[bi], x_ld)
                    _w(sc, sPRM, params_ready)
                    sqA = sSC.inc(sc.activation(out=junkf[:, :], in_=xbuf[bi][:, 0:V // 2],
                                                func=AF.Square))
                    _w(ve, sSC, sqA)
                    rA = vs_(ve.tensor_reduce(out=stat[:, 8 * bi + 1:8 * bi + 2],
                                              in_=junkf[:, :], axis=mybir.AxisListType.X,
                                              op=ALU.add))
                    _w(sc, sDV, rA)
                    sqB = sSC.inc(sc.activation(out=junkf[:, :], in_=xbuf[bi][:, V // 2:V],
                                                func=AF.Square))
                    _w(ve, sSC, sqB)
                    vs_(ve.tensor_reduce(out=stat[:, 8 * bi + 3:8 * bi + 4],
                                         in_=junkf[:, :], axis=mybir.AxisListType.X,
                                         op=ALU.add))
                    ve.tensor_tensor(out=stat[:, 8 * bi + 6:8 * bi + 7],
                                     in0=stat[:, 8 * bi + 1:8 * bi + 2],
                                     in1=stat[:, 8 * bi + 3:8 * bi + 4], op=ALU.add)
                    vs_(ve.tensor_scalar(out=stat[:, 8 * bi + 2:8 * bi + 3],
                                         in0=stat[:, 8 * bi + 0:8 * bi + 1],
                                         scalar1=1.0 / V, scalar2=None, op0=ALU.mult))
                    ve.tensor_scalar(out=stat[:, 8 * bi + 3:8 * bi + 4],
                                     in0=stat[:, 8 * bi + 6:8 * bi + 7],
                                     scalar1=1.0 / V, scalar2=None, op0=ALU.mult)
                    vs_(ve.tensor_tensor(out=stat[:, 8 * bi + 4:8 * bi + 5],
                                         in0=stat[:, 8 * bi + 2:8 * bi + 3],
                                         in1=stat[:, 8 * bi + 2:8 * bi + 3], op=ALU.mult))
                    var_done = sDV.inc(ve.tensor_tensor(
                        out=stat[:, 8 * bi + 5:8 * bi + 6],
                        in0=stat[:, 8 * bi + 3:8 * bi + 4],
                        in1=stat[:, 8 * bi + 4:8 * bi + 5], op=ALU.subtract))
                    _w(sc, sDV, var_done)
                    lnv = sSC.inc(sc.activation(out=stat[:, 8 * bi + 6:8 * bi + 7],
                                                in_=stat[:, 8 * bi + 5:8 * bi + 6],
                                                func=AF.Ln, bias=P["eps"][:, 0:1]))
                    sc.wait_ge(sSC.h, lnv)
                    rstd_done = sSC.inc(sc.activation(
                        out=stat[:, 8 * bi + 7:8 * bi + 8],
                        in_=stat[:, 8 * bi + 6:8 * bi + 7], func=AF.Exp, scale=-0.5))
                    s_, v_ = xnb_free[bi]
                    _w(ve, s_, v_)
                    _w(ve, sSC, rstd_done)
                    xn_done = sDV.inc(ve.tensor_scalar(
                        out=xnb[bi][:, :], in0=xbuf[bi][:, :],
                        scalar1=stat[:, 8 * bi + 2:8 * bi + 3],
                        scalar2=stat[:, 8 * bi + 7:8 * bi + 8],
                        op0=ALU.subtract, op1=ALU.mult))
                    xbuf_free[bi] = (sDV, xn_done)
                    if "xn0" in dbg_aps and tc == 0:
                        _w(sy, sDV, xn_done)
                        tv = sDBG.inc(sy.dma_start(out=dbg_aps["xn0"], in_=xnb[bi][:, :]), 16)
                        _w(ve, sDBG, tv)
                        _w(sc, sDBG, tv)
                    if "st0" in dbg_aps and tc == 0:
                        _w(sy, sDV, xn_done)
                        tv = sDBG.inc(sy.dma_start(out=dbg_aps["st0"], in_=stat[:, :]), 16)
                        _w(ve, sDBG, tv)
                        _w(sc, sDBG, tv)
                    last_tp = 0
                    for g in range(4):
                        tb = g % 2
                        s_, v_ = tp_bank_free[tb]
                        _w(te, s_, v_)
                        _w(te, sDV, xn_done)
                        for k in range(8):
                            vc = g * 8 + k
                            instr = te.transpose(pbank_b16(4 + tb)[:, k * 128:(k + 1) * 128],
                                                 xnb[bi][:, vc * 128:(vc + 1) * 128],
                                                 identb[:, :])
                        tp_done = sMM.inc(instr)
                        last_tp = tp_done
                        if g == 0:
                            s_, v_ = stg_free[sbl]
                            _w(ve, s_, v_)
                            _w(ve, sSL[sbl], stg_wr[sbl])
                        _w(ve, sMM, tp_done)
                        dst = stg[sbl][:, :].rearrange("p (v t) -> p v t", t=512)
                        src = pbank_b16(4 + tb)[:, :].rearrange("p (k t) -> p k t", t=128)
                        ev = sDV.inc(ve.tensor_copy(
                            out=dst[:, g * 8:(g + 1) * 8, st_off:st_off + 128], in_=src))
                        tp_bank_free[tb] = (sDV, ev)
                        stg_evac[sbl] = ev
                    xnb_free[bi] = (sMM, last_tp)

                    if tc % 4 == 3:
                        _w(sy, sDV, stg_evac[sbl])
                        stg_wr[sbl] = sSL[sbl].inc(sy.dma_start(
                            out=xnt[:, :, sl * 512:(sl + 1) * 512],
                            in_=stg[sbl][:, :].rearrange("p (v t) -> p v t", t=512)), 16)
                        _w(te, sONE, winT_ready)
                        _w(te, sDV, stg_evac[sbl])
                        win_done = [0] * C4
                        for mc in range(C4):
                            bank_wait(mc)
                            for vc in range(VC32):
                                instr = te.matmul(
                                    pbank(mc),
                                    winTb[:, vc * DIN + mc * 128:vc * DIN + mc * 128 + 128],
                                    stg[sbl][:, vc * 512:(vc + 1) * 512],
                                    start=(vc == 0), stop=(vc == VC32 - 1))
                            win_done[mc] = sMM.inc(instr)
                        for mc in range(C4):
                            _w(ve, sMM, win_done[mc])
                            h_a_done = sDV.inc(ve.tensor_scalar(
                                out=h[:, mc * TOK + sl * 512:mc * TOK + (sl + 1) * 512],
                                in0=pbank(mc), scalar1=P["bin"][:, mc:mc + 1],
                                scalar2=None, op0=ALU.add))
                            bank_free[mc] = (sDV, h_a_done)
                        stg_free[sbl] = (sMM, win_done[C4 - 1])

                if "h_a" in dbg_aps:
                    _w(sy, sDV, h_a_done)
                    tv = sDBG.inc(sy.dma_start(out=dbg_aps["h_a"], in_=h[:, :]), 16)
                    _w(ve, sDBG, tv)
                for eng in (ve, sc, te, gp, sy):
                    _w(eng, sDV, sDV.n)
                    _w(eng, sSC, sSC.n)
                    _w(eng, sMM, sMM.n)
                    for fam in (sX[0], sX[1], sSL[0], sSL[1]):
                        _w(eng, fam, fam.n)
                    _w(eng, sPRM, sPRM.n)
                    _w(eng, sONE, winT_ready)

            with ExitStack() as bctx:
                bctx.enter_context(nc.named_scope("stageB"))
                bsb = lambda n, shp, dt=F32: bctx.enter_context(nc.sbuf_tensor(n, shp, dt))
                zA = bsb("zA", [128, 2 * TOK])
                zB = bsb("zB", [128, 2 * TOK])
                zAr, zAi = zA[:, 0:TOK], zA[:, TOK:2 * TOK]
                zBr, zBi = zB[:, 0:TOK], zB[:, TOK:2 * TOK]
                y2buf = zA[:, :].bitcast(BF16)
                convr = bsb("convr", [128, C4 * TOK])
                y2in = convr[:, :].bitcast(BF16)[:, 0:C4 * TOK]
                y_c = bsb("y_c", [128, TOK])
                mu_all = bsb("mu_all", [128, TOK])
                rstd_all = bsb("rstd_all", [128, TOK])
                sqb = [bsb("sq0", [128, 512]), bsb("sq1", [128, 512])]
                varb = bsb("varb", [128, 512])
                m2b = bsb("m2b", [128, 512])
                w1t = bsb("w1t", [128, C4 * DFF], BF16)
                w2t = bsb("w2t", [128, FC16 * DIN], BF16)
                P1r = bsb("P1r", [128, C4 * Q])
                P1i = bsb("P1i", [128, C4 * Q])
                nP1i = bsb("nP1i", [128, C4 * Q])
                PBr = bsb("PBr", [128, C4 * NB])
                PBi = bsb("PBi", [128, C4 * NB])
                S0r = bsb("S0r", [128, C4 * NB])
                S0i = bsb("S0i", [128, C4 * NB])
                E2r = bsb("E2r", [128, NB]); E2i = bsb("E2i", [128, NB])
                T2r = bsb("T2r", [128, NB]); T2i = bsb("T2i", [128, NB])
                tA = bsb("tA", [128, 512]); tB = bsb("tB", [128, 512])
                Sinr4 = bsb("Sinr4", [128, C4 * NB])
                Sini4 = bsb("Sini4", [128, C4 * NB])
                ccrow = bsb("ccrow", [1, 1024], F32)
                Gsb = bsb("Gsb", [8, 1024], F32)
                Ssel8 = bsb("Ssel8", [128, 8])
                Ic2 = bsb("Ic2", [128, 32])

                w1_ready = [0] * NLAY
                w2_ready = [0] * NLAY
                w1_lastmm = [0] * NLAY
                w2_lastmm = [0] * NLAY
                tI1 = w1t[:, :].bitcast(F32)[:, 0:TOK]
                tI2 = w1t[:, :].bitcast(F32)[:, TOK:2 * TOK]
                wu_ready = conv_chunks(sCVu, wu_bf, wu_i, NGE, 0, 5)

                def ln_stats():
                    _w(te, sDV, sDV.n)
                    _w(sc, sDV, sDV.n)
                    rstd_v = 0
                    for t4 in range(4):
                        bmu, bsq = 4 + (t4 % 2), 6 + (t4 % 2)
                        sq_mm = [0, 0]
                        for c in range(C4):
                            pbi = c % 2
                            _w(sc, sMM, sq_mm[pbi])
                            sqv = sSC.inc(sc.activation(
                                out=sqb[pbi][:, :], in_=hsl(c, t4 * 512, (t4 + 1) * 512),
                                func=AF.Square))
                            if c == 0:
                                bank_wait(bmu)
                            te.matmul(pbank(bmu), omt[:, :],
                                      hsl(c, t4 * 512, (t4 + 1) * 512),
                                      start=(c == 0), stop=(c == C4 - 1))
                            if c == 0:
                                bank_wait(bsq)
                            _w(te, sSC, sqv)
                            sq_mm[pbi] = sMM.inc(te.matmul(pbank(bsq), omt[:, :],
                                                           sqb[pbi][:, :],
                                                           start=(c == 0), stop=(c == C4 - 1)))
                        sq_grp = sq_mm[(C4 - 1) % 2]
                        _w(sc, sMM, sq_grp)
                        muc = sSC.inc(sc.copy(out=mu_all[:, t4 * 512:(t4 + 1) * 512],
                                              in_=pbank(bmu)))
                        _w(ve, sSC, muc)
                        _w(ve, sMM, sq_grp)
                        ve.tensor_tensor(out=m2b[:, :],
                                         in0=mu_all[:, t4 * 512:(t4 + 1) * 512],
                                         in1=mu_all[:, t4 * 512:(t4 + 1) * 512], op=ALU.mult)
                        var_v = sDV.inc(ve.tensor_tensor(out=varb[:, :], in0=pbank(bsq),
                                                         in1=m2b[:, :], op=ALU.subtract))
                        bank_free[bmu] = (sSC, muc)
                        bank_free[bsq] = (sDV, var_v)
                        _w(sc, sDV, var_v)
                        sc.activation(out=m2b[:, :], in_=varb[:, :], func=AF.Ln,
                                      bias=P["eps"][:, 0:1])
                        rstd_v = sSC.inc(sc.activation(
                            out=rstd_all[:, t4 * 512:(t4 + 1) * 512],
                            in_=m2b[:, :], func=AF.Exp, scale=-0.5))
                    return rstd_v

                def vsync(instr):
                    v = sDV.inc(instr)
                    ve.wait_ge(sDV.h, v)
                    return v

                def ssync(instr):
                    v = sSC.inc(instr)
                    sc.wait_ge(sSC.h, v)
                    return v

                def ctab_all(Tr, Ti, width):
                    k = 1
                    while k < width:
                        n = min(k, width - k)
                        last = None
                        for c in range(C4):
                            o = c * width
                            cr, ci = Tr[:, o + k - 1:o + k], Ti[:, o + k - 1:o + k]
                            ve.tensor_scalar(out=tA[:, c * 64:c * 64 + n],
                                             in0=Tr[:, o:o + n], scalar1=cr,
                                             scalar2=None, op0=ALU.mult)
                            ve.tensor_scalar(out=tB[:, c * 64:c * 64 + n],
                                             in0=Ti[:, o:o + n], scalar1=ci,
                                             scalar2=None, op0=ALU.mult)
                            ve.tensor_scalar(out=tA[:, 256 + c * 64:256 + c * 64 + n],
                                             in0=Tr[:, o:o + n], scalar1=ci,
                                             scalar2=None, op0=ALU.mult)
                            last = ve.tensor_scalar(out=tB[:, 256 + c * 64:256 + c * 64 + n],
                                                    in0=Ti[:, o:o + n], scalar1=cr,
                                                    scalar2=None, op0=ALU.mult)
                        vsync(last)
                        for c in range(C4):
                            o = c * width
                            ve.tensor_tensor(out=Tr[:, o + k:o + k + n],
                                             in0=tA[:, c * 64:c * 64 + n],
                                             in1=tB[:, c * 64:c * 64 + n], op=ALU.subtract)
                            last = ve.tensor_tensor(out=Ti[:, o + k:o + k + n],
                                                    in0=tA[:, 256 + c * 64:256 + c * 64 + n],
                                                    in1=tB[:, 256 + c * 64:256 + c * 64 + n],
                                                    op=ALU.add)
                        vsync(last)
                        k *= 2

                for l in range(NLAY):
                    lc = lambda t, c: P[t][:, l * 4 + c:l * 4 + c + 1]
                    _w(ve, sMM, sMM.n)
                    _w(sc, sMM, sMM.n)

                    rstd1 = ln_stats()
                    last = None
                    for c in range(C4):
                        ve.tensor_copy(out=P1r[:, c * Q:c * Q + 1],
                                       in_=pcr[:, l * 4 + c:l * 4 + c + 1])
                        last = ve.tensor_copy(out=P1i[:, c * Q:c * Q + 1],
                                              in_=pci[:, l * 4 + c:l * 4 + c + 1])
                    vsync(last)
                    ctab_all(P1r, P1i, Q)
                    ve.tensor_scalar(out=nP1i[:, :], in0=P1i[:, :], scalar1=-1.0,
                                     scalar2=None, op0=ALU.mult)
                    last = None
                    for c in range(C4):
                        ve.tensor_copy(out=PBr[:, c * NB:c * NB + 1],
                                       in_=P1r[:, c * Q + Q - 1:c * Q + Q])
                        last = ve.tensor_copy(out=PBi[:, c * NB:c * NB + 1],
                                              in_=P1i[:, c * Q + Q - 1:c * Q + Q])
                    vsync(last)
                    ctab_all(PBr, PBi, NB)
                    for c in range(C4):
                        P1r_c = P1r[:, c * Q:(c + 1) * Q]
                        P1i_c = P1i[:, c * Q:(c + 1) * Q]
                        nP1i_c = nP1i[:, c * Q:(c + 1) * Q]
                        PBr_c = PBr[:, c * NB:(c + 1) * NB]
                        PBi_c = PBi[:, c * NB:(c + 1) * NB]
                        S0r_c = S0r[:, c * NB:(c + 1) * NB]
                        S0i_c = S0i[:, c * NB:(c + 1) * NB]
                        convr_c = convr[:, c * TOK:(c + 1) * TOK]
                        _w(ve, sSC, rstd1)
                        ve.tensor_tensor(out=y_c[:, :], in0=hsl(c, 0, TOK),
                                         in1=mu_all[:, :], op=ALU.subtract)
                        ve.tensor_tensor(out=y_c[:, :], in0=y_c[:, :],
                                         in1=rstd_all[:, :], op=ALU.mult)
                        yv = sDV.inc(ve.tensor_scalar(out=y_c[:, :], in0=y_c[:, :],
                                                      scalar1=lc("l1w", c),
                                                      scalar2=lc("l1b", c),
                                                      op0=ALU.mult, op1=ALU.add))
                        _w(sc, sDV, yv)
                        sil_v = sSC.inc(sc.activation(out=y_c[:, :], in_=y_c[:, :],
                                                      func=AF.Silu))
                        _w(ve, sSC, sil_v)
                        yscl = sDV.inc(ve.tensor_scalar(out=y_c[:, :], in0=y_c[:, :],
                                                        scalar1=lc("scl", c), scalar2=None,
                                                        op0=ALU.mult))
                        if "y0" in dbg_aps and l == 0 and c == 0:
                            ydbg = sDV.inc(ve.nop())
                            _w(sy, sDV, ydbg)
                            tv = sDBG.inc(sy.dma_start(out=dbg_aps["y0"], in_=y_c[:, :]), 16)
                            _w(ve, sDBG, tv)
                        y3 = y_c[:, :].rearrange("p (b q) -> p b q", q=Q)
                        c3r = zBr.rearrange("p (b q) -> p b q", q=Q)
                        c3i = zBi.rearrange("p (b q) -> p b q", q=Q)
                        i3a = tI1[:, :].rearrange("p (b q) -> p b q", q=Q)
                        i3b = tI2[:, :].rearrange("p (b q) -> p b q", q=Q)
                        ve.scalar_tensor_tensor(out=c3r[:, :, 1:Q], in0=y3[:, :, 0:Q - 1],
                                                scalar=P1r_c[:, 0:1], in1=y3[:, :, 1:Q],
                                                op0=ALU.mult, op1=ALU.add)
                        vstep = vsync(ve.tensor_copy(out=c3r[:, :, 0:1], in_=y3[:, :, 0:1]))
                        _w(sc, sDV, yscl)
                        sstep = sSC.inc(sc.activation(out=c3i[:, :, 1:Q],
                                                      in_=y3[:, :, 0:Q - 1], func=AF.Copy,
                                                      scale=P1i_c[:, 0:1]))
                        _w(gp, sSC, sstep)
                        gstep = sGP.inc(gp.memset(c3i[:, :, 0:1], 0.0))
                        pairs = [(zAr, zAi), (zBr, zBi)]
                        cur = 1
                        s = 2
                        while s < Q:
                            last = (s * 2 >= Q)
                            nxt = 1 - cur
                            cr_, ci_ = pairs[cur]
                            nr_ = convr_c if last else pairs[nxt][0]
                            ni_ = pairs[nxt][1]
                            a3r = cr_.rearrange("p (b q) -> p b q", q=Q)
                            a3i = ci_.rearrange("p (b q) -> p b q", q=Q)
                            b3r = nr_.rearrange("p (b q) -> p b q", q=Q)
                            b3i = ni_.rearrange("p (b q) -> p b q", q=Q)
                            _w(sc, sDV, vstep)
                            _w(sc, sGP, gstep)
                            sc.activation(out=i3a[:, :, s:Q], in_=a3r[:, :, 0:Q - s],
                                          func=AF.Copy, scale=P1i_c[:, s - 1:s])
                            sstep = sSC.inc(sc.activation(out=i3b[:, :, s:Q],
                                                          in_=a3i[:, :, 0:Q - s],
                                                          func=AF.Copy,
                                                          scale=P1r_c[:, s - 1:s]))
                            _w(ve, sGP, gstep)
                            ve.scalar_tensor_tensor(out=b3r[:, :, s:Q],
                                                    in0=a3r[:, :, 0:Q - s],
                                                    scalar=P1r_c[:, s - 1:s],
                                                    in1=a3r[:, :, s:Q],
                                                    op0=ALU.mult, op1=ALU.add)
                            ve.scalar_tensor_tensor(out=b3r[:, :, s:Q],
                                                    in0=a3i[:, :, 0:Q - s],
                                                    scalar=nP1i_c[:, s - 1:s],
                                                    in1=b3r[:, :, s:Q],
                                                    op0=ALU.mult, op1=ALU.add)
                            vstep = vsync(ve.tensor_copy(out=b3r[:, :, 0:s],
                                                         in_=a3r[:, :, 0:s]))
                            _w(gp, sSC, sstep)
                            g1 = sGP.inc(gp.tensor_tensor(out=b3i[:, :, s:Q],
                                                          in0=i3a[:, :, s:Q],
                                                          in1=a3i[:, :, s:Q], op=ALU.add))
                            gp.wait_ge(sGP.h, g1)
                            gp.tensor_tensor(out=b3i[:, :, s:Q], in0=i3b[:, :, s:Q],
                                             in1=b3i[:, :, s:Q], op=ALU.add)
                            g2 = sGP.inc(gp.tensor_copy(out=b3i[:, :, 0:s],
                                                        in_=a3i[:, :, 0:s]))
                            gp.wait_ge(sGP.h, g2)
                            gstep = g2
                            cur = nxt
                            s *= 2
                        zfi = pairs[cur][1]
                        cv3 = convr_c.rearrange("p (b q) -> p b q", q=Q)
                        zi3 = zfi.rearrange("p (b q) -> p b q", q=Q)
                        _w(ve, sGP, gstep)
                        ve.tensor_copy(out=E2r[:, :], in_=cv3[:, :, Q - 1:Q])
                        vsync(ve.tensor_copy(out=E2i[:, :], in_=zi3[:, :, Q - 1:Q]))
                        srcs = [(E2r, E2i), (T2r, T2i)]
                        cu = 0
                        s = 1
                        while s < NB:
                            last = (s * 2 >= NB)
                            sr, si_ = srcs[cu]
                            dr, di_ = (S0r_c, S0i_c) if last else srcs[1 - cu]
                            ve.scalar_tensor_tensor(out=dr[:, s:NB], in0=sr[:, 0:NB - s],
                                                    scalar=PBr_c[:, s - 1:s],
                                                    in1=sr[:, s:NB],
                                                    op0=ALU.mult, op1=ALU.add)
                            ve.scalar_tensor_tensor(out=di_[:, s:NB], in0=sr[:, 0:NB - s],
                                                    scalar=PBi_c[:, s - 1:s],
                                                    in1=si_[:, s:NB],
                                                    op0=ALU.mult, op1=ALU.add)
                            vsync(ve.tensor_scalar(out=tA[:, 0:NB - s],
                                                   in0=si_[:, 0:NB - s],
                                                   scalar1=PBi_c[:, s - 1:s], scalar2=None,
                                                   op0=ALU.mult))
                            ve.tensor_tensor(out=dr[:, s:NB], in0=dr[:, s:NB],
                                             in1=tA[:, 0:NB - s], op=ALU.subtract)
                            ve.scalar_tensor_tensor(out=di_[:, s:NB], in0=si_[:, 0:NB - s],
                                                    scalar=PBr_c[:, s - 1:s],
                                                    in1=di_[:, s:NB],
                                                    op0=ALU.mult, op1=ALU.add)
                            ve.tensor_copy(out=dr[:, 0:s], in_=sr[:, 0:s])
                            vsync(ve.tensor_copy(out=di_[:, 0:s], in_=si_[:, 0:s]))
                            cu = 1 - cu
                            s *= 2
                        s0c = sDV.n
                        _w(te, sDV, s0c)
                        bank_wait(6)
                        te.transpose(pbank(6)[0:1, 0:128], S0r_c[:, NB - 1:NB], identf[:, :])
                        tpv = sMM.inc(te.transpose(pbank(6)[0:1, 128:256],
                                                   S0i_c[:, NB - 1:NB], identf[:, :]))
                        _w(ve, sMM, tpv)
                        ve.tensor_copy(out=ccrow[0:1, c * 128:(c + 1) * 128],
                                       in_=pbank(6)[0:1, 0:128])
                        ccv = sDV.inc(ve.tensor_copy(
                            out=ccrow[0:1, 512 + c * 128:512 + (c + 1) * 128],
                            in_=pbank(6)[0:1, 128:256]))
                        bank_free[6] = (sDV, ccv)

                    _w(gp, sDV, sDV.n)
                    _w(gp, sSC, sSC.n)
                    cc_wr = sCCD.inc(gp.dma_start(out=cc_in, in_=ccrow[0:1, :]), 16)
                    _w(gp, sCCD, cc_wr)
                    cc_done = sCC.inc(gp.collective_compute(
                        "AllGather", ALU.bypass, replica_groups=[list(range(NCORES))],
                        ins=[cc_in.opt()], outs=[cc_out.opt()]))
                    _w(gp, sCC, cc_done)
                    g_ld = sCCD.inc(gp.dma_start(out=Gsb[:, :], in_=cc_out), 16)
                    if l > 0:
                        _w(gp, sMM, w1_lastmm[l - 1])
                        _w(gp, sMM, w2_lastmm[l - 1])
                    w1_ready[l] = sW1.inc(gp.dma_start(out=w1t[:, :], in_=w1T[l],
                                                       max_dma_last_dim=8192), 16)
                    w2_ready[l] = sW2.inc(gp.dma_start(out=w2t[:, :], in_=w2T[l],
                                                       max_dma_last_dim=8192), 16)
                    NWDE = HG * IC * 128 * 1024
                    if l == 0:
                        wu_ready = conv_chunks(sCVu, wu_bf, wu_i, NGE, 5, 11)
                    elif l == 1:
                        wd_ready = conv_chunks(sCVd, wd_bf, wd_i, NWDE, 0, 6)
                    elif l == 2:
                        wd_ready = conv_chunks(sCVd, wd_bf, wd_i, NWDE, 6, 11)

                    _w(te, sCCD, g_ld)
                    bank_wait(7)
                    for c in range(C4):
                        te.matmul(pbank(7)[:, c:c + 1], Gsb[0:8, c * 128:(c + 1) * 128],
                                  selsb[:, :], start=True, stop=True)
                        instr = te.matmul(pbank(7)[:, 4 + c:5 + c],
                                          Gsb[0:8, 512 + c * 128:512 + (c + 1) * 128],
                                          selsb[:, :], start=True, stop=True)
                    selv = sMM.inc(instr)
                    _w(ve, sMM, selv)
                    sv2 = vsync(ve.tensor_copy(out=Ssel8[:, :], in_=pbank(7)[:, 0:8]))
                    bank_free[7] = (sDV, sv2)
                    L4 = slice(l * 4, l * 4 + 4)
                    P2r4 = PBr[:, :].rearrange("p (c n) -> p c n", n=NB)[:, :, NB - 1:NB]
                    P2i4 = PBi[:, :].rearrange("p (c n) -> p c n", n=NB)[:, :, NB - 1:NB]
                    I_r = Ic2[:, 0:4]; I_i = Ic2[:, 4:8]; nI_i = Ic2[:, 8:12]
                    t1r = Ic2[:, 12:16]; t1i = Ic2[:, 16:20]
                    m1 = Ic2[:, 20:24]; m2 = Ic2[:, 24:28]; m3 = Ic2[:, 28:32]
                    p2r_f = tA[:, 0:4]; p2i_f = tA[:, 4:8]
                    ve.tensor_copy(out=p2r_f.rearrange("p (c n) -> p c n", n=1), in_=P2r4)
                    vsync(ve.tensor_copy(out=p2i_f.rearrange("p (c n) -> p c n", n=1), in_=P2i4))
                    ve.tensor_tensor(out=m1, in0=p2r_f, in1=lcsr[:, L4], op=ALU.mult)
                    ve.tensor_tensor(out=m2, in0=p2i_f, in1=lcsi[:, L4], op=ALU.mult)
                    ve.tensor_tensor(out=m3, in0=p2r_f, in1=lcsi[:, L4], op=ALU.mult)
                    vsync(ve.tensor_tensor(out=t1i, in0=p2i_f, in1=lcsr[:, L4], op=ALU.mult))
                    ve.tensor_tensor(out=t1r, in0=m1, in1=m2, op=ALU.subtract)
                    vsync(ve.tensor_tensor(out=m1, in0=m3, in1=t1i, op=ALU.add))
                    ve.tensor_tensor(out=m2, in0=t1r, in1=Ssel8[:, 0:4], op=ALU.add)
                    vsync(ve.tensor_tensor(out=m3, in0=m1, in1=Ssel8[:, 4:8], op=ALU.add))
                    ve.tensor_scalar(out=t1r, in0=m2, scalar1=P["mv"][:, 0:1],
                                     scalar2=None, op0=ALU.mult)
                    vsync(ve.tensor_scalar(out=t1i, in0=m3, scalar1=P["mv"][:, 0:1],
                                           scalar2=None, op0=ALU.mult))
                    ve.tensor_scalar(out=m1, in0=lcsr[:, L4], scalar1=P["mb"][:, 0:1],
                                     scalar2=None, op0=ALU.mult)
                    vsync(ve.tensor_scalar(out=m2, in0=lcsi[:, L4], scalar1=P["mb"][:, 0:1],
                                           scalar2=None, op0=ALU.mult))
                    ve.tensor_tensor(out=I_r, in0=m1, in1=t1r, op=ALU.add)
                    vsync(ve.tensor_tensor(out=I_i, in0=m2, in1=t1i, op=ALU.add))
                    vsync(ve.tensor_scalar(out=nI_i, in0=I_i, scalar1=-1.0, scalar2=None,
                                           op0=ALU.mult))
                    last = None
                    for c in range(C4):
                        ve.tensor_copy(out=Sinr4[:, c * NB:c * NB + 1], in_=I_r[:, c:c + 1])
                        last = ve.tensor_copy(out=Sini4[:, c * NB:c * NB + 1],
                                              in_=I_i[:, c:c + 1])
                    vsync(last)
                    for c in range(C4):
                        last = ve.scalar_tensor_tensor(
                            out=Sinr4[:, c * NB + 1:(c + 1) * NB],
                            in0=PBr[:, c * NB:(c + 1) * NB - 1], scalar=I_r[:, c:c + 1],
                            in1=S0r[:, c * NB:(c + 1) * NB - 1], op0=ALU.mult, op1=ALU.add)
                    vsync(last)
                    for c in range(C4):
                        last = ve.scalar_tensor_tensor(
                            out=Sinr4[:, c * NB + 1:(c + 1) * NB],
                            in0=PBi[:, c * NB:(c + 1) * NB - 1], scalar=nI_i[:, c:c + 1],
                            in1=Sinr4[:, c * NB + 1:(c + 1) * NB], op0=ALU.mult, op1=ALU.add)
                    vsync(last)
                    for c in range(C4):
                        last = ve.scalar_tensor_tensor(
                            out=Sini4[:, c * NB + 1:(c + 1) * NB],
                            in0=PBr[:, c * NB:(c + 1) * NB - 1], scalar=I_i[:, c:c + 1],
                            in1=S0i[:, c * NB:(c + 1) * NB - 1], op0=ALU.mult, op1=ALU.add)
                    vsync(last)
                    for c in range(C4):
                        last = ve.scalar_tensor_tensor(
                            out=Sini4[:, c * NB + 1:(c + 1) * NB],
                            in0=PBi[:, c * NB:(c + 1) * NB - 1], scalar=I_r[:, c:c + 1],
                            in1=Sini4[:, c * NB + 1:(c + 1) * NB], op0=ALU.mult, op1=ALU.add)
                    vsync(last)
                    if "cv0" in dbg_aps and l == 0:
                        cdbg = sDV.inc(ve.nop())
                        _w(sy, sDV, cdbg)
                        tv = sDBG.inc(sy.dma_start(out=dbg_aps["cv0"], in_=convr[:, 0:TOK]), 16)
                        _w(ve, sDBG, tv)
                    last = None
                    for c in range(C4):
                        last = ve.tensor_tensor(out=h[:, c * TOK:(c + 1) * TOK],
                                                in0=h[:, c * TOK:(c + 1) * TOK],
                                                in1=convr[:, c * TOK:(c + 1) * TOK],
                                                op=ALU.add)
                    vsync(last)
                    for c in range(C4):
                        h3 = h[:, c * TOK:(c + 1) * TOK].rearrange("p (b q) -> p b q", q=Q)
                        for j in range(NB):
                            last = ve.scalar_tensor_tensor(
                                out=h3[:, j, :], in0=P1r[:, c * Q:(c + 1) * Q],
                                scalar=Sinr4[:, c * NB + j:c * NB + j + 1],
                                in1=h3[:, j, :], op0=ALU.mult, op1=ALU.add)
                    vsync(last)
                    for c in range(C4):
                        h3 = h[:, c * TOK:(c + 1) * TOK].rearrange("p (b q) -> p b q", q=Q)
                        for j in range(NB):
                            last = ve.scalar_tensor_tensor(
                                out=h3[:, j, :], in0=nP1i[:, c * Q:(c + 1) * Q],
                                scalar=Sini4[:, c * NB + j:c * NB + j + 1],
                                in1=h3[:, j, :], op0=ALU.mult, op1=ALU.add)
                    vsync(last)

                    rstd2 = ln_stats()
                    y2_done = 0
                    for c in range(C4):
                        _w(ve, sSC, rstd2)
                        ve.tensor_tensor(out=y_c[:, :], in0=hsl(c, 0, TOK),
                                         in1=mu_all[:, :], op=ALU.subtract)
                        ve.tensor_tensor(out=y_c[:, :], in0=y_c[:, :],
                                         in1=rstd_all[:, :], op=ALU.mult)
                        y2_done = sDV.inc(ve.tensor_scalar(
                            out=y2in[:, c * TOK:(c + 1) * TOK], in0=y_c[:, :],
                            scalar1=lc("l2w", c), scalar2=lc("l2b", c),
                            op0=ALU.mult, op1=ALU.add))

                    _w(te, sW1, w1_ready[l])
                    _w(te, sW2, w2_ready[l])
                    _w(te, sDV, y2_done)
                    for t4 in range(4):
                        sil_last = 0
                        for fc in range(FC16):
                            bk = fc % 2
                            bank_wait(bk)
                            for kc in range(C4):
                                instr = te.matmul(
                                    pbank(bk),
                                    w1t[:, kc * DFF + fc * 128:kc * DFF + fc * 128 + 128],
                                    y2in[:, kc * TOK + t4 * 512:kc * TOK + (t4 + 1) * 512],
                                    start=(kc == 0), stop=(kc == C4 - 1))
                            w1v = sMM.inc(instr)
                            w1_lastmm[l] = w1v
                            _w(sc, sMM, w1v)
                            sil_last = sSC.inc(sc.activation(
                                out=y2buf[:, fc * 512:(fc + 1) * 512], in_=pbank(bk),
                                func=AF.Silu,
                                bias=P["b1"][:, l * FC16 + fc:l * FC16 + fc + 1]))
                            bank_free[bk] = (sSC, sil_last)
                        _w(te, sSC, sil_last)
                        for c in range(C4):
                            bk = 2 + c % 2
                            bank_wait(bk)
                            for fc in range(FC16):
                                instr = te.matmul(
                                    pbank(bk),
                                    w2t[:, fc * DIN + c * 128:fc * DIN + c * 128 + 128],
                                    y2buf[:, fc * 512:(fc + 1) * 512],
                                    start=(fc == 0), stop=(fc == FC16 - 1))
                            w2v = sMM.inc(instr)
                            w2_lastmm[l] = w2v
                            _w(ve, sMM, w2v)
                            hv = sDV.inc(ve.scalar_tensor_tensor(
                                out=hsl(c, t4 * 512, (t4 + 1) * 512), in0=pbank(bk),
                                scalar=lc("b2", c),
                                in1=hsl(c, t4 * 512, (t4 + 1) * 512),
                                op0=ALU.add, op1=ALU.add))
                            bank_free[bk] = (sDV, hv)

                if "h_f" in dbg_aps:
                    _w(sy, sDV, sDV.n)
                    tv = sDBG.inc(sy.dma_start(out=dbg_aps["h_f"], in_=h[:, :]), 16)
                    _w(ve, sDBG, tv)

                rstdf = ln_stats()
                hf_done = 0
                for c in range(C4):
                    _w(ve, sSC, rstdf)
                    ve.tensor_tensor(out=y_c[:, :], in0=hsl(c, 0, TOK), in1=mu_all[:, :],
                                     op=ALU.subtract)
                    ve.tensor_tensor(out=y_c[:, :], in0=y_c[:, :], in1=rstd_all[:, :],
                                     op=ALU.mult)
                    hf_done = sDV.inc(ve.tensor_scalar(
                        out=hf[:, c * TOK:(c + 1) * TOK], in0=y_c[:, :],
                        scalar1=P["lnlw"][:, c:c + 1], scalar2=P["lnlb"][:, c:c + 1],
                        op0=ALU.mult, op1=ALU.add))

                for eng in (ve, sc, te, gp, sy):
                    _w(eng, sDV, sDV.n)
                    _w(eng, sSC, sSC.n)
                    _w(eng, sMM, sMM.n)
                    _w(eng, sW1, sW1.n)
                    _w(eng, sW2, sW2.n)
                    _w(eng, sCCD, sCCD.n)

        with ExitStack() as cctx:
            cctx.enter_context(nc.named_scope("stageC"))
            csb = lambda n, shp, dt=F32: cctx.enter_context(nc.sbuf_tensor(n, shp, dt))
            hh = csb("hh", [128, IC * 512], BF16)
            boutb = csb("boutb", [1, V], BF16)
            bout_ready = sONE.inc(gp.dma_start(out=boutb[:, :], in_=bout_in,
                                               max_dma_last_dim=8192), 16)
            OV = csb("OV", [128, 18432], F32)
            OVb = OV[:, :].bitcast(BF16)
            xnTs = OVb[:, 0:16384]
            wgbuf = [OVb[:, 16384 + i * 4096:16384 + (i + 1) * 4096] for i in range(2)]
            wubuf = [OVb[:, 24576 + i * 4096:24576 + (i + 1) * 4096] for i in range(2)]
            gsil = [OV[:, 16384 + i * 512:16384 + (i + 1) * 512] for i in range(2)]
            wdbuf = [OVb[:, i * 4096:(i + 1) * 4096] for i in range(2)]
            wobuf = [OVb[:, 8192 + i * 4096:8192 + (i + 1) * 4096] for i in range(2)]
            xstg = OV[:, 8192:12288]
            ostg = OV[:, 12288:16384]
            dsb = [OV[:, 16384 + i * 512:16384 + (i + 1) * 512] for i in range(2)]
            sumb = [OV[:, 17408 + i * 512:17408 + (i + 1) * 512] for i in range(2)]

            x4 = x_in.rearrange("(s b p) (g c) -> p s g b c", p=128, b=4, c=1024)
            o4 = out_t.rearrange("(s b p) (g c) -> p s g b c", p=128, b=4, c=1024)

            gbank_free = [(sDV, 0)] * 2
            ubank_free = [(sDV, 0)] * 2
            gsil_free = [(sDV, 0)] * 2
            wg_cons = [(sMM, 0)] * 2
            wu_cons = [(sMM, 0)] * 2
            wo_cons = [(sMM, 0)] * 2
            ostg_free = (sOS[0], 0)
            xstg_free = (sDV, 0)
            prev_down_reads = 0
            prev_store = 0

            for s in range(4):
                _w(sy, sDV, prev_down_reads)
                _w(sy, sOS[0], sOS[0].n)
                _w(sy, sOS[1], sOS[1].n)
                _w(sy, sCVg, wg_ready)
                _w(sc, sOS[0], sOS[0].n)
                _w(sc, sOS[1], sOS[1].n)
                xnT_ld = sXNT.inc(sy.dma_start(
                    out=xnTs.rearrange("p (v t) -> p v t", t=512),
                    in_=xnt[:, :, s * 512:(s + 1) * 512]), 16)

                gate_end = 0
                for i in range(IC):
                    bi = i % 2
                    s_, v_ = wg_cons[bi]
                    _w(sy, s_, v_)
                    wg_ld = sWG[bi].inc(sy.dma_start(out=wgbuf[bi], in_=wg_bf[i]), 16)
                    s_, v_ = wu_cons[bi]
                    _w(sy, s_, v_)
                    if i < 2:
                        _w(sy, sCVu, wu_ready)
                    wu_ld = sWU[bi].inc(sy.dma_start(out=wubuf[bi], in_=wu_bf[i]), 16)
                    _w(te, sWG[bi], wg_ld)
                    if i < 2:
                        _w(te, sXNT, xnT_ld)
                    s_, v_ = gbank_free[bi]
                    _w(te, s_, v_)
                    for hc in range(VC32):
                        instr = te.matmul(pbank(bi), wgbuf[bi][:, hc * 128:(hc + 1) * 128],
                                          xnTs[:, hc * 512:(hc + 1) * 512],
                                          start=(hc == 0), stop=(hc == VC32 - 1))
                    g_done = sMM.inc(instr)
                    wg_cons[bi] = (sMM, g_done)
                    _w(te, sWU[bi], wu_ld)
                    s_, v_ = ubank_free[bi]
                    _w(te, s_, v_)
                    for hc in range(VC32):
                        instr = te.matmul(pbank(2 + bi),
                                          wubuf[bi][:, hc * 128:(hc + 1) * 128],
                                          xnTs[:, hc * 512:(hc + 1) * 512],
                                          start=(hc == 0), stop=(hc == VC32 - 1))
                    u_done = sMM.inc(instr)
                    wu_cons[bi] = (sMM, u_done)
                    gate_end = u_done
                    s_, v_ = gsil_free[bi]
                    _w(sc, s_, v_)
                    _w(sc, sMM, g_done)
                    sil_v = sSC.inc(sc.activation(out=gsil[bi], in_=pbank(bi),
                                                  func=AF.Silu,
                                                  scale=P["sg"][:, i:i + 1]))
                    gbank_free[bi] = (sSC, sil_v)
                    _w(ve, sMM, u_done)
                    _w(ve, sSC, sil_v)
                    hh_v = sDV.inc(ve.scalar_tensor_tensor(
                        out=hh[:, i * 512:(i + 1) * 512], in0=pbank(2 + bi),
                        scalar=P["su"][:, i:i + 1], in1=gsil[bi],
                        op0=ALU.mult, op1=ALU.mult))
                    ubank_free[bi] = (sDV, hh_v)
                    gsil_free[bi] = (sDV, hh_v)
                hh_done = sDV.n

                bank_free[0] = gbank_free[0]
                bank_free[1] = gbank_free[1]
                bank_free[2] = ubank_free[0]
                bank_free[3] = ubank_free[1]

                wd_cons = [(sMM, 0)] * 2
                per_hc_add = 0
                NWD = (IC + 3) // 4
                for hg in range(HG):
                    s_, v_ = xstg_free
                    _w(sc, s_, v_)
                    if hg == 0:
                        _w(sc, sMM, gate_end)
                    xs_ld = sXS.inc(sc.dma_start(
                        out=xstg.rearrange("p (b c) -> p b c", c=1024),
                        in_=x4[:, s, hg]), 16)
                    s_, v_ = wo_cons[hg % 2]
                    _w(gp, s_, v_)
                    _w(gp, sMM, gate_end)
                    if s == 0 and hg == 0:
                        _w(gp, sCVd, wd_ready)
                        _w(gp, sONE, bout_ready)
                    wo_ld = sWO[hg % 2].inc(gp.dma_start(
                        out=wobuf[hg % 2].rearrange("p (f c) -> p f c", c=1024),
                        in_=woT[hg].rearrange("f p c -> p f c"),
                        max_dma_last_dim=8192), 16)
                    down_done = 0
                    for ib in range(NWD):
                        bi = ib % 2
                        i0 = ib * 4
                        ni = min(4, IC - i0)
                        s_, v_ = wd_cons[bi]
                        _w(sy, s_, v_)
                        if ib < 2:
                            _w(sy, sMM, gate_end)
                        wd_ld = sWD[bi].inc(sy.dma_start(
                            out=wdbuf[bi][:, 0:ni * 1024].rearrange("p (i c) -> p i c", c=1024),
                            in_=wd_bf[hg, i0:i0 + ni].rearrange("i p c -> p i c")), 16)
                        _w(te, sWD[bi], wd_ld)
                        if ib == 0:
                            _w(te, sDV, hh_done)
                        for ii in range(ni):
                            i = i0 + ii
                            for hc in range(8):
                                if i == 0:
                                    bank_wait(hc)
                                instr = te.matmul(
                                    pbank(hc),
                                    wdbuf[bi][:, ii * 1024 + hc * 128:ii * 1024 + (hc + 1) * 128],
                                    hh[:, i * 512:(i + 1) * 512],
                                    start=(i == 0), stop=(i == IC - 1))
                        grp_v = sMM.inc(instr)
                        wd_cons[bi] = (sMM, grp_v)
                        down_done = grp_v
                    dv = [0] * 8
                    wo_done = [0] * 8
                    sm = [0] * 8
                    tp = [0] * 8

                    def ve_dsb(hc):
                        _w(ve, sMM, down_done)
                        dv[hc] = sDV.inc(ve.tensor_scalar(
                            out=dsb[hc % 2], in0=pbank(hc),
                            scalar1=P["sd"][:, hg * 8 + hc:hg * 8 + hc + 1],
                            scalar2=None, op0=ALU.mult))

                    def te_wout(hc):
                        _w(te, sDV, dv[hc])
                        if hc == 0:
                            _w(te, sWO[hg % 2], wo_ld)
                        for fc in range(C4):
                            te.matmul(pbank(hc),
                                      wobuf[hg % 2][:, fc * 1024 + hc * 128:fc * 1024 + hc * 128 + 128],
                                      hf[:, fc * TOK + s * 512:fc * TOK + (s + 1) * 512],
                                      start=(fc == 0), stop=False)
                        wo_done[hc] = sMM.inc(te.matmul(
                            pbank(hc),
                            boutb[0:1, hg * 1024 + hc * 128:hg * 1024 + hc * 128 + 128],
                            onesb[0:1, :], start=False, stop=True))

                    def ve_sum(hc):
                        _w(ve, sMM, wo_done[hc])
                        if hc >= 2:
                            _w(ve, sMM, tp[hc - 2])
                        sm[hc] = sDV.inc(ve.tensor_tensor(out=sumb[hc % 2], in0=pbank(hc),
                                                          in1=dsb[hc % 2], op=ALU.add))

                    def te_tp(hc):
                        _w(te, sDV, sm[hc])
                        for tb in range(4):
                            instr = te.transpose(pbank(hc)[:, tb * 128:(tb + 1) * 128],
                                                 sumb[hc % 2][:, tb * 128:(tb + 1) * 128],
                                                 identf[:, :])
                        tp[hc] = sMM.inc(instr)

                    def ve_add(hc):
                        nonlocal per_hc_add
                        _w(ve, sMM, tp[hc])
                        _w(ve, sXS, xs_ld)
                        if hc == 0:
                            s_, v_ = ostg_free
                            _w(ve, s_, v_)
                        for tb in range(4):
                            per_hc_add = sDV.inc(ve.tensor_tensor(
                                out=ostg[:, tb * 1024 + hc * 128:tb * 1024 + hc * 128 + 128],
                                in0=pbank(hc)[:, tb * 128:(tb + 1) * 128],
                                in1=xstg[:, tb * 1024 + hc * 128:tb * 1024 + hc * 128 + 128],
                                op=ALU.add))
                        bank_free[hc] = (sDV, per_hc_add)

                    ve_dsb(0)
                    ve_dsb(1)
                    te_wout(0)
                    ve_sum(0)
                    te_tp(0)
                    for hc in range(1, 8):
                        te_wout(hc)
                        if hc + 1 < 8:
                            ve_dsb(hc + 1)
                        ve_sum(hc)
                        te_tp(hc)
                        ve_add(hc - 1)
                    ve_add(7)
                    wo_cons[hg % 2] = (sMM, tp[7])
                    _w(sc, sDV, per_hc_add)
                    prev_store = sOS[hg % 2].inc(sc.dma_start(
                        out=o4[:, s, hg],
                        in_=ostg.rearrange("p (b c) -> p b c", c=1024)), 16)
                    ostg_free = (sOS[hg % 2], prev_store)
                    xstg_free = (sDV, per_hc_add)
                prev_down_reads = per_hc_add
                gbank_free = [bank_free[0], bank_free[1]]
                ubank_free = [bank_free[2], bank_free[3]]

            _w(sy, sOS[0], sOS[0].n)
            _w(sy, sOS[1], sOS[1].n)
            _w(sy, sDBG, sDBG.n)

    return nc, dbg_aps


def host_inputs(L0_attn_x, L0_attn_Win, L0_attn_bin, L0_attn_ln1_w, L0_attn_ln1_b,
                L0_attn_phz_re, L0_attn_phz_im, L0_attn_lc_re, L0_attn_lc_im,
                L0_attn_sc_lin, L0_attn_ln2_w, L0_attn_ln2_b, L0_attn_W1, L0_attn_b1,
                L0_attn_W2, L0_attn_b2, L0_attn_lnl_w, L0_attn_lnl_b, L0_attn_Wout,
                L0_attn_bout, L0_ffn_w_gate, L0_ffn_s_gate, L0_ffn_w_up, L0_ffn_s_up,
                L0_ffn_w_down, L0_ffn_s_down):
    f32 = np.float32
    cc = np.ascontiguousarray

    def pc(v, k):
        return cc(np.asarray(v, f32).reshape(k, 128).T)

    def pc_l(v):
        return cc(np.asarray(v, f32).reshape(NLAY, 4, 128).transpose(2, 0, 1)
                  .reshape(128, 16))

    winT_h = cc(np.asarray(L0_attn_Win, f32).T.reshape(VC32, 128, DIN)
                .transpose(1, 0, 2).reshape(128, VC32 * DIN))
    w1T_h = cc(np.asarray(L0_attn_W1, f32).transpose(0, 2, 1)
               .reshape(NLAY, C4, 128, DFF).transpose(0, 2, 1, 3)
               .reshape(NLAY, 128, C4 * DFF))
    w2T_h = cc(np.asarray(L0_attn_W2, f32).transpose(0, 2, 1)
               .reshape(NLAY, FC16, 128, DIN).transpose(0, 2, 1, 3)
               .reshape(NLAY, 128, FC16 * DIN))
    woT_h = cc(np.asarray(L0_attn_Wout, f32).T.reshape(C4, 128, HG, 1024)
               .transpose(2, 0, 1, 3))
    wg_h = cc(np.asarray(L0_ffn_w_gate).T.reshape(VC32, 128, IC, 128)
              .transpose(2, 1, 0, 3).reshape(IC, 128, 4096).astype(np.int32))
    wu_h = cc(np.asarray(L0_ffn_w_up).T.reshape(VC32, 128, IC, 128)
              .transpose(2, 1, 0, 3).reshape(IC, 128, 4096).astype(np.int32))
    wd_h = cc(np.asarray(L0_ffn_w_down).T.reshape(IC, 128, HG, 1024)
              .transpose(2, 0, 1, 3).astype(np.int32))

    base = dict(
        winT=winT_h, w1T=w1T_h, w2T=w2T_h, woT=woT_h, wg_i=wg_h, wu_i=wu_h, wd_i=wd_h,
        p_bin=pc(L0_attn_bin, C4), p_l1w=pc_l(L0_attn_ln1_w), p_l1b=pc_l(L0_attn_ln1_b),
        p_l2w=pc_l(L0_attn_ln2_w), p_l2b=pc_l(L0_attn_ln2_b), p_scl=pc_l(L0_attn_sc_lin),
        p_phr=pc_l(L0_attn_phz_re), p_phi=pc_l(L0_attn_phz_im),
        p_lcr=pc_l(L0_attn_lc_re), p_lci=pc_l(L0_attn_lc_im),
        p_b2=pc_l(L0_attn_b2),
        p_lnlw=pc(L0_attn_lnl_w, C4), p_lnlb=pc(L0_attn_lnl_b, C4),
        p_b1=cc(np.asarray(L0_attn_b1, f32).reshape(NLAY, FC16, 128)
                .transpose(2, 0, 1).reshape(128, NLAY * FC16)),
        p_sg=pc(L0_ffn_s_gate, IC), p_su=pc(L0_ffn_s_up, IC), p_sd=pc(L0_ffn_s_down, 32),
        p_eps=np.full((128, 1), EPS, f32),
        ident_b=np.eye(128, dtype=ml_dtypes.bfloat16),
        ident_f=np.eye(128, dtype=f32),
        om=np.full((128, 128), 1.0 / DIN, f32),
        bout=cc(np.asarray(L0_attn_bout, f32).reshape(1, V)),
        onesr=np.ones((1, 512), ml_dtypes.bfloat16),
    )
    x = np.asarray(L0_attn_x, f32)
    in_maps = []
    for core in range(NCORES):
        b, half = core // 2, core % 2
        m = dict(base)
        m["x"] = cc(x[b, half * TOK:(half + 1) * TOK, :])
        m["p_mv"] = np.full((128, 1), float(half), f32)
        m["p_mb"] = np.full((128, 1), 1.0 - float(half), f32)
        selv = np.zeros((8, 1), f32)
        selv[core - 1 if half == 1 else core] = 1.0
        m["sel"] = selv
        in_maps.append(m)
    return in_maps


_NC_CACHE = {}


def kernel(**inputs):
    if "main" not in _NC_CACHE:
        _NC_CACHE["main"] = build()
    nc, _ = _NC_CACHE["main"]
    in_maps = host_inputs(**inputs)
    res = run_bass_kernel_spmd(nc, in_maps, core_ids=list(range(NCORES)))
    out = np.empty((B, L, V), np.float32)
    for core in range(NCORES):
        b, half = core // 2, core % 2
        out[b, half * TOK:(half + 1) * TOK, :] = res.results[core]["out"]
    return out


# version 07
# speedup vs baseline: 1.0042x; 1.0042x over previous
import numpy as np
import ml_dtypes
from contextlib import ExitStack

import concourse.bass as bass
import concourse.mybir as mybir
from concourse.bass_utils import run_bass_kernel_spmd

F32 = mybir.dt.float32
BF16 = mybir.dt.bfloat16
I32 = mybir.dt.int32
AF = mybir.ActivationFunctionType
ALU = mybir.AluOpType

B, L, V = 4, 4096, 4096
TOK = 2048
DIN, DFF, ITR = 512, 2048, 11008
NLAY = 4
Q, NB = 128, 16
C4 = DIN // 128
FC16 = DFF // 128
VC32 = V // 128
IC = ITR // 128
HG = 4
EPS = 1e-5
NCORES = 8


class Sem:
    def __init__(self, nc, ctx, name):
        self.h = ctx.enter_context(nc.semaphore(name))
        self.n = 0

    def inc(self, instr, k=1):
        instr.then_inc(self.h, k)
        self.n += k
        return self.n


def _w(eng, sem, val):
    if val > 0:
        eng.wait_ge(sem.h, val)


def build(dbg=()):
    nc = bass.Bass("TRN2", target_bir_lowering=False, debug=False)
    ap = lambda t: t.ap()

    x_in = ap(nc.dram_tensor("x", [TOK, V], F32, kind="ExternalInput"))
    winT = ap(nc.dram_tensor("winT", [128, VC32 * DIN], F32, kind="ExternalInput"))
    w1T = ap(nc.dram_tensor("w1T", [NLAY, 128, C4 * DFF], F32, kind="ExternalInput"))
    w2T = ap(nc.dram_tensor("w2T", [NLAY, 128, FC16 * DIN], F32, kind="ExternalInput"))
    woT = ap(nc.dram_tensor("woT", [HG, C4, 128, 1024], F32, kind="ExternalInput"))
    wg_i = ap(nc.dram_tensor("wg_i", [IC, 128, 4096], I32, kind="ExternalInput"))
    wu_i = ap(nc.dram_tensor("wu_i", [IC, 128, 4096], I32, kind="ExternalInput"))
    wd_i = ap(nc.dram_tensor("wd_i", [HG, IC, 128, 1024], I32, kind="ExternalInput"))
    pv = {}
    for name, k in [("bin", C4), ("l1w", 16), ("l1b", 16), ("l2w", 16), ("l2b", 16),
                    ("scl", 16), ("phr", 16), ("phi", 16), ("lcr", 16), ("lci", 16),
                    ("b2", 16), ("lnlw", C4), ("lnlb", C4), ("b1", NLAY * FC16),
                    ("sg", IC), ("su", IC), ("sd", 32), ("mv", 1), ("mb", 1), ("eps", 1)]:
        pv[name] = ap(nc.dram_tensor("p_" + name, [128, k], F32, kind="ExternalInput"))
    ident_b = ap(nc.dram_tensor("ident_b", [128, 128], BF16, kind="ExternalInput"))
    ident_f = ap(nc.dram_tensor("ident_f", [128, 128], F32, kind="ExternalInput"))
    om_in = ap(nc.dram_tensor("om", [128, 128], F32, kind="ExternalInput"))
    sel_in = ap(nc.dram_tensor("sel", [8, 1], F32, kind="ExternalInput"))
    bout_in = ap(nc.dram_tensor("bout", [1, V], F32, kind="ExternalInput"))
    ones_in = ap(nc.dram_tensor("onesr", [1, 512], BF16, kind="ExternalInput"))
    out_t = ap(nc.dram_tensor("out", [TOK, V], F32, kind="ExternalOutput"))

    wg_bf = ap(nc.dram_tensor("wg_bf", [IC, 128, 4096], BF16))
    wu_bf = ap(nc.dram_tensor("wu_bf", [IC, 128, 4096], BF16))
    wd_bf = ap(nc.dram_tensor("wd_bf", [HG, IC, 128, 1024], BF16))
    xnt = nc.dram_tensor("xnt", [VC32, 128, TOK], BF16).ap().rearrange("v p t -> p v t")
    cc_in = ap(nc.dram_tensor("cc_in", [1, 1024], F32))
    cc_out = ap(nc.dram_tensor("cc_out", [8, 1024], F32))

    dbg_aps = {}
    for name in dbg:
        shp = {"h_a": [128, C4 * TOK], "h_f": [128, C4 * TOK], "xn0": [128, V],
               "y0": [128, TOK], "cv0": [128, TOK], "pc": [128, 32],
               "st0": [128, 16]}[name]
        dt = {"xn0": BF16}.get(name, F32)
        dbg_aps[name] = ap(nc.dram_tensor("dbg_" + name, shp, dt, kind="ExternalOutput"))

    with ExitStack() as ctx:
        sCC = Sem(nc, ctx, "sCC")
        sMM = Sem(nc, ctx, "sMM")
        sDV = Sem(nc, ctx, "sDV")
        sSC = Sem(nc, ctx, "sSC")
        sPRM = Sem(nc, ctx, "sPRM")
        sCVg = Sem(nc, ctx, "sCVg")
        sCVu = Sem(nc, ctx, "sCVu")
        sCVd = Sem(nc, ctx, "sCVd")
        sONE = Sem(nc, ctx, "sONE")
        sW1 = Sem(nc, ctx, "sW1")
        sW2 = Sem(nc, ctx, "sW2")
        sCCD = Sem(nc, ctx, "sCCD")
        sX = [Sem(nc, ctx, "sX0"), Sem(nc, ctx, "sX1")]
        sSL = [Sem(nc, ctx, "sSL0"), Sem(nc, ctx, "sSL1")]
        sXNT = Sem(nc, ctx, "sXNT")
        sWG = [Sem(nc, ctx, "sWG0"), Sem(nc, ctx, "sWG1")]
        sWU = [Sem(nc, ctx, "sWU0"), Sem(nc, ctx, "sWU1")]
        sWD = [Sem(nc, ctx, "sWDa"), Sem(nc, ctx, "sWDb")]
        sWO = [Sem(nc, ctx, "sWO0"), Sem(nc, ctx, "sWO1")]
        sXS = Sem(nc, ctx, "sXS")
        sOS = [Sem(nc, ctx, "sOS0"), Sem(nc, ctx, "sOS1")]
        sDBG = Sem(nc, ctx, "sDBG")
        sGP = Sem(nc, ctx, "sGP")
        sy, sc, ve, te, gp = nc.sync, nc.scalar, nc.vector, nc.tensor, nc.gpsimd

        sb = lambda n, shp, dt=F32: ctx.enter_context(nc.sbuf_tensor(n, shp, dt))
        P = {k: sb("sp_" + k, [128, v.shape[-1]]) for k, v in pv.items()}
        identb = sb("identb", [128, 128], BF16)
        identf = sb("identf", [128, 128], F32)
        omt = sb("omt", [128, 128], F32)
        selsb = sb("selsb", [8, 1], F32)
        onesb = sb("onesb", [1, 512], BF16)
        hf = sb("hf", [128, C4 * TOK], BF16)
        lcsr = sb("lcsr", [128, 16], F32)
        lcsi = sb("lcsi", [128, 16], F32)
        pcr = sb("pcr", [128, 16], F32)
        pci = sb("pci", [128, 16], F32)
        tmp16a = sb("tmp16a", [128, 16], F32)
        tmp16b = sb("tmp16b", [128, 16], F32)
        tmp16c = sb("tmp16c", [128, 16], F32)
        tmp16d = sb("tmp16d", [128, 16], F32)
        tmp16e = sb("tmp16e", [128, 16], F32)

        ps = ctx.enter_context(nc.psum_tensor("ps", [128, 8 * 512], F32))
        pbank = lambda b: ps[:, b * 512:(b + 1) * 512]
        pbank_b16 = lambda b: ps[:, :].bitcast(BF16)[:, b * 1024:(b + 1) * 1024]
        bank_free = [(sDV, 0)] * 8

        def bank_wait(b):
            s_, v_ = bank_free[b]
            _w(te, s_, v_)

        for k in P:
            sPRM.inc(sy.dma_start(out=P[k][:, :], in_=pv[k]), 16)
        sPRM.inc(sy.dma_start(out=identb[:, :], in_=ident_b), 16)
        sPRM.inc(sy.dma_start(out=identf[:, :], in_=ident_f), 16)
        sPRM.inc(sy.dma_start(out=omt[:, :], in_=om_in), 16)
        sPRM.inc(sy.dma_start(out=selsb[:, :], in_=sel_in), 16)
        params_ready = sPRM.inc(sy.dma_start(out=onesb[:, :], in_=ones_in), 16)

        CHUNK = 4 * 1024 * 1024

        def conv_chunks(sem, dst, src, n_elem, c0=0, c1=10**9):
            dims = len(src.shape)
            names = " ".join(f"a{i}" for i in range(dims))
            sflat = src.rearrange(f"{names} -> ({names})")
            dflat = dst.rearrange(f"{names} -> ({names})")
            v = 0
            ncall = (n_elem + CHUNK - 1) // CHUNK
            for ci in range(max(0, c0), min(ncall, c1)):
                o = ci * CHUNK
                e = min(n_elem, o + CHUNK)
                v = sem.inc(gp.dma_start(out=dflat[o:e], in_=sflat[o:e],
                                         max_dma_last_dim=8192), 16)
            return v

        NGE = IC * 128 * 4096
        wg_ready = wu_ready = wd_ready = 0


        def vfence(instr):
            v = sDV.inc(instr)
            ve.wait_ge(sDV.h, v)
            return v

        def sfence(instr):
            v = sSC.inc(instr)
            sc.wait_ge(sSC.h, v)
            return v

        _w(ve, sPRM, params_ready)
        ve.tensor_tensor(out=tmp16a[:, :], in0=P["phr"][:, :], in1=P["phr"][:, :], op=ALU.mult)
        vfence(ve.tensor_tensor(out=tmp16b[:, :], in0=P["phi"][:, :], in1=P["phi"][:, :], op=ALU.mult))
        r2_done = sDV.inc(ve.tensor_tensor(out=tmp16c[:, :], in0=tmp16a[:, :],
                                           in1=tmp16b[:, :], op=ALU.add))
        _w(sc, sDV, r2_done)
        sfence(sc.activation(out=tmp16a[:, :], in_=tmp16c[:, :], func=AF.Ln))
        sfence(sc.activation(out=tmp16b[:, :], in_=tmp16a[:, :], func=AF.Exp, scale=0.5))
        sc.activation(out=tmp16d[:, :], in_=tmp16a[:, :], func=AF.Exp, scale=-0.5)
        mag_done = sSC.inc(sc.activation(out=tmp16e[:, :], in_=tmp16b[:, :], func=AF.Tanh))
        _w(ve, sSC, mag_done)
        vfence(ve.tensor_tensor(out=tmp16c[:, :], in0=tmp16e[:, :], in1=tmp16d[:, :], op=ALU.mult))
        ve.tensor_tensor(out=pcr[:, :], in0=P["phr"][:, :], in1=tmp16c[:, :], op=ALU.mult)
        ve.tensor_tensor(out=pci[:, :], in0=P["phi"][:, :], in1=tmp16c[:, :], op=ALU.mult)
        ve.tensor_tensor(out=lcsr[:, :], in0=P["lcr"][:, :], in1=P["scl"][:, :], op=ALU.mult)
        pcv = sDV.inc(ve.tensor_tensor(out=lcsi[:, :], in0=P["lci"][:, :], in1=P["scl"][:, :], op=ALU.mult))
        if "pc" in dbg_aps:
            _w(sy, sDV, pcv)
            sDBG.inc(sy.dma_start(out=dbg_aps["pc"][:, 0:16], in_=pcr[:, :]), 16)
            sDBG.inc(sy.dma_start(out=dbg_aps["pc"][:, 16:32], in_=pci[:, :]), 16)

        with ExitStack() as abctx:
            absb = lambda n, shp, dt=F32: abctx.enter_context(nc.sbuf_tensor(n, shp, dt))
            h = absb("h", [128, C4 * TOK], F32)
            hsl = lambda c, a, b_: h[:, c * TOK + a:c * TOK + b_]

            with ExitStack() as actx:
                actx.enter_context(nc.named_scope("stageA"))
                asb = lambda n, shp, dt=F32: actx.enter_context(nc.sbuf_tensor(n, shp, dt))
                xbuf = [asb("xa", [128, V]), asb("xb", [128, V])]
                xnb = [asb("xna", [128, V], BF16), asb("xnb2", [128, V], BF16)]
                stg = [asb("stga", [128, VC32 * 512], BF16),
                       asb("stgb", [128, VC32 * 512], BF16)]
                winTb = asb("winTb", [128, VC32 * DIN], BF16)
                stat = asb("stat", [128, 16], F32)
                junkf = asb("junkf", [128, V // 2], F32)
                winT_ready = sONE.inc(gp.dma_start(out=winTb[:, :], in_=winT,
                                                   max_dma_last_dim=8192), 16)
                wg_ready = conv_chunks(sCVg, wg_bf, wg_i, NGE)
                _w(te, sPRM, params_ready)

                xbuf_free = [(sDV, 0), (sDV, 0)]
                xnb_free = [(sMM, 0), (sMM, 0)]
                stg_free = [(sMM, 0), (sMM, 0)]
                stg_wr = [0, 0]
                tp_bank_free = [(sDV, 0), (sDV, 0)]
                stg_evac = [0, 0]
                h_a_done = 0

                for tc in range(16):
                    bi = tc % 2
                    sl = tc // 4
                    sbl = sl % 2
                    st_off = (tc % 4) * 128
                    s_, v_ = xbuf_free[bi]
                    _w(sy, s_, v_)
                    x_ld = sX[bi].inc(sy.dma_start(out=xbuf[bi][:, :],
                                                   in_=x_in[tc * 128:(tc + 1) * 128, :]), 16)
                    def vs_(instr):
                        v = sDV.inc(instr)
                        ve.wait_ge(sDV.h, v)
                        return v

                    _w(ve, sX[bi], x_ld)
                    _w(ve, sPRM, params_ready)
                    vs_(ve.tensor_reduce(out=stat[:, 8 * bi + 0:8 * bi + 1],
                                         in_=xbuf[bi][:, :], axis=mybir.AxisListType.X,
                                         op=ALU.add))
                    _w(sc, sX[bi], x_ld)
                    _w(sc, sPRM, params_ready)
                    sqA = sSC.inc(sc.activation(out=junkf[:, :], in_=xbuf[bi][:, 0:V // 2],
                                                func=AF.Square))
                    _w(ve, sSC, sqA)
                    rA = vs_(ve.tensor_reduce(out=stat[:, 8 * bi + 1:8 * bi + 2],
                                              in_=junkf[:, :], axis=mybir.AxisListType.X,
                                              op=ALU.add))
                    _w(sc, sDV, rA)
                    sqB = sSC.inc(sc.activation(out=junkf[:, :], in_=xbuf[bi][:, V // 2:V],
                                                func=AF.Square))
                    _w(ve, sSC, sqB)
                    vs_(ve.tensor_reduce(out=stat[:, 8 * bi + 3:8 * bi + 4],
                                         in_=junkf[:, :], axis=mybir.AxisListType.X,
                                         op=ALU.add))
                    ve.tensor_tensor(out=stat[:, 8 * bi + 6:8 * bi + 7],
                                     in0=stat[:, 8 * bi + 1:8 * bi + 2],
                                     in1=stat[:, 8 * bi + 3:8 * bi + 4], op=ALU.add)
                    vs_(ve.tensor_scalar(out=stat[:, 8 * bi + 2:8 * bi + 3],
                                         in0=stat[:, 8 * bi + 0:8 * bi + 1],
                                         scalar1=1.0 / V, scalar2=None, op0=ALU.mult))
                    ve.tensor_scalar(out=stat[:, 8 * bi + 3:8 * bi + 4],
                                     in0=stat[:, 8 * bi + 6:8 * bi + 7],
                                     scalar1=1.0 / V, scalar2=None, op0=ALU.mult)
                    vs_(ve.tensor_tensor(out=stat[:, 8 * bi + 4:8 * bi + 5],
                                         in0=stat[:, 8 * bi + 2:8 * bi + 3],
                                         in1=stat[:, 8 * bi + 2:8 * bi + 3], op=ALU.mult))
                    var_done = sDV.inc(ve.tensor_tensor(
                        out=stat[:, 8 * bi + 5:8 * bi + 6],
                        in0=stat[:, 8 * bi + 3:8 * bi + 4],
                        in1=stat[:, 8 * bi + 4:8 * bi + 5], op=ALU.subtract))
                    _w(sc, sDV, var_done)
                    lnv = sSC.inc(sc.activation(out=stat[:, 8 * bi + 6:8 * bi + 7],
                                                in_=stat[:, 8 * bi + 5:8 * bi + 6],
                                                func=AF.Ln, bias=P["eps"][:, 0:1]))
                    sc.wait_ge(sSC.h, lnv)
                    rstd_done = sSC.inc(sc.activation(
                        out=stat[:, 8 * bi + 7:8 * bi + 8],
                        in_=stat[:, 8 * bi + 6:8 * bi + 7], func=AF.Exp, scale=-0.5))
                    s_, v_ = xnb_free[bi]
                    _w(ve, s_, v_)
                    _w(ve, sSC, rstd_done)
                    xn_done = sDV.inc(ve.tensor_scalar(
                        out=xnb[bi][:, :], in0=xbuf[bi][:, :],
                        scalar1=stat[:, 8 * bi + 2:8 * bi + 3],
                        scalar2=stat[:, 8 * bi + 7:8 * bi + 8],
                        op0=ALU.subtract, op1=ALU.mult))
                    xbuf_free[bi] = (sDV, xn_done)
                    if "xn0" in dbg_aps and tc == 0:
                        _w(sy, sDV, xn_done)
                        tv = sDBG.inc(sy.dma_start(out=dbg_aps["xn0"], in_=xnb[bi][:, :]), 16)
                        _w(ve, sDBG, tv)
                        _w(sc, sDBG, tv)
                    if "st0" in dbg_aps and tc == 0:
                        _w(sy, sDV, xn_done)
                        tv = sDBG.inc(sy.dma_start(out=dbg_aps["st0"], in_=stat[:, :]), 16)
                        _w(ve, sDBG, tv)
                        _w(sc, sDBG, tv)
                    last_tp = 0
                    for g in range(4):
                        tb = g % 2
                        s_, v_ = tp_bank_free[tb]
                        _w(te, s_, v_)
                        _w(te, sDV, xn_done)
                        for k in range(8):
                            vc = g * 8 + k
                            instr = te.transpose(pbank_b16(4 + tb)[:, k * 128:(k + 1) * 128],
                                                 xnb[bi][:, vc * 128:(vc + 1) * 128],
                                                 identb[:, :])
                        tp_done = sMM.inc(instr)
                        last_tp = tp_done
                        if g == 0:
                            s_, v_ = stg_free[sbl]
                            _w(ve, s_, v_)
                            _w(ve, sSL[sbl], stg_wr[sbl])
                        _w(ve, sMM, tp_done)
                        dst = stg[sbl][:, :].rearrange("p (v t) -> p v t", t=512)
                        src = pbank_b16(4 + tb)[:, :].rearrange("p (k t) -> p k t", t=128)
                        ev = sDV.inc(ve.tensor_copy(
                            out=dst[:, g * 8:(g + 1) * 8, st_off:st_off + 128], in_=src))
                        tp_bank_free[tb] = (sDV, ev)
                        stg_evac[sbl] = ev
                    xnb_free[bi] = (sMM, last_tp)

                    if tc % 4 == 3:
                        _w(sy, sDV, stg_evac[sbl])
                        stg_wr[sbl] = sSL[sbl].inc(sy.dma_start(
                            out=xnt[:, :, sl * 512:(sl + 1) * 512],
                            in_=stg[sbl][:, :].rearrange("p (v t) -> p v t", t=512)), 16)
                        _w(te, sONE, winT_ready)
                        _w(te, sDV, stg_evac[sbl])
                        win_done = [0] * C4
                        for mc in range(C4):
                            bank_wait(mc)
                            for vc in range(VC32):
                                instr = te.matmul(
                                    pbank(mc),
                                    winTb[:, vc * DIN + mc * 128:vc * DIN + mc * 128 + 128],
                                    stg[sbl][:, vc * 512:(vc + 1) * 512],
                                    start=(vc == 0), stop=(vc == VC32 - 1))
                            win_done[mc] = sMM.inc(instr)
                        for mc in range(C4):
                            _w(ve, sMM, win_done[mc])
                            h_a_done = sDV.inc(ve.tensor_scalar(
                                out=h[:, mc * TOK + sl * 512:mc * TOK + (sl + 1) * 512],
                                in0=pbank(mc), scalar1=P["bin"][:, mc:mc + 1],
                                scalar2=None, op0=ALU.add))
                            bank_free[mc] = (sDV, h_a_done)
                        stg_free[sbl] = (sMM, win_done[C4 - 1])

                if "h_a" in dbg_aps:
                    _w(sy, sDV, h_a_done)
                    tv = sDBG.inc(sy.dma_start(out=dbg_aps["h_a"], in_=h[:, :]), 16)
                    _w(ve, sDBG, tv)
                for eng in (ve, sc, te, gp, sy):
                    _w(eng, sDV, sDV.n)
                    _w(eng, sSC, sSC.n)
                    _w(eng, sMM, sMM.n)
                    for fam in (sX[0], sX[1], sSL[0], sSL[1]):
                        _w(eng, fam, fam.n)
                    _w(eng, sPRM, sPRM.n)
                    _w(eng, sONE, winT_ready)

            with ExitStack() as bctx:
                bctx.enter_context(nc.named_scope("stageB"))
                bsb = lambda n, shp, dt=F32: bctx.enter_context(nc.sbuf_tensor(n, shp, dt))
                zA = bsb("zA", [128, 2 * TOK])
                zB = bsb("zB", [128, 2 * TOK])
                zAr, zAi = zA[:, 0:TOK], zA[:, TOK:2 * TOK]
                zBr, zBi = zB[:, 0:TOK], zB[:, TOK:2 * TOK]
                y2buf = zA[:, :].bitcast(BF16)
                convr = bsb("convr", [128, C4 * TOK])
                y2in = convr[:, :].bitcast(BF16)[:, 0:C4 * TOK]
                y_c = bsb("y_c", [128, TOK])
                mu_all = bsb("mu_all", [128, TOK])
                rstd_all = bsb("rstd_all", [128, TOK])
                sq4 = bsb("sq4", [128, 4 * 512])
                var4 = sq4
                m2b4 = bsb("m2b4", [128, 4 * 512])
                w1t = bsb("w1t", [128, C4 * DFF], BF16)
                w2t = bsb("w2t", [128, FC16 * DIN], BF16)
                P1r = bsb("P1r", [128, C4 * Q])
                P1i = bsb("P1i", [128, C4 * Q])
                nP1i = bsb("nP1i", [128, C4 * Q])
                PBr = bsb("PBr", [128, C4 * NB])
                PBi = bsb("PBi", [128, C4 * NB])
                S0r = bsb("S0r", [128, C4 * NB])
                S0i = bsb("S0i", [128, C4 * NB])
                E2r = bsb("E2r", [128, NB]); E2i = bsb("E2i", [128, NB])
                T2r = bsb("T2r", [128, NB]); T2i = bsb("T2i", [128, NB])
                tA = bsb("tA", [128, 512]); tB = bsb("tB", [128, 512])
                Sinr4 = bsb("Sinr4", [128, C4 * NB])
                Sini4 = bsb("Sini4", [128, C4 * NB])
                ccrow = m2b4[0:1, 0:1024]
                Gsb = m2b4[0:8, 1024:2048]
                Ssel8 = bsb("Ssel8", [128, 8])
                Ic2 = bsb("Ic2", [128, 32])

                w1_ready = [0] * NLAY
                w2_ready = [0] * NLAY
                w1_lastmm = [0] * NLAY
                w2_lastmm = [0] * NLAY
                tI1 = w1t[:, :].bitcast(F32)[:, 0:TOK]
                tI2 = w1t[:, :].bitcast(F32)[:, TOK:2 * TOK]
                wu_ready = conv_chunks(sCVu, wu_bf, wu_i, NGE, 0, 5)

                def ln_stats():
                    _w(te, sDV, sDV.n)
                    _w(sc, sDV, sDV.n)
                    sq_done = [0] * C4
                    prev_grp = 0
                    for c in range(C4):
                        _w(sc, sMM, prev_grp)
                        for t4 in range(4):
                            sq_done[c] = sSC.inc(sc.activation(
                                out=sq4[:, t4 * 512:(t4 + 1) * 512],
                                in_=hsl(c, t4 * 512, (t4 + 1) * 512), func=AF.Square))
                        for t4 in range(4):
                            if c == 0:
                                bank_wait(t4)
                            te.matmul(pbank(t4), omt[:, :],
                                      hsl(c, t4 * 512, (t4 + 1) * 512),
                                      start=(c == 0), stop=(c == C4 - 1))
                        _w(te, sSC, sq_done[c])
                        for t4 in range(4):
                            if c == 0:
                                bank_wait(4 + t4)
                            instr = te.matmul(pbank(4 + t4), omt[:, :],
                                              sq4[:, t4 * 512:(t4 + 1) * 512],
                                              start=(c == 0), stop=(c == C4 - 1))
                        prev_grp = sMM.inc(instr)
                    mm_done = prev_grp
                    _w(sc, sMM, mm_done)
                    muc = 0
                    for t4 in range(4):
                        muc = sSC.inc(sc.copy(out=mu_all[:, t4 * 512:(t4 + 1) * 512],
                                              in_=pbank(t4)))
                    _w(ve, sSC, muc)
                    _w(ve, sMM, mm_done)
                    var_v = 0
                    for t4 in range(4):
                        ve.tensor_tensor(out=m2b4[:, t4 * 512:(t4 + 1) * 512],
                                         in0=mu_all[:, t4 * 512:(t4 + 1) * 512],
                                         in1=mu_all[:, t4 * 512:(t4 + 1) * 512],
                                         op=ALU.mult)
                        var_v = sDV.inc(ve.tensor_tensor(
                            out=var4[:, t4 * 512:(t4 + 1) * 512], in0=pbank(4 + t4),
                            in1=m2b4[:, t4 * 512:(t4 + 1) * 512], op=ALU.subtract))
                        bank_free[t4] = (sSC, muc)
                        bank_free[4 + t4] = (sDV, var_v)
                    _w(sc, sDV, var_v)
                    rstd_v = 0
                    for t4 in range(4):
                        sc.activation(out=m2b4[:, t4 * 512:(t4 + 1) * 512],
                                      in_=var4[:, t4 * 512:(t4 + 1) * 512], func=AF.Ln,
                                      bias=P["eps"][:, 0:1])
                    for t4 in range(4):
                        rstd_v = sSC.inc(sc.activation(
                            out=rstd_all[:, t4 * 512:(t4 + 1) * 512],
                            in_=m2b4[:, t4 * 512:(t4 + 1) * 512],
                            func=AF.Exp, scale=-0.5))
                    return rstd_v

                def vsync(instr):
                    v = sDV.inc(instr)
                    ve.wait_ge(sDV.h, v)
                    return v

                def ssync(instr):
                    v = sSC.inc(instr)
                    sc.wait_ge(sSC.h, v)
                    return v

                def ctab_all(Tr, Ti, width):
                    k = 1
                    while k < width:
                        n = min(k, width - k)
                        last = None
                        for c in range(C4):
                            o = c * width
                            cr, ci = Tr[:, o + k - 1:o + k], Ti[:, o + k - 1:o + k]
                            ve.tensor_scalar(out=tA[:, c * 64:c * 64 + n],
                                             in0=Tr[:, o:o + n], scalar1=cr,
                                             scalar2=None, op0=ALU.mult)
                            ve.tensor_scalar(out=tB[:, c * 64:c * 64 + n],
                                             in0=Ti[:, o:o + n], scalar1=ci,
                                             scalar2=None, op0=ALU.mult)
                            ve.tensor_scalar(out=tA[:, 256 + c * 64:256 + c * 64 + n],
                                             in0=Tr[:, o:o + n], scalar1=ci,
                                             scalar2=None, op0=ALU.mult)
                            last = ve.tensor_scalar(out=tB[:, 256 + c * 64:256 + c * 64 + n],
                                                    in0=Ti[:, o:o + n], scalar1=cr,
                                                    scalar2=None, op0=ALU.mult)
                        vsync(last)
                        for c in range(C4):
                            o = c * width
                            ve.tensor_tensor(out=Tr[:, o + k:o + k + n],
                                             in0=tA[:, c * 64:c * 64 + n],
                                             in1=tB[:, c * 64:c * 64 + n], op=ALU.subtract)
                            last = ve.tensor_tensor(out=Ti[:, o + k:o + k + n],
                                                    in0=tA[:, 256 + c * 64:256 + c * 64 + n],
                                                    in1=tB[:, 256 + c * 64:256 + c * 64 + n],
                                                    op=ALU.add)
                        vsync(last)
                        k *= 2

                for l in range(NLAY):
                    lc = lambda t, c: P[t][:, l * 4 + c:l * 4 + c + 1]
                    _w(ve, sMM, sMM.n)
                    _w(sc, sMM, sMM.n)

                    rstd1 = ln_stats()
                    last = None
                    for c in range(C4):
                        ve.tensor_copy(out=P1r[:, c * Q:c * Q + 1],
                                       in_=pcr[:, l * 4 + c:l * 4 + c + 1])
                        last = ve.tensor_copy(out=P1i[:, c * Q:c * Q + 1],
                                              in_=pci[:, l * 4 + c:l * 4 + c + 1])
                    vsync(last)
                    ctab_all(P1r, P1i, Q)
                    ve.tensor_scalar(out=nP1i[:, :], in0=P1i[:, :], scalar1=-1.0,
                                     scalar2=None, op0=ALU.mult)
                    last = None
                    for c in range(C4):
                        ve.tensor_copy(out=PBr[:, c * NB:c * NB + 1],
                                       in_=P1r[:, c * Q + Q - 1:c * Q + Q])
                        last = ve.tensor_copy(out=PBi[:, c * NB:c * NB + 1],
                                              in_=P1i[:, c * Q + Q - 1:c * Q + Q])
                    vsync(last)
                    ctab_all(PBr, PBi, NB)
                    for c in range(C4):
                        P1r_c = P1r[:, c * Q:(c + 1) * Q]
                        P1i_c = P1i[:, c * Q:(c + 1) * Q]
                        nP1i_c = nP1i[:, c * Q:(c + 1) * Q]
                        PBr_c = PBr[:, c * NB:(c + 1) * NB]
                        PBi_c = PBi[:, c * NB:(c + 1) * NB]
                        S0r_c = S0r[:, c * NB:(c + 1) * NB]
                        S0i_c = S0i[:, c * NB:(c + 1) * NB]
                        convr_c = convr[:, c * TOK:(c + 1) * TOK]
                        _w(ve, sSC, rstd1)
                        ve.tensor_tensor(out=y_c[:, :], in0=hsl(c, 0, TOK),
                                         in1=mu_all[:, :], op=ALU.subtract)
                        ve.tensor_tensor(out=y_c[:, :], in0=y_c[:, :],
                                         in1=rstd_all[:, :], op=ALU.mult)
                        yv = sDV.inc(ve.tensor_scalar(out=y_c[:, :], in0=y_c[:, :],
                                                      scalar1=lc("l1w", c),
                                                      scalar2=lc("l1b", c),
                                                      op0=ALU.mult, op1=ALU.add))
                        _w(sc, sDV, yv)
                        sil_v = sSC.inc(sc.activation(out=y_c[:, :], in_=y_c[:, :],
                                                      func=AF.Silu))
                        _w(ve, sSC, sil_v)
                        yscl = sDV.inc(ve.tensor_scalar(out=y_c[:, :], in0=y_c[:, :],
                                                        scalar1=lc("scl", c), scalar2=None,
                                                        op0=ALU.mult))
                        if "y0" in dbg_aps and l == 0 and c == 0:
                            ydbg = sDV.inc(ve.nop())
                            _w(sy, sDV, ydbg)
                            tv = sDBG.inc(sy.dma_start(out=dbg_aps["y0"], in_=y_c[:, :]), 16)
                            _w(ve, sDBG, tv)
                        y3 = y_c[:, :].rearrange("p (b q) -> p b q", q=Q)
                        c3r = zBr.rearrange("p (b q) -> p b q", q=Q)
                        c3i = zBi.rearrange("p (b q) -> p b q", q=Q)
                        i3a = tI1[:, :].rearrange("p (b q) -> p b q", q=Q)
                        i3b = tI2[:, :].rearrange("p (b q) -> p b q", q=Q)
                        ve.scalar_tensor_tensor(out=c3r[:, :, 1:Q], in0=y3[:, :, 0:Q - 1],
                                                scalar=P1r_c[:, 0:1], in1=y3[:, :, 1:Q],
                                                op0=ALU.mult, op1=ALU.add)
                        vstep = vsync(ve.tensor_copy(out=c3r[:, :, 0:1], in_=y3[:, :, 0:1]))
                        _w(sc, sDV, yscl)
                        sstep = sSC.inc(sc.activation(out=c3i[:, :, 1:Q],
                                                      in_=y3[:, :, 0:Q - 1], func=AF.Copy,
                                                      scale=P1i_c[:, 0:1]))
                        _w(gp, sSC, sstep)
                        gstep = sGP.inc(gp.memset(c3i[:, :, 0:1], 0.0))
                        pairs = [(zAr, zAi), (zBr, zBi)]
                        cur = 1
                        s = 2
                        while s < Q:
                            last = (s * 2 >= Q)
                            nxt = 1 - cur
                            cr_, ci_ = pairs[cur]
                            nr_ = convr_c if last else pairs[nxt][0]
                            ni_ = pairs[nxt][1]
                            a3r = cr_.rearrange("p (b q) -> p b q", q=Q)
                            a3i = ci_.rearrange("p (b q) -> p b q", q=Q)
                            b3r = nr_.rearrange("p (b q) -> p b q", q=Q)
                            b3i = ni_.rearrange("p (b q) -> p b q", q=Q)
                            _w(sc, sDV, vstep)
                            _w(sc, sGP, gstep)
                            sc.activation(out=i3a[:, :, s:Q], in_=a3r[:, :, 0:Q - s],
                                          func=AF.Copy, scale=P1i_c[:, s - 1:s])
                            sstep = sSC.inc(sc.activation(out=i3b[:, :, s:Q],
                                                          in_=a3i[:, :, 0:Q - s],
                                                          func=AF.Copy,
                                                          scale=P1r_c[:, s - 1:s]))
                            _w(ve, sGP, gstep)
                            ve.scalar_tensor_tensor(out=b3r[:, :, s:Q],
                                                    in0=a3r[:, :, 0:Q - s],
                                                    scalar=P1r_c[:, s - 1:s],
                                                    in1=a3r[:, :, s:Q],
                                                    op0=ALU.mult, op1=ALU.add)
                            ve.scalar_tensor_tensor(out=b3r[:, :, s:Q],
                                                    in0=a3i[:, :, 0:Q - s],
                                                    scalar=nP1i_c[:, s - 1:s],
                                                    in1=b3r[:, :, s:Q],
                                                    op0=ALU.mult, op1=ALU.add)
                            vstep = vsync(ve.tensor_copy(out=b3r[:, :, 0:s],
                                                         in_=a3r[:, :, 0:s]))
                            _w(gp, sSC, sstep)
                            g1 = sGP.inc(gp.tensor_tensor(out=b3i[:, :, s:Q],
                                                          in0=i3a[:, :, s:Q],
                                                          in1=a3i[:, :, s:Q], op=ALU.add))
                            gp.wait_ge(sGP.h, g1)
                            gp.tensor_tensor(out=b3i[:, :, s:Q], in0=i3b[:, :, s:Q],
                                             in1=b3i[:, :, s:Q], op=ALU.add)
                            g2 = sGP.inc(gp.tensor_copy(out=b3i[:, :, 0:s],
                                                        in_=a3i[:, :, 0:s]))
                            gp.wait_ge(sGP.h, g2)
                            gstep = g2
                            cur = nxt
                            s *= 2
                        zfi = pairs[cur][1]
                        cv3 = convr_c.rearrange("p (b q) -> p b q", q=Q)
                        zi3 = zfi.rearrange("p (b q) -> p b q", q=Q)
                        _w(ve, sGP, gstep)
                        ve.tensor_copy(out=E2r[:, :], in_=cv3[:, :, Q - 1:Q])
                        vsync(ve.tensor_copy(out=E2i[:, :], in_=zi3[:, :, Q - 1:Q]))
                        srcs = [(E2r, E2i), (T2r, T2i)]
                        cu = 0
                        s = 1
                        while s < NB:
                            last = (s * 2 >= NB)
                            sr, si_ = srcs[cu]
                            dr, di_ = (S0r_c, S0i_c) if last else srcs[1 - cu]
                            ve.scalar_tensor_tensor(out=dr[:, s:NB], in0=sr[:, 0:NB - s],
                                                    scalar=PBr_c[:, s - 1:s],
                                                    in1=sr[:, s:NB],
                                                    op0=ALU.mult, op1=ALU.add)
                            ve.scalar_tensor_tensor(out=di_[:, s:NB], in0=sr[:, 0:NB - s],
                                                    scalar=PBi_c[:, s - 1:s],
                                                    in1=si_[:, s:NB],
                                                    op0=ALU.mult, op1=ALU.add)
                            vsync(ve.tensor_scalar(out=tA[:, 0:NB - s],
                                                   in0=si_[:, 0:NB - s],
                                                   scalar1=PBi_c[:, s - 1:s], scalar2=None,
                                                   op0=ALU.mult))
                            ve.tensor_tensor(out=dr[:, s:NB], in0=dr[:, s:NB],
                                             in1=tA[:, 0:NB - s], op=ALU.subtract)
                            ve.scalar_tensor_tensor(out=di_[:, s:NB], in0=si_[:, 0:NB - s],
                                                    scalar=PBr_c[:, s - 1:s],
                                                    in1=di_[:, s:NB],
                                                    op0=ALU.mult, op1=ALU.add)
                            ve.tensor_copy(out=dr[:, 0:s], in_=sr[:, 0:s])
                            vsync(ve.tensor_copy(out=di_[:, 0:s], in_=si_[:, 0:s]))
                            cu = 1 - cu
                            s *= 2
                        s0c = sDV.n
                        _w(te, sDV, s0c)
                        bank_wait(6)
                        te.transpose(pbank(6)[0:1, 0:128], S0r_c[:, NB - 1:NB], identf[:, :])
                        tpv = sMM.inc(te.transpose(pbank(6)[0:1, 128:256],
                                                   S0i_c[:, NB - 1:NB], identf[:, :]))
                        _w(ve, sMM, tpv)
                        ve.tensor_copy(out=ccrow[0:1, c * 128:(c + 1) * 128],
                                       in_=pbank(6)[0:1, 0:128])
                        ccv = sDV.inc(ve.tensor_copy(
                            out=ccrow[0:1, 512 + c * 128:512 + (c + 1) * 128],
                            in_=pbank(6)[0:1, 128:256]))
                        bank_free[6] = (sDV, ccv)

                    _w(gp, sDV, sDV.n)
                    _w(gp, sSC, sSC.n)
                    cc_wr = sCCD.inc(gp.dma_start(out=cc_in, in_=ccrow[0:1, :]), 16)
                    _w(gp, sCCD, cc_wr)
                    cc_done = sCC.inc(gp.collective_compute(
                        "AllGather", ALU.bypass, replica_groups=[list(range(NCORES))],
                        ins=[cc_in.opt()], outs=[cc_out.opt()]))
                    _w(gp, sCC, cc_done)
                    g_ld = sCCD.inc(gp.dma_start(out=Gsb[:, :], in_=cc_out), 16)
                    if l > 0:
                        _w(gp, sMM, w1_lastmm[l - 1])
                        _w(gp, sMM, w2_lastmm[l - 1])
                    w1_ready[l] = sW1.inc(gp.dma_start(out=w1t[:, :], in_=w1T[l],
                                                       max_dma_last_dim=8192), 16)
                    w2_ready[l] = sW2.inc(gp.dma_start(out=w2t[:, :], in_=w2T[l],
                                                       max_dma_last_dim=8192), 16)
                    NWDE = HG * IC * 128 * 1024
                    if l == 0:
                        wu_ready = conv_chunks(sCVu, wu_bf, wu_i, NGE, 5, 11)
                    elif l == 1:
                        wd_ready = conv_chunks(sCVd, wd_bf, wd_i, NWDE, 0, 6)
                    elif l == 2:
                        wd_ready = conv_chunks(sCVd, wd_bf, wd_i, NWDE, 6, 11)

                    _w(te, sCCD, g_ld)
                    bank_wait(7)
                    for c in range(C4):
                        te.matmul(pbank(7)[:, c:c + 1], Gsb[0:8, c * 128:(c + 1) * 128],
                                  selsb[:, :], start=True, stop=True)
                        instr = te.matmul(pbank(7)[:, 4 + c:5 + c],
                                          Gsb[0:8, 512 + c * 128:512 + (c + 1) * 128],
                                          selsb[:, :], start=True, stop=True)
                    selv = sMM.inc(instr)
                    _w(ve, sMM, selv)
                    sv2 = vsync(ve.tensor_copy(out=Ssel8[:, :], in_=pbank(7)[:, 0:8]))
                    bank_free[7] = (sDV, sv2)
                    L4 = slice(l * 4, l * 4 + 4)
                    P2r4 = PBr[:, :].rearrange("p (c n) -> p c n", n=NB)[:, :, NB - 1:NB]
                    P2i4 = PBi[:, :].rearrange("p (c n) -> p c n", n=NB)[:, :, NB - 1:NB]
                    I_r = Ic2[:, 0:4]; I_i = Ic2[:, 4:8]; nI_i = Ic2[:, 8:12]
                    t1r = Ic2[:, 12:16]; t1i = Ic2[:, 16:20]
                    m1 = Ic2[:, 20:24]; m2 = Ic2[:, 24:28]; m3 = Ic2[:, 28:32]
                    p2r_f = tA[:, 0:4]; p2i_f = tA[:, 4:8]
                    ve.tensor_copy(out=p2r_f.rearrange("p (c n) -> p c n", n=1), in_=P2r4)
                    vsync(ve.tensor_copy(out=p2i_f.rearrange("p (c n) -> p c n", n=1), in_=P2i4))
                    ve.tensor_tensor(out=m1, in0=p2r_f, in1=lcsr[:, L4], op=ALU.mult)
                    ve.tensor_tensor(out=m2, in0=p2i_f, in1=lcsi[:, L4], op=ALU.mult)
                    ve.tensor_tensor(out=m3, in0=p2r_f, in1=lcsi[:, L4], op=ALU.mult)
                    vsync(ve.tensor_tensor(out=t1i, in0=p2i_f, in1=lcsr[:, L4], op=ALU.mult))
                    ve.tensor_tensor(out=t1r, in0=m1, in1=m2, op=ALU.subtract)
                    vsync(ve.tensor_tensor(out=m1, in0=m3, in1=t1i, op=ALU.add))
                    ve.tensor_tensor(out=m2, in0=t1r, in1=Ssel8[:, 0:4], op=ALU.add)
                    vsync(ve.tensor_tensor(out=m3, in0=m1, in1=Ssel8[:, 4:8], op=ALU.add))
                    ve.tensor_scalar(out=t1r, in0=m2, scalar1=P["mv"][:, 0:1],
                                     scalar2=None, op0=ALU.mult)
                    vsync(ve.tensor_scalar(out=t1i, in0=m3, scalar1=P["mv"][:, 0:1],
                                           scalar2=None, op0=ALU.mult))
                    ve.tensor_scalar(out=m1, in0=lcsr[:, L4], scalar1=P["mb"][:, 0:1],
                                     scalar2=None, op0=ALU.mult)
                    vsync(ve.tensor_scalar(out=m2, in0=lcsi[:, L4], scalar1=P["mb"][:, 0:1],
                                           scalar2=None, op0=ALU.mult))
                    ve.tensor_tensor(out=I_r, in0=m1, in1=t1r, op=ALU.add)
                    vsync(ve.tensor_tensor(out=I_i, in0=m2, in1=t1i, op=ALU.add))
                    vsync(ve.tensor_scalar(out=nI_i, in0=I_i, scalar1=-1.0, scalar2=None,
                                           op0=ALU.mult))
                    last = None
                    for c in range(C4):
                        ve.tensor_copy(out=Sinr4[:, c * NB:c * NB + 1], in_=I_r[:, c:c + 1])
                        last = ve.tensor_copy(out=Sini4[:, c * NB:c * NB + 1],
                                              in_=I_i[:, c:c + 1])
                    vsync(last)
                    for c in range(C4):
                        last = ve.scalar_tensor_tensor(
                            out=Sinr4[:, c * NB + 1:(c + 1) * NB],
                            in0=PBr[:, c * NB:(c + 1) * NB - 1], scalar=I_r[:, c:c + 1],
                            in1=S0r[:, c * NB:(c + 1) * NB - 1], op0=ALU.mult, op1=ALU.add)
                    vsync(last)
                    for c in range(C4):
                        last = ve.scalar_tensor_tensor(
                            out=Sinr4[:, c * NB + 1:(c + 1) * NB],
                            in0=PBi[:, c * NB:(c + 1) * NB - 1], scalar=nI_i[:, c:c + 1],
                            in1=Sinr4[:, c * NB + 1:(c + 1) * NB], op0=ALU.mult, op1=ALU.add)
                    vsync(last)
                    for c in range(C4):
                        last = ve.scalar_tensor_tensor(
                            out=Sini4[:, c * NB + 1:(c + 1) * NB],
                            in0=PBr[:, c * NB:(c + 1) * NB - 1], scalar=I_i[:, c:c + 1],
                            in1=S0i[:, c * NB:(c + 1) * NB - 1], op0=ALU.mult, op1=ALU.add)
                    vsync(last)
                    for c in range(C4):
                        last = ve.scalar_tensor_tensor(
                            out=Sini4[:, c * NB + 1:(c + 1) * NB],
                            in0=PBi[:, c * NB:(c + 1) * NB - 1], scalar=I_r[:, c:c + 1],
                            in1=Sini4[:, c * NB + 1:(c + 1) * NB], op0=ALU.mult, op1=ALU.add)
                    vsync(last)
                    if "cv0" in dbg_aps and l == 0:
                        cdbg = sDV.inc(ve.nop())
                        _w(sy, sDV, cdbg)
                        tv = sDBG.inc(sy.dma_start(out=dbg_aps["cv0"], in_=convr[:, 0:TOK]), 16)
                        _w(ve, sDBG, tv)
                    last = None
                    for c in range(C4):
                        last = ve.tensor_tensor(out=h[:, c * TOK:(c + 1) * TOK],
                                                in0=h[:, c * TOK:(c + 1) * TOK],
                                                in1=convr[:, c * TOK:(c + 1) * TOK],
                                                op=ALU.add)
                    vsync(last)
                    for c in range(C4):
                        h3 = h[:, c * TOK:(c + 1) * TOK].rearrange("p (b q) -> p b q", q=Q)
                        for j in range(NB):
                            last = ve.scalar_tensor_tensor(
                                out=h3[:, j, :], in0=P1r[:, c * Q:(c + 1) * Q],
                                scalar=Sinr4[:, c * NB + j:c * NB + j + 1],
                                in1=h3[:, j, :], op0=ALU.mult, op1=ALU.add)
                    vsync(last)
                    for c in range(C4):
                        h3 = h[:, c * TOK:(c + 1) * TOK].rearrange("p (b q) -> p b q", q=Q)
                        for j in range(NB):
                            last = ve.scalar_tensor_tensor(
                                out=h3[:, j, :], in0=nP1i[:, c * Q:(c + 1) * Q],
                                scalar=Sini4[:, c * NB + j:c * NB + j + 1],
                                in1=h3[:, j, :], op0=ALU.mult, op1=ALU.add)
                    vsync(last)

                    rstd2 = ln_stats()
                    y2_done = 0
                    for c in range(C4):
                        _w(ve, sSC, rstd2)
                        ve.tensor_tensor(out=y_c[:, :], in0=hsl(c, 0, TOK),
                                         in1=mu_all[:, :], op=ALU.subtract)
                        ve.tensor_tensor(out=y_c[:, :], in0=y_c[:, :],
                                         in1=rstd_all[:, :], op=ALU.mult)
                        y2_done = sDV.inc(ve.tensor_scalar(
                            out=y2in[:, c * TOK:(c + 1) * TOK], in0=y_c[:, :],
                            scalar1=lc("l2w", c), scalar2=lc("l2b", c),
                            op0=ALU.mult, op1=ALU.add))

                    _w(te, sW1, w1_ready[l])
                    _w(te, sW2, w2_ready[l])
                    _w(te, sDV, y2_done)
                    for t4 in range(4):
                        sil_last = 0
                        for fc in range(FC16):
                            bk = fc % 2
                            bank_wait(bk)
                            for kc in range(C4):
                                instr = te.matmul(
                                    pbank(bk),
                                    w1t[:, kc * DFF + fc * 128:kc * DFF + fc * 128 + 128],
                                    y2in[:, kc * TOK + t4 * 512:kc * TOK + (t4 + 1) * 512],
                                    start=(kc == 0), stop=(kc == C4 - 1))
                            w1v = sMM.inc(instr)
                            w1_lastmm[l] = w1v
                            _w(sc, sMM, w1v)
                            sil_last = sSC.inc(sc.activation(
                                out=y2buf[:, fc * 512:(fc + 1) * 512], in_=pbank(bk),
                                func=AF.Silu,
                                bias=P["b1"][:, l * FC16 + fc:l * FC16 + fc + 1]))
                            bank_free[bk] = (sSC, sil_last)
                        _w(te, sSC, sil_last)
                        for c in range(C4):
                            bk = 2 + c % 2
                            bank_wait(bk)
                            for fc in range(FC16):
                                instr = te.matmul(
                                    pbank(bk),
                                    w2t[:, fc * DIN + c * 128:fc * DIN + c * 128 + 128],
                                    y2buf[:, fc * 512:(fc + 1) * 512],
                                    start=(fc == 0), stop=(fc == FC16 - 1))
                            w2v = sMM.inc(instr)
                            w2_lastmm[l] = w2v
                            _w(ve, sMM, w2v)
                            hv = sDV.inc(ve.scalar_tensor_tensor(
                                out=hsl(c, t4 * 512, (t4 + 1) * 512), in0=pbank(bk),
                                scalar=lc("b2", c),
                                in1=hsl(c, t4 * 512, (t4 + 1) * 512),
                                op0=ALU.add, op1=ALU.add))
                            bank_free[bk] = (sDV, hv)

                if "h_f" in dbg_aps:
                    _w(sy, sDV, sDV.n)
                    tv = sDBG.inc(sy.dma_start(out=dbg_aps["h_f"], in_=h[:, :]), 16)
                    _w(ve, sDBG, tv)

                rstdf = ln_stats()
                hf_done = 0
                for c in range(C4):
                    _w(ve, sSC, rstdf)
                    ve.tensor_tensor(out=y_c[:, :], in0=hsl(c, 0, TOK), in1=mu_all[:, :],
                                     op=ALU.subtract)
                    ve.tensor_tensor(out=y_c[:, :], in0=y_c[:, :], in1=rstd_all[:, :],
                                     op=ALU.mult)
                    hf_done = sDV.inc(ve.tensor_scalar(
                        out=hf[:, c * TOK:(c + 1) * TOK], in0=y_c[:, :],
                        scalar1=P["lnlw"][:, c:c + 1], scalar2=P["lnlb"][:, c:c + 1],
                        op0=ALU.mult, op1=ALU.add))

                for eng in (ve, sc, te, gp, sy):
                    _w(eng, sDV, sDV.n)
                    _w(eng, sSC, sSC.n)
                    _w(eng, sMM, sMM.n)
                    _w(eng, sW1, sW1.n)
                    _w(eng, sW2, sW2.n)
                    _w(eng, sCCD, sCCD.n)

        with ExitStack() as cctx:
            cctx.enter_context(nc.named_scope("stageC"))
            csb = lambda n, shp, dt=F32: cctx.enter_context(nc.sbuf_tensor(n, shp, dt))
            hh = csb("hh", [128, IC * 512], BF16)
            boutb = csb("boutb", [1, V], BF16)
            bout_ready = sONE.inc(gp.dma_start(out=boutb[:, :], in_=bout_in,
                                               max_dma_last_dim=8192), 16)
            OV = csb("OV", [128, 18432], F32)
            OVb = OV[:, :].bitcast(BF16)
            xnTs = OVb[:, 0:16384]
            wgbuf = [OVb[:, 16384 + i * 4096:16384 + (i + 1) * 4096] for i in range(2)]
            wubuf = [OVb[:, 24576 + i * 4096:24576 + (i + 1) * 4096] for i in range(2)]
            gsil = [OV[:, 16384 + i * 512:16384 + (i + 1) * 512] for i in range(2)]
            wdbuf = [OVb[:, i * 4096:(i + 1) * 4096] for i in range(2)]
            wobuf = [OVb[:, 8192 + i * 4096:8192 + (i + 1) * 4096] for i in range(2)]
            xstg = OV[:, 8192:12288]
            ostg = OV[:, 12288:16384]
            dsb = [OV[:, 16384 + i * 512:16384 + (i + 1) * 512] for i in range(2)]
            sumb = [OV[:, 17408 + i * 512:17408 + (i + 1) * 512] for i in range(2)]

            x4 = x_in.rearrange("(s b p) (g c) -> p s g b c", p=128, b=4, c=1024)
            o4 = out_t.rearrange("(s b p) (g c) -> p s g b c", p=128, b=4, c=1024)

            gbank_free = [(sDV, 0)] * 2
            ubank_free = [(sDV, 0)] * 2
            gsil_free = [(sDV, 0)] * 2
            wg_cons = [(sMM, 0)] * 2
            wu_cons = [(sMM, 0)] * 2
            wo_cons = [(sMM, 0)] * 2
            ostg_free = (sOS[0], 0)
            xstg_free = (sDV, 0)
            prev_down_reads = 0
            prev_store = 0

            for s in range(4):
                _w(sy, sDV, prev_down_reads)
                _w(sy, sOS[0], sOS[0].n)
                _w(sy, sOS[1], sOS[1].n)
                _w(sy, sCVg, wg_ready)
                _w(sc, sOS[0], sOS[0].n)
                _w(sc, sOS[1], sOS[1].n)
                xnT_ld = sXNT.inc(sy.dma_start(
                    out=xnTs.rearrange("p (v t) -> p v t", t=512),
                    in_=xnt[:, :, s * 512:(s + 1) * 512]), 16)

                gate_end = 0
                for i in range(IC):
                    bi = i % 2
                    s_, v_ = wg_cons[bi]
                    _w(sy, s_, v_)
                    wg_ld = sWG[bi].inc(sy.dma_start(out=wgbuf[bi], in_=wg_bf[i]), 16)
                    s_, v_ = wu_cons[bi]
                    _w(sy, s_, v_)
                    if i < 2:
                        _w(sy, sCVu, wu_ready)
                    wu_ld = sWU[bi].inc(sy.dma_start(out=wubuf[bi], in_=wu_bf[i]), 16)
                    _w(te, sWG[bi], wg_ld)
                    if i < 2:
                        _w(te, sXNT, xnT_ld)
                    s_, v_ = gbank_free[bi]
                    _w(te, s_, v_)
                    for hc in range(VC32):
                        instr = te.matmul(pbank(bi), wgbuf[bi][:, hc * 128:(hc + 1) * 128],
                                          xnTs[:, hc * 512:(hc + 1) * 512],
                                          start=(hc == 0), stop=(hc == VC32 - 1))
                    g_done = sMM.inc(instr)
                    wg_cons[bi] = (sMM, g_done)
                    _w(te, sWU[bi], wu_ld)
                    s_, v_ = ubank_free[bi]
                    _w(te, s_, v_)
                    for hc in range(VC32):
                        instr = te.matmul(pbank(2 + bi),
                                          wubuf[bi][:, hc * 128:(hc + 1) * 128],
                                          xnTs[:, hc * 512:(hc + 1) * 512],
                                          start=(hc == 0), stop=(hc == VC32 - 1))
                    u_done = sMM.inc(instr)
                    wu_cons[bi] = (sMM, u_done)
                    gate_end = u_done
                    s_, v_ = gsil_free[bi]
                    _w(sc, s_, v_)
                    _w(sc, sMM, g_done)
                    sil_v = sSC.inc(sc.activation(out=gsil[bi], in_=pbank(bi),
                                                  func=AF.Silu,
                                                  scale=P["sg"][:, i:i + 1]))
                    gbank_free[bi] = (sSC, sil_v)
                    _w(ve, sMM, u_done)
                    _w(ve, sSC, sil_v)
                    hh_v = sDV.inc(ve.scalar_tensor_tensor(
                        out=hh[:, i * 512:(i + 1) * 512], in0=pbank(2 + bi),
                        scalar=P["su"][:, i:i + 1], in1=gsil[bi],
                        op0=ALU.mult, op1=ALU.mult))
                    ubank_free[bi] = (sDV, hh_v)
                    gsil_free[bi] = (sDV, hh_v)
                hh_done = sDV.n

                bank_free[0] = gbank_free[0]
                bank_free[1] = gbank_free[1]
                bank_free[2] = ubank_free[0]
                bank_free[3] = ubank_free[1]

                wd_cons = [(sMM, 0)] * 2
                per_hc_add = 0
                NWD = (IC + 3) // 4
                for hg in range(HG):
                    s_, v_ = xstg_free
                    _w(sc, s_, v_)
                    if hg == 0:
                        _w(sc, sMM, gate_end)
                    xs_ld = sXS.inc(sc.dma_start(
                        out=xstg.rearrange("p (b c) -> p b c", c=1024),
                        in_=x4[:, s, hg]), 16)
                    s_, v_ = wo_cons[hg % 2]
                    _w(gp, s_, v_)
                    _w(gp, sMM, gate_end)
                    if s == 0 and hg == 0:
                        _w(gp, sCVd, wd_ready)
                        _w(gp, sONE, bout_ready)
                    wo_ld = sWO[hg % 2].inc(gp.dma_start(
                        out=wobuf[hg % 2].rearrange("p (f c) -> p f c", c=1024),
                        in_=woT[hg].rearrange("f p c -> p f c"),
                        max_dma_last_dim=8192), 16)
                    down_done = 0
                    for ib in range(NWD):
                        bi = ib % 2
                        i0 = ib * 4
                        ni = min(4, IC - i0)
                        s_, v_ = wd_cons[bi]
                        _w(sy, s_, v_)
                        if ib < 2:
                            _w(sy, sMM, gate_end)
                        wd_ld = sWD[bi].inc(sy.dma_start(
                            out=wdbuf[bi][:, 0:ni * 1024].rearrange("p (i c) -> p i c", c=1024),
                            in_=wd_bf[hg, i0:i0 + ni].rearrange("i p c -> p i c")), 16)
                        _w(te, sWD[bi], wd_ld)
                        if ib == 0:
                            _w(te, sDV, hh_done)
                        for ii in range(ni):
                            i = i0 + ii
                            for hc in range(8):
                                if i == 0:
                                    bank_wait(hc)
                                instr = te.matmul(
                                    pbank(hc),
                                    wdbuf[bi][:, ii * 1024 + hc * 128:ii * 1024 + (hc + 1) * 128],
                                    hh[:, i * 512:(i + 1) * 512],
                                    start=(i == 0), stop=(i == IC - 1))
                        grp_v = sMM.inc(instr)
                        wd_cons[bi] = (sMM, grp_v)
                        down_done = grp_v
                    dv = [0] * 8
                    wo_done = [0] * 8
                    sm = [0] * 8
                    tp = [0] * 8

                    def ve_dsb(hc):
                        _w(ve, sMM, down_done)
                        dv[hc] = sDV.inc(ve.tensor_scalar(
                            out=dsb[hc % 2], in0=pbank(hc),
                            scalar1=P["sd"][:, hg * 8 + hc:hg * 8 + hc + 1],
                            scalar2=None, op0=ALU.mult))

                    def te_wout(hc):
                        _w(te, sDV, dv[hc])
                        if hc == 0:
                            _w(te, sWO[hg % 2], wo_ld)
                        for fc in range(C4):
                            te.matmul(pbank(hc),
                                      wobuf[hg % 2][:, fc * 1024 + hc * 128:fc * 1024 + hc * 128 + 128],
                                      hf[:, fc * TOK + s * 512:fc * TOK + (s + 1) * 512],
                                      start=(fc == 0), stop=False)
                        wo_done[hc] = sMM.inc(te.matmul(
                            pbank(hc),
                            boutb[0:1, hg * 1024 + hc * 128:hg * 1024 + hc * 128 + 128],
                            onesb[0:1, :], start=False, stop=True))

                    def ve_sum(hc):
                        _w(ve, sMM, wo_done[hc])
                        if hc >= 2:
                            _w(ve, sMM, tp[hc - 2])
                        sm[hc] = sDV.inc(ve.tensor_tensor(out=sumb[hc % 2], in0=pbank(hc),
                                                          in1=dsb[hc % 2], op=ALU.add))

                    def te_tp(hc):
                        _w(te, sDV, sm[hc])
                        for tb in range(4):
                            instr = te.transpose(pbank(hc)[:, tb * 128:(tb + 1) * 128],
                                                 sumb[hc % 2][:, tb * 128:(tb + 1) * 128],
                                                 identf[:, :])
                        tp[hc] = sMM.inc(instr)

                    def ve_add(hc):
                        nonlocal per_hc_add
                        _w(ve, sMM, tp[hc])
                        _w(ve, sXS, xs_ld)
                        if hc == 0:
                            s_, v_ = ostg_free
                            _w(ve, s_, v_)
                        for tb in range(4):
                            per_hc_add = sDV.inc(ve.tensor_tensor(
                                out=ostg[:, tb * 1024 + hc * 128:tb * 1024 + hc * 128 + 128],
                                in0=pbank(hc)[:, tb * 128:(tb + 1) * 128],
                                in1=xstg[:, tb * 1024 + hc * 128:tb * 1024 + hc * 128 + 128],
                                op=ALU.add))
                        bank_free[hc] = (sDV, per_hc_add)

                    ve_dsb(0)
                    ve_dsb(1)
                    te_wout(0)
                    ve_sum(0)
                    te_tp(0)
                    for hc in range(1, 8):
                        te_wout(hc)
                        if hc + 1 < 8:
                            ve_dsb(hc + 1)
                        ve_sum(hc)
                        te_tp(hc)
                        ve_add(hc - 1)
                    ve_add(7)
                    wo_cons[hg % 2] = (sMM, tp[7])
                    _w(sc, sDV, per_hc_add)
                    prev_store = sOS[hg % 2].inc(sc.dma_start(
                        out=o4[:, s, hg],
                        in_=ostg.rearrange("p (b c) -> p b c", c=1024)), 16)
                    ostg_free = (sOS[hg % 2], prev_store)
                    xstg_free = (sDV, per_hc_add)
                prev_down_reads = per_hc_add
                gbank_free = [bank_free[0], bank_free[1]]
                ubank_free = [bank_free[2], bank_free[3]]

            _w(sy, sOS[0], sOS[0].n)
            _w(sy, sOS[1], sOS[1].n)
            _w(sy, sDBG, sDBG.n)

    return nc, dbg_aps


def host_inputs(L0_attn_x, L0_attn_Win, L0_attn_bin, L0_attn_ln1_w, L0_attn_ln1_b,
                L0_attn_phz_re, L0_attn_phz_im, L0_attn_lc_re, L0_attn_lc_im,
                L0_attn_sc_lin, L0_attn_ln2_w, L0_attn_ln2_b, L0_attn_W1, L0_attn_b1,
                L0_attn_W2, L0_attn_b2, L0_attn_lnl_w, L0_attn_lnl_b, L0_attn_Wout,
                L0_attn_bout, L0_ffn_w_gate, L0_ffn_s_gate, L0_ffn_w_up, L0_ffn_s_up,
                L0_ffn_w_down, L0_ffn_s_down):
    f32 = np.float32
    cc = np.ascontiguousarray

    def pc(v, k):
        return cc(np.asarray(v, f32).reshape(k, 128).T)

    def pc_l(v):
        return cc(np.asarray(v, f32).reshape(NLAY, 4, 128).transpose(2, 0, 1)
                  .reshape(128, 16))

    winT_h = cc(np.asarray(L0_attn_Win, f32).T.reshape(VC32, 128, DIN)
                .transpose(1, 0, 2).reshape(128, VC32 * DIN))
    w1T_h = cc(np.asarray(L0_attn_W1, f32).transpose(0, 2, 1)
               .reshape(NLAY, C4, 128, DFF).transpose(0, 2, 1, 3)
               .reshape(NLAY, 128, C4 * DFF))
    w2T_h = cc(np.asarray(L0_attn_W2, f32).transpose(0, 2, 1)
               .reshape(NLAY, FC16, 128, DIN).transpose(0, 2, 1, 3)
               .reshape(NLAY, 128, FC16 * DIN))
    woT_h = cc(np.asarray(L0_attn_Wout, f32).T.reshape(C4, 128, HG, 1024)
               .transpose(2, 0, 1, 3))
    wg_h = cc(np.asarray(L0_ffn_w_gate).T.reshape(VC32, 128, IC, 128)
              .transpose(2, 1, 0, 3).reshape(IC, 128, 4096).astype(np.int32))
    wu_h = cc(np.asarray(L0_ffn_w_up).T.reshape(VC32, 128, IC, 128)
              .transpose(2, 1, 0, 3).reshape(IC, 128, 4096).astype(np.int32))
    wd_h = cc(np.asarray(L0_ffn_w_down).T.reshape(IC, 128, HG, 1024)
              .transpose(2, 0, 1, 3).astype(np.int32))

    base = dict(
        winT=winT_h, w1T=w1T_h, w2T=w2T_h, woT=woT_h, wg_i=wg_h, wu_i=wu_h, wd_i=wd_h,
        p_bin=pc(L0_attn_bin, C4), p_l1w=pc_l(L0_attn_ln1_w), p_l1b=pc_l(L0_attn_ln1_b),
        p_l2w=pc_l(L0_attn_ln2_w), p_l2b=pc_l(L0_attn_ln2_b), p_scl=pc_l(L0_attn_sc_lin),
        p_phr=pc_l(L0_attn_phz_re), p_phi=pc_l(L0_attn_phz_im),
        p_lcr=pc_l(L0_attn_lc_re), p_lci=pc_l(L0_attn_lc_im),
        p_b2=pc_l(L0_attn_b2),
        p_lnlw=pc(L0_attn_lnl_w, C4), p_lnlb=pc(L0_attn_lnl_b, C4),
        p_b1=cc(np.asarray(L0_attn_b1, f32).reshape(NLAY, FC16, 128)
                .transpose(2, 0, 1).reshape(128, NLAY * FC16)),
        p_sg=pc(L0_ffn_s_gate, IC), p_su=pc(L0_ffn_s_up, IC), p_sd=pc(L0_ffn_s_down, 32),
        p_eps=np.full((128, 1), EPS, f32),
        ident_b=np.eye(128, dtype=ml_dtypes.bfloat16),
        ident_f=np.eye(128, dtype=f32),
        om=np.full((128, 128), 1.0 / DIN, f32),
        bout=cc(np.asarray(L0_attn_bout, f32).reshape(1, V)),
        onesr=np.ones((1, 512), ml_dtypes.bfloat16),
    )
    x = np.asarray(L0_attn_x, f32)
    in_maps = []
    for core in range(NCORES):
        b, half = core // 2, core % 2
        m = dict(base)
        m["x"] = cc(x[b, half * TOK:(half + 1) * TOK, :])
        m["p_mv"] = np.full((128, 1), float(half), f32)
        m["p_mb"] = np.full((128, 1), 1.0 - float(half), f32)
        selv = np.zeros((8, 1), f32)
        selv[core - 1 if half == 1 else core] = 1.0
        m["sel"] = selv
        in_maps.append(m)
    return in_maps


_NC_CACHE = {}


def kernel(**inputs):
    if "main" not in _NC_CACHE:
        _NC_CACHE["main"] = build()
    nc, _ = _NC_CACHE["main"]
    in_maps = host_inputs(**inputs)
    res = run_bass_kernel_spmd(nc, in_maps, core_ids=list(range(NCORES)))
    out = np.empty((B, L, V), np.float32)
    for core in range(NCORES):
        b, half = core // 2, core % 2
        out[b, half * TOK:(half + 1) * TOK, :] = res.results[core]["out"]
    return out


# version 14
# speedup vs baseline: 1.0187x; 1.0144x over previous
import numpy as np
import ml_dtypes
from contextlib import ExitStack

import concourse.bass as bass
import concourse.mybir as mybir
from concourse.bass_utils import run_bass_kernel_spmd

F32 = mybir.dt.float32
BF16 = mybir.dt.bfloat16
I32 = mybir.dt.int32
AF = mybir.ActivationFunctionType
ALU = mybir.AluOpType

B, L, V = 4, 4096, 4096
TOK = 2048
DIN, DFF, ITR = 512, 2048, 11008
NLAY = 4
Q, NB = 128, 16
C4 = DIN // 128
FC16 = DFF // 128
VC32 = V // 128
IC = ITR // 128
HG = 4
EPS = 1e-5
NCORES = 8


class Sem:
    def __init__(self, nc, ctx, name):
        self.h = ctx.enter_context(nc.semaphore(name))
        self.n = 0

    def inc(self, instr, k=1):
        instr.then_inc(self.h, k)
        self.n += k
        return self.n


def _w(eng, sem, val):
    if val > 0:
        eng.wait_ge(sem.h, val)


def build(dbg=()):
    nc = bass.Bass("TRN2", target_bir_lowering=False, debug=False)
    ap = lambda t: t.ap()

    x_in = ap(nc.dram_tensor("x", [TOK, V], F32, kind="ExternalInput"))
    winT = ap(nc.dram_tensor("winT", [128, VC32 * DIN], F32, kind="ExternalInput"))
    w1T = ap(nc.dram_tensor("w1T", [NLAY, 128, C4 * DFF], F32, kind="ExternalInput"))
    w2T = ap(nc.dram_tensor("w2T", [NLAY, 128, FC16 * DIN], F32, kind="ExternalInput"))
    woT = ap(nc.dram_tensor("woT", [HG, C4, 128, 1024], F32, kind="ExternalInput"))
    wg_i = ap(nc.dram_tensor("wg_i", [IC, 128, 4096], I32, kind="ExternalInput"))
    wu_i = ap(nc.dram_tensor("wu_i", [IC, 128, 4096], I32, kind="ExternalInput"))
    wd_i = ap(nc.dram_tensor("wd_i", [HG, IC, 128, 1024], I32, kind="ExternalInput"))
    pv = {}
    for name, k in [("bin", C4), ("l1w", 16), ("l1b", 16), ("l2w", 16), ("l2b", 16),
                    ("scl", 16), ("phr", 16), ("phi", 16), ("lcr", 16), ("lci", 16),
                    ("b2", 16), ("lnlw", C4), ("lnlb", C4), ("b1", NLAY * FC16),
                    ("sg", IC), ("su", IC), ("sd", 32), ("mv", 1), ("mb", 1), ("eps", 1)]:
        pv[name] = ap(nc.dram_tensor("p_" + name, [128, k], F32, kind="ExternalInput"))
    ident_b = ap(nc.dram_tensor("ident_b", [128, 128], BF16, kind="ExternalInput"))
    ident_f = ap(nc.dram_tensor("ident_f", [128, 128], F32, kind="ExternalInput"))
    om_in = ap(nc.dram_tensor("om", [128, 128], F32, kind="ExternalInput"))
    sel_in = ap(nc.dram_tensor("sel", [8, 1], F32, kind="ExternalInput"))
    bout_in = ap(nc.dram_tensor("bout", [1, V], F32, kind="ExternalInput"))
    ones_in = ap(nc.dram_tensor("onesr", [1, 512], BF16, kind="ExternalInput"))
    out_t = ap(nc.dram_tensor("out", [TOK, V], F32, kind="ExternalOutput"))

    wg_bf = ap(nc.dram_tensor("wg_bf", [IC, 128, 4096], BF16))
    wu_bf = ap(nc.dram_tensor("wu_bf", [IC, 128, 4096], BF16))
    wd_bf = ap(nc.dram_tensor("wd_bf", [HG, IC, 128, 1024], BF16))
    xnt = nc.dram_tensor("xnt", [VC32, 128, TOK], BF16).ap().rearrange("v p t -> p v t")
    cc_in = ap(nc.dram_tensor("cc_in", [1, 1024], F32))
    cc_out = ap(nc.dram_tensor("cc_out", [8, 1024], F32))

    dbg_aps = {}
    for name in dbg:
        shp = {"h_a": [128, C4 * TOK], "h_f": [128, C4 * TOK], "xn0": [128, V],
               "y0": [128, TOK], "cv0": [128, TOK], "pc": [128, 32],
               "st0": [128, 16], "hp3": [128, C4 * TOK],
               "h_0": [128, C4 * TOK]}[name]
        dt = {"xn0": BF16}.get(name, F32)
        dbg_aps[name] = ap(nc.dram_tensor("dbg_" + name, shp, dt, kind="ExternalOutput"))

    with ExitStack() as ctx:
        sCC = Sem(nc, ctx, "sCC")
        sMM = Sem(nc, ctx, "sMM")
        sDV = Sem(nc, ctx, "sDV")
        sSC = Sem(nc, ctx, "sSC")
        sPRM = Sem(nc, ctx, "sPRM")
        sCVg = Sem(nc, ctx, "sCVg")
        sCVu = Sem(nc, ctx, "sCVu")
        sCVd = Sem(nc, ctx, "sCVd")
        sONE = Sem(nc, ctx, "sONE")
        sW1 = Sem(nc, ctx, "sW1")
        sW2 = Sem(nc, ctx, "sW2")
        sCCD = Sem(nc, ctx, "sCCD")
        sX = [Sem(nc, ctx, "sX0"), Sem(nc, ctx, "sX1")]
        sSL = [Sem(nc, ctx, "sSL0"), Sem(nc, ctx, "sSL1")]
        sXNT = Sem(nc, ctx, "sXNT")
        sWG = [Sem(nc, ctx, "sWG0"), Sem(nc, ctx, "sWG1")]
        sWU = [Sem(nc, ctx, "sWU0"), Sem(nc, ctx, "sWU1")]
        sWD = [Sem(nc, ctx, "sWDa"), Sem(nc, ctx, "sWDb")]
        sWO = [Sem(nc, ctx, "sWO0"), Sem(nc, ctx, "sWO1")]
        sXS = Sem(nc, ctx, "sXS")
        sOS = [Sem(nc, ctx, "sOS0"), Sem(nc, ctx, "sOS1")]
        sDBG = Sem(nc, ctx, "sDBG")
        sGP = Sem(nc, ctx, "sGP")
        sy, sc, ve, te, gp = nc.sync, nc.scalar, nc.vector, nc.tensor, nc.gpsimd

        sb = lambda n, shp, dt=F32: ctx.enter_context(nc.sbuf_tensor(n, shp, dt))
        P = {k: sb("sp_" + k, [128, v.shape[-1]]) for k, v in pv.items()}
        identb = sb("identb", [128, 128], BF16)
        identf = sb("identf", [128, 128], F32)
        omt = sb("omt", [128, 128], F32)
        selsb = sb("selsb", [8, 1], F32)
        onesb = sb("onesb", [1, 512], BF16)
        hf = sb("hf", [128, C4 * TOK], BF16)
        lcsr = sb("lcsr", [128, 16], F32)
        lcsi = sb("lcsi", [128, 16], F32)
        pcr = sb("pcr", [128, 16], F32)
        pci = sb("pci", [128, 16], F32)
        tmp16a = sb("tmp16a", [128, 16], F32)
        tmp16b = sb("tmp16b", [128, 16], F32)
        tmp16c = sb("tmp16c", [128, 16], F32)
        tmp16d = sb("tmp16d", [128, 16], F32)
        tmp16e = sb("tmp16e", [128, 16], F32)

        ps = ctx.enter_context(nc.psum_tensor("ps", [128, 8 * 512], F32))
        pbank = lambda b: ps[:, b * 512:(b + 1) * 512]
        pbank_b16 = lambda b: ps[:, :].bitcast(BF16)[:, b * 1024:(b + 1) * 1024]
        bank_free = [(sDV, 0)] * 8

        def bank_wait(b):
            s_, v_ = bank_free[b]
            _w(te, s_, v_)

        for k in P:
            sPRM.inc(sy.dma_start(out=P[k][:, :], in_=pv[k]), 16)
        sPRM.inc(sy.dma_start(out=identb[:, :], in_=ident_b), 16)
        sPRM.inc(sy.dma_start(out=identf[:, :], in_=ident_f), 16)
        sPRM.inc(sy.dma_start(out=omt[:, :], in_=om_in), 16)
        sPRM.inc(sy.dma_start(out=selsb[:, :], in_=sel_in), 16)
        params_ready = sPRM.inc(sy.dma_start(out=onesb[:, :], in_=ones_in), 16)

        CHUNK = 4 * 1024 * 1024

        def conv_chunks(sem, dst, src, n_elem, c0=0, c1=10**9):
            dims = len(src.shape)
            names = " ".join(f"a{i}" for i in range(dims))
            sflat = src.rearrange(f"{names} -> ({names})")
            dflat = dst.rearrange(f"{names} -> ({names})")
            v = 0
            ncall = (n_elem + CHUNK - 1) // CHUNK
            for ci in range(max(0, c0), min(ncall, c1)):
                o = ci * CHUNK
                e = min(n_elem, o + CHUNK)
                v = sem.inc(gp.dma_start(out=dflat[o:e], in_=sflat[o:e],
                                         max_dma_last_dim=8192), 16)
            return v

        NGE = IC * 128 * 4096
        wg_ready = wu_ready = wd_ready = 0


        def vfence(instr):
            v = sDV.inc(instr)
            ve.wait_ge(sDV.h, v)
            return v

        def sfence(instr):
            v = sSC.inc(instr)
            sc.wait_ge(sSC.h, v)
            return v

        _w(ve, sPRM, params_ready)
        ve.tensor_tensor(out=tmp16a[:, :], in0=P["phr"][:, :], in1=P["phr"][:, :], op=ALU.mult)
        vfence(ve.tensor_tensor(out=tmp16b[:, :], in0=P["phi"][:, :], in1=P["phi"][:, :], op=ALU.mult))
        r2_done = sDV.inc(ve.tensor_tensor(out=tmp16c[:, :], in0=tmp16a[:, :],
                                           in1=tmp16b[:, :], op=ALU.add))
        _w(sc, sDV, r2_done)
        sfence(sc.activation(out=tmp16a[:, :], in_=tmp16c[:, :], func=AF.Ln))
        sfence(sc.activation(out=tmp16b[:, :], in_=tmp16a[:, :], func=AF.Exp, scale=0.5))
        sc.activation(out=tmp16d[:, :], in_=tmp16a[:, :], func=AF.Exp, scale=-0.5)
        mag_done = sSC.inc(sc.activation(out=tmp16e[:, :], in_=tmp16b[:, :], func=AF.Tanh))
        _w(ve, sSC, mag_done)
        vfence(ve.tensor_tensor(out=tmp16c[:, :], in0=tmp16e[:, :], in1=tmp16d[:, :], op=ALU.mult))
        ve.tensor_tensor(out=pcr[:, :], in0=P["phr"][:, :], in1=tmp16c[:, :], op=ALU.mult)
        ve.tensor_tensor(out=pci[:, :], in0=P["phi"][:, :], in1=tmp16c[:, :], op=ALU.mult)
        ve.tensor_tensor(out=lcsr[:, :], in0=P["lcr"][:, :], in1=P["scl"][:, :], op=ALU.mult)
        pcv = sDV.inc(ve.tensor_tensor(out=lcsi[:, :], in0=P["lci"][:, :], in1=P["scl"][:, :], op=ALU.mult))
        if "pc" in dbg_aps:
            _w(sy, sDV, pcv)
            sDBG.inc(sy.dma_start(out=dbg_aps["pc"][:, 0:16], in_=pcr[:, :]), 16)
            sDBG.inc(sy.dma_start(out=dbg_aps["pc"][:, 16:32], in_=pci[:, :]), 16)

        with ExitStack() as abctx:
            absb = lambda n, shp, dt=F32: abctx.enter_context(nc.sbuf_tensor(n, shp, dt))
            h = absb("h", [128, C4 * TOK], F32)
            hsl = lambda c, a, b_: h[:, c * TOK + a:c * TOK + b_]

            with ExitStack() as actx:
                actx.enter_context(nc.named_scope("stageA"))
                asb = lambda n, shp, dt=F32: actx.enter_context(nc.sbuf_tensor(n, shp, dt))
                xbuf = [asb("xa", [128, V]), asb("xb", [128, V])]
                xnb = [asb("xna", [128, V], BF16), asb("xnb2", [128, V], BF16)]
                stg = [asb("stga", [128, VC32 * 512], BF16),
                       asb("stgb", [128, VC32 * 512], BF16)]
                winTb = asb("winTb", [128, VC32 * DIN], BF16)
                stat = asb("stat", [128, 16], F32)
                junkf = asb("junkf", [128, V // 2], F32)
                winT_ready = sONE.inc(gp.dma_start(out=winTb[:, :], in_=winT,
                                                   max_dma_last_dim=8192), 16)
                wg_ready = conv_chunks(sCVg, wg_bf, wg_i, NGE)
                _w(te, sPRM, params_ready)

                xbuf_free = [(sDV, 0), (sDV, 0)]
                xnb_free = [(sMM, 0), (sMM, 0)]
                junk_free = (sDV, 0)
                stg_free = [(sMM, 0), (sMM, 0)]
                stg_wr = [0, 0]
                tp_bank_free = [(sDV, 0), (sDV, 0)]
                stg_evac = [0, 0]
                h_a_done = 0

                for tc in range(16):
                    bi = tc % 2
                    sl = tc // 4
                    sbl = sl % 2
                    st_off = (tc % 4) * 128
                    s_, v_ = xbuf_free[bi]
                    _w(sy, s_, v_)
                    x_ld = sX[bi].inc(sy.dma_start(out=xbuf[bi][:, :],
                                                   in_=x_in[tc * 128:(tc + 1) * 128, :]), 16)
                    _w(ve, sX[bi], x_ld)
                    _w(ve, sPRM, params_ready)
                    ve.tensor_reduce(out=stat[:, 8 * bi + 0:8 * bi + 1],
                                     in_=xbuf[bi][:, :], axis=mybir.AxisListType.X,
                                     op=ALU.add)
                    _w(sc, sX[bi], x_ld)
                    _w(sc, sPRM, params_ready)
                    s_, v_ = junk_free
                    _w(sc, s_, v_)
                    sqA = sSC.inc(sc.activation(out=junkf[:, :],
                                                in_=xbuf[bi][:, 0:V // 2],
                                                func=AF.Square))
                    _w(ve, sSC, sqA)
                    rA = sDV.inc(ve.tensor_reduce(
                        out=stat[:, 8 * bi + 1:8 * bi + 2], in_=junkf[:, :],
                        axis=mybir.AxisListType.X, op=ALU.add))
                    _w(sc, sDV, rA)
                    sqB = sSC.inc(sc.activation(out=junkf[:, :],
                                                in_=xbuf[bi][:, V // 2:V],
                                                func=AF.Square))
                    _w(ve, sSC, sqB)
                    red2 = sDV.inc(ve.tensor_reduce(
                        out=stat[:, 8 * bi + 3:8 * bi + 4], in_=junkf[:, :],
                        axis=mybir.AxisListType.X, op=ALU.add))
                    junk_free = (sDV, red2)
                    ve.wait_ge(sDV.h, red2)
                    ve.tensor_tensor(out=stat[:, 8 * bi + 6:8 * bi + 7],
                                     in0=stat[:, 8 * bi + 1:8 * bi + 2],
                                     in1=stat[:, 8 * bi + 3:8 * bi + 4], op=ALU.add)
                    w2_ = sDV.inc(ve.tensor_scalar(out=stat[:, 8 * bi + 2:8 * bi + 3],
                                                   in0=stat[:, 8 * bi + 0:8 * bi + 1],
                                                   scalar1=1.0 / V, scalar2=None,
                                                   op0=ALU.mult))
                    ve.wait_ge(sDV.h, w2_)
                    ve.tensor_scalar(out=stat[:, 8 * bi + 3:8 * bi + 4],
                                     in0=stat[:, 8 * bi + 6:8 * bi + 7],
                                     scalar1=1.0 / V, scalar2=None, op0=ALU.mult)
                    w4_ = sDV.inc(ve.tensor_tensor(out=stat[:, 8 * bi + 4:8 * bi + 5],
                                                   in0=stat[:, 8 * bi + 2:8 * bi + 3],
                                                   in1=stat[:, 8 * bi + 2:8 * bi + 3],
                                                   op=ALU.mult))
                    ve.wait_ge(sDV.h, w4_)
                    var_done = sDV.inc(ve.tensor_tensor(
                        out=stat[:, 8 * bi + 5:8 * bi + 6],
                        in0=stat[:, 8 * bi + 3:8 * bi + 4],
                        in1=stat[:, 8 * bi + 4:8 * bi + 5], op=ALU.subtract))
                    _w(sc, sDV, var_done)
                    lnv = sSC.inc(sc.activation(out=stat[:, 8 * bi + 6:8 * bi + 7],
                                                in_=stat[:, 8 * bi + 5:8 * bi + 6],
                                                func=AF.Ln, bias=P["eps"][:, 0:1]))
                    sc.wait_ge(sSC.h, lnv)
                    rstd_done = sSC.inc(sc.activation(
                        out=stat[:, 8 * bi + 7:8 * bi + 8],
                        in_=stat[:, 8 * bi + 6:8 * bi + 7], func=AF.Exp, scale=-0.5))
                    s_, v_ = xnb_free[bi]
                    _w(ve, s_, v_)
                    _w(ve, sSC, rstd_done)
                    xn_done = sDV.inc(ve.tensor_scalar(
                        out=xnb[bi][:, :], in0=xbuf[bi][:, :],
                        scalar1=stat[:, 8 * bi + 2:8 * bi + 3],
                        scalar2=stat[:, 8 * bi + 7:8 * bi + 8],
                        op0=ALU.subtract, op1=ALU.mult))
                    xbuf_free[bi] = (sDV, xn_done)
                    if "xn0" in dbg_aps and tc == 0:
                        _w(sy, sDV, xn_done)
                        tv = sDBG.inc(sy.dma_start(out=dbg_aps["xn0"], in_=xnb[bi][:, :]), 16)
                        _w(ve, sDBG, tv)
                        _w(sc, sDBG, tv)
                    if "st0" in dbg_aps and tc == 0:
                        _w(sy, sDV, xn_done)
                        tv = sDBG.inc(sy.dma_start(out=dbg_aps["st0"], in_=stat[:, :]), 16)
                        _w(ve, sDBG, tv)
                        _w(sc, sDBG, tv)
                    last_tp = 0
                    for g in range(4):
                        tb = g % 2
                        s_, v_ = tp_bank_free[tb]
                        _w(te, s_, v_)
                        _w(te, sDV, xn_done)
                        for k in range(8):
                            vc = g * 8 + k
                            instr = te.transpose(pbank_b16(4 + tb)[:, k * 128:(k + 1) * 128],
                                                 xnb[bi][:, vc * 128:(vc + 1) * 128],
                                                 identb[:, :])
                        tp_done = sMM.inc(instr)
                        last_tp = tp_done
                        if g == 0:
                            s_, v_ = stg_free[sbl]
                            _w(ve, s_, v_)
                            _w(ve, sSL[sbl], stg_wr[sbl])
                        _w(ve, sMM, tp_done)
                        dst = stg[sbl][:, :].rearrange("p (v t) -> p v t", t=512)
                        src = pbank_b16(4 + tb)[:, :].rearrange("p (k t) -> p k t", t=128)
                        ev = sDV.inc(ve.tensor_copy(
                            out=dst[:, g * 8:(g + 1) * 8, st_off:st_off + 128], in_=src))
                        tp_bank_free[tb] = (sDV, ev)
                        stg_evac[sbl] = ev
                    xnb_free[bi] = (sMM, last_tp)

                    if tc % 4 == 3:
                        _w(sy, sDV, stg_evac[sbl])
                        stg_wr[sbl] = sSL[sbl].inc(sy.dma_start(
                            out=xnt[:, :, sl * 512:(sl + 1) * 512],
                            in_=stg[sbl][:, :].rearrange("p (v t) -> p v t", t=512)), 16)
                        _w(te, sONE, winT_ready)
                        _w(te, sDV, stg_evac[sbl])
                        win_done = [0] * C4
                        for mc in range(C4):
                            bank_wait(mc)
                            for vc in range(VC32):
                                instr = te.matmul(
                                    pbank(mc),
                                    winTb[:, vc * DIN + mc * 128:vc * DIN + mc * 128 + 128],
                                    stg[sbl][:, vc * 512:(vc + 1) * 512],
                                    start=(vc == 0), stop=(vc == VC32 - 1))
                            win_done[mc] = sMM.inc(instr)
                        for mc in range(C4):
                            _w(ve, sMM, win_done[mc])
                            h_a_done = sDV.inc(ve.tensor_scalar(
                                out=h[:, mc * TOK + sl * 512:mc * TOK + (sl + 1) * 512],
                                in0=pbank(mc), scalar1=P["bin"][:, mc:mc + 1],
                                scalar2=None, op0=ALU.add))
                            bank_free[mc] = (sDV, h_a_done)
                        stg_free[sbl] = (sMM, win_done[C4 - 1])

                if "h_a" in dbg_aps:
                    _w(sy, sDV, h_a_done)
                    tv = sDBG.inc(sy.dma_start(out=dbg_aps["h_a"], in_=h[:, :]), 16)
                    _w(ve, sDBG, tv)
                for eng in (ve, sc, te, gp, sy):
                    _w(eng, sDV, sDV.n)
                    _w(eng, sSC, sSC.n)
                    _w(eng, sMM, sMM.n)
                    for fam in (sX[0], sX[1], sSL[0], sSL[1]):
                        _w(eng, fam, fam.n)
                    _w(eng, sPRM, sPRM.n)
                    _w(eng, sONE, winT_ready)

            with ExitStack() as bctx:
                bctx.enter_context(nc.named_scope("stageB"))
                bsb = lambda n, shp, dt=F32: bctx.enter_context(nc.sbuf_tensor(n, shp, dt))
                zA = bsb("zA", [128, 2 * TOK])
                zB = bsb("zB", [128, 2 * TOK])
                zAr, zAi = zA[:, 0:TOK], zA[:, TOK:2 * TOK]
                zBr, zBi = zB[:, 0:TOK], zB[:, TOK:2 * TOK]
                y2buf = zA[:, :].bitcast(BF16)
                convr = bsb("convr", [128, C4 * TOK])
                y2in = convr[:, :].bitcast(BF16)[:, 0:C4 * TOK]
                y_c = bsb("y_c", [128, TOK])
                mu_all = bsb("mu_all", [128, TOK])
                rstd_all = bsb("rstd_all", [128, TOK])
                sq4 = bsb("sq4", [128, 4 * 512])
                var4 = sq4
                m2b4 = bsb("m2b4", [128, 4 * 512])
                w1t = bsb("w1t", [128, C4 * DFF], BF16)
                w2t = bsb("w2t", [128, FC16 * DIN], BF16)
                P1r = bsb("P1r", [128, C4 * Q])
                P1i = bsb("P1i", [128, C4 * Q])
                nP1i = bsb("nP1i", [128, C4 * Q])
                PBr = bsb("PBr", [128, C4 * NB])
                PBi = bsb("PBi", [128, C4 * NB])
                S0r = bsb("S0r", [128, C4 * NB])
                S0i = bsb("S0i", [128, C4 * NB])
                E2r = bsb("E2r", [128, NB]); E2i = bsb("E2i", [128, NB])
                T2r = bsb("T2r", [128, NB]); T2i = bsb("T2i", [128, NB])
                tA = bsb("tA", [128, 512]); tB = bsb("tB", [128, 512])
                Sinr4 = bsb("Sinr4", [128, C4 * NB])
                Sini4 = bsb("Sini4", [128, C4 * NB])
                ccrow = m2b4[0:1, 0:1024]
                Gsb = m2b4[0:8, 1024:2048]
                Ssel8 = bsb("Ssel8", [128, 8])
                Ic2 = bsb("Ic2", [128, 32])

                w1_ready = [0] * NLAY
                w2_ready = [0] * NLAY
                w1_lastmm = [0] * NLAY
                w2_lastmm = [0] * NLAY
                tI1 = w1t[:, :].bitcast(F32)[:, 0:TOK]
                tI2 = w1t[:, :].bitcast(F32)[:, TOK:2 * TOK]
                wu_ready = conv_chunks(sCVu, wu_bf, wu_i, NGE, 0, 5)

                def ln_stats():
                    _w(te, sDV, sDV.n)
                    _w(sc, sDV, sDV.n)
                    sq_done = [0] * C4
                    prev_grp = 0
                    for c in range(C4):
                        _w(sc, sMM, prev_grp)
                        for t4 in range(4):
                            sq_done[c] = sSC.inc(sc.activation(
                                out=sq4[:, t4 * 512:(t4 + 1) * 512],
                                in_=hsl(c, t4 * 512, (t4 + 1) * 512), func=AF.Square))
                        for t4 in range(4):
                            if c == 0:
                                bank_wait(t4)
                            te.matmul(pbank(t4), omt[:, :],
                                      hsl(c, t4 * 512, (t4 + 1) * 512),
                                      start=(c == 0), stop=(c == C4 - 1))
                        _w(te, sSC, sq_done[c])
                        for t4 in range(4):
                            if c == 0:
                                bank_wait(4 + t4)
                            instr = te.matmul(pbank(4 + t4), omt[:, :],
                                              sq4[:, t4 * 512:(t4 + 1) * 512],
                                              start=(c == 0), stop=(c == C4 - 1))
                        prev_grp = sMM.inc(instr)
                    mm_done = prev_grp
                    _w(sc, sMM, mm_done)
                    muc = 0
                    for t4 in range(4):
                        muc = sSC.inc(sc.copy(out=mu_all[:, t4 * 512:(t4 + 1) * 512],
                                              in_=pbank(t4)))
                    _w(ve, sSC, muc)
                    _w(ve, sMM, mm_done)
                    var_v = 0
                    for t4 in range(4):
                        ve.tensor_tensor(out=m2b4[:, t4 * 512:(t4 + 1) * 512],
                                         in0=mu_all[:, t4 * 512:(t4 + 1) * 512],
                                         in1=mu_all[:, t4 * 512:(t4 + 1) * 512],
                                         op=ALU.mult)
                        var_v = sDV.inc(ve.tensor_tensor(
                            out=var4[:, t4 * 512:(t4 + 1) * 512], in0=pbank(4 + t4),
                            in1=m2b4[:, t4 * 512:(t4 + 1) * 512], op=ALU.subtract))
                        bank_free[t4] = (sSC, muc)
                        bank_free[4 + t4] = (sDV, var_v)
                    _w(sc, sDV, var_v)
                    rstd_v = 0
                    for t4 in range(4):
                        sc.activation(out=m2b4[:, t4 * 512:(t4 + 1) * 512],
                                      in_=var4[:, t4 * 512:(t4 + 1) * 512], func=AF.Ln,
                                      bias=P["eps"][:, 0:1])
                    for t4 in range(4):
                        rstd_v = sSC.inc(sc.activation(
                            out=rstd_all[:, t4 * 512:(t4 + 1) * 512],
                            in_=m2b4[:, t4 * 512:(t4 + 1) * 512],
                            func=AF.Exp, scale=-0.5))
                    return rstd_v

                def vsync(instr):
                    v = sDV.inc(instr)
                    ve.wait_ge(sDV.h, v)
                    return v

                def ssync(instr):
                    v = sSC.inc(instr)
                    sc.wait_ge(sSC.h, v)
                    return v

                def ctab_all(Tr, Ti, width):
                    k = 1
                    while k < width:
                        n = min(k, width - k)
                        for c in range(C4):
                            o = c * width
                            cr, ci = Tr[:, o + k - 1:o + k], Ti[:, o + k - 1:o + k]
                            ve.tensor_scalar(out=tA[:, c * 64:c * 64 + n],
                                             in0=Tr[:, o:o + n], scalar1=cr,
                                             scalar2=None, op0=ALU.mult)
                            ve.tensor_scalar(out=tB[:, c * 64:c * 64 + n],
                                             in0=Ti[:, o:o + n], scalar1=ci,
                                             scalar2=None, op0=ALU.mult)
                            ve.tensor_scalar(out=tA[:, 256 + c * 64:256 + c * 64 + n],
                                             in0=Tr[:, o:o + n], scalar1=ci,
                                             scalar2=None, op0=ALU.mult)
                            last = ve.tensor_scalar(out=tB[:, 256 + c * 64:256 + c * 64 + n],
                                                    in0=Ti[:, o:o + n], scalar1=cr,
                                                    scalar2=None, op0=ALU.mult)
                        vsync(last)
                        for c in range(C4):
                            o = c * width
                            ve.tensor_tensor(out=Tr[:, o + k:o + k + n],
                                             in0=tA[:, c * 64:c * 64 + n],
                                             in1=tB[:, c * 64:c * 64 + n], op=ALU.subtract)
                            last = ve.tensor_tensor(out=Ti[:, o + k:o + k + n],
                                                    in0=tA[:, 256 + c * 64:256 + c * 64 + n],
                                                    in1=tB[:, 256 + c * 64:256 + c * 64 + n],
                                                    op=ALU.add)
                        vsync(last)
                        k *= 2

                for l in range(NLAY):
                    lc = lambda t, c: P[t][:, l * 4 + c:l * 4 + c + 1]
                    _w(ve, sMM, sMM.n)
                    _w(sc, sMM, sMM.n)

                    rstd1 = ln_stats()
                    last = None
                    for c in range(C4):
                        ve.tensor_copy(out=P1r[:, c * Q:c * Q + 1],
                                       in_=pcr[:, l * 4 + c:l * 4 + c + 1])
                        last = ve.tensor_copy(out=P1i[:, c * Q:c * Q + 1],
                                              in_=pci[:, l * 4 + c:l * 4 + c + 1])
                    vsync(last)
                    ctab_all(P1r, P1i, Q)
                    ve.tensor_scalar(out=nP1i[:, :], in0=P1i[:, :], scalar1=-1.0,
                                     scalar2=None, op0=ALU.mult)
                    last = None
                    for c in range(C4):
                        ve.tensor_copy(out=PBr[:, c * NB:c * NB + 1],
                                       in_=P1r[:, c * Q + Q - 1:c * Q + Q])
                        last = ve.tensor_copy(out=PBi[:, c * NB:c * NB + 1],
                                              in_=P1i[:, c * Q + Q - 1:c * Q + Q])
                    vsync(last)
                    ctab_all(PBr, PBi, NB)
                    for c in range(C4):
                        P1r_c = P1r[:, c * Q:(c + 1) * Q]
                        P1i_c = P1i[:, c * Q:(c + 1) * Q]
                        nP1i_c = nP1i[:, c * Q:(c + 1) * Q]
                        PBr_c = PBr[:, c * NB:(c + 1) * NB]
                        PBi_c = PBi[:, c * NB:(c + 1) * NB]
                        S0r_c = S0r[:, c * NB:(c + 1) * NB]
                        S0i_c = S0i[:, c * NB:(c + 1) * NB]
                        convr_c = convr[:, c * TOK:(c + 1) * TOK]
                        _w(ve, sSC, rstd1)
                        ve.tensor_tensor(out=y_c[:, :], in0=hsl(c, 0, TOK),
                                         in1=mu_all[:, :], op=ALU.subtract)
                        ve.tensor_tensor(out=y_c[:, :], in0=y_c[:, :],
                                         in1=rstd_all[:, :], op=ALU.mult)
                        yv = sDV.inc(ve.tensor_scalar(out=y_c[:, :], in0=y_c[:, :],
                                                      scalar1=lc("l1w", c),
                                                      scalar2=lc("l1b", c),
                                                      op0=ALU.mult, op1=ALU.add))
                        _w(sc, sDV, yv)
                        sil_v = sSC.inc(sc.activation(out=y_c[:, :], in_=y_c[:, :],
                                                      func=AF.Silu))
                        _w(ve, sSC, sil_v)
                        yscl = sDV.inc(ve.tensor_scalar(out=y_c[:, :], in0=y_c[:, :],
                                                        scalar1=lc("scl", c), scalar2=None,
                                                        op0=ALU.mult))
                        if "y0" in dbg_aps and l == 0 and c == 0:
                            ydbg = sDV.inc(ve.nop())
                            _w(sy, sDV, ydbg)
                            tv = sDBG.inc(sy.dma_start(out=dbg_aps["y0"], in_=y_c[:, :]), 16)
                            _w(ve, sDBG, tv)
                        y3 = y_c[:, :].rearrange("p (b q) -> p b q", q=Q)
                        c3r = zBr.rearrange("p (b q) -> p b q", q=Q)
                        c3i = zBi.rearrange("p (b q) -> p b q", q=Q)
                        i3a = tI1[:, :].rearrange("p (b q) -> p b q", q=Q)
                        ve.scalar_tensor_tensor(out=c3r[:, :, 1:Q], in0=y3[:, :, 0:Q - 1],
                                                scalar=P1r_c[:, 0:1], in1=y3[:, :, 1:Q],
                                                op0=ALU.mult, op1=ALU.add)
                        vstep = sDV.inc(ve.tensor_copy(out=c3r[:, :, 0:1], in_=y3[:, :, 0:1]))
                        ve.wait_ge(sDV.h, vstep)
                        _w(sc, sDV, yscl)
                        sstep = sSC.inc(sc.activation(out=c3i[:, :, 1:Q],
                                                      in_=y3[:, :, 0:Q - 1], func=AF.Copy,
                                                      scale=P1i_c[:, 0:1]))
                        _w(gp, sSC, sstep)
                        gstep = sGP.inc(gp.memset(c3i[:, :, 0:1], 0.0))
                        sstep_prev = sstep
                        pairs = [(zAr, zAi), (zBr, zBi)]
                        cur = 1
                        s = 2
                        while s < Q:
                            last = (s * 2 >= Q)
                            nxt = 1 - cur
                            cr_, ci_ = pairs[cur]
                            nr_ = convr_c if last else pairs[nxt][0]
                            ni_ = pairs[nxt][1]
                            a3r = cr_.rearrange("p (b q) -> p b q", q=Q)
                            a3i = ci_.rearrange("p (b q) -> p b q", q=Q)
                            b3r = nr_.rearrange("p (b q) -> p b q", q=Q)
                            b3i = ni_.rearrange("p (b q) -> p b q", q=Q)
                            _w(sc, sDV, vstep)
                            _w(sc, sGP, gstep)
                            sstep = sSC.inc(sc.activation(out=i3a[:, :, s:Q],
                                                          in_=a3r[:, :, 0:Q - s],
                                                          func=AF.Copy,
                                                          scale=P1i_c[:, s - 1:s]))
                            _w(ve, sGP, gstep)
                            _w(ve, sSC, sstep_prev)
                            ve.scalar_tensor_tensor(out=b3r[:, :, s:Q],
                                                    in0=a3r[:, :, 0:Q - s],
                                                    scalar=P1r_c[:, s - 1:s],
                                                    in1=a3r[:, :, s:Q],
                                                    op0=ALU.mult, op1=ALU.add)
                            ve.scalar_tensor_tensor(out=b3r[:, :, s:Q],
                                                    in0=a3i[:, :, 0:Q - s],
                                                    scalar=nP1i_c[:, s - 1:s],
                                                    in1=b3r[:, :, s:Q],
                                                    op0=ALU.mult, op1=ALU.add)
                            ve.scalar_tensor_tensor(out=b3i[:, :, s:Q],
                                                    in0=a3i[:, :, 0:Q - s],
                                                    scalar=P1r_c[:, s - 1:s],
                                                    in1=a3i[:, :, s:Q],
                                                    op0=ALU.mult, op1=ALU.add)
                            vstep = sDV.inc(ve.tensor_copy(out=b3r[:, :, 0:s],
                                                           in_=a3r[:, :, 0:s]))
                            ve.wait_ge(sDV.h, vstep)
                            _w(gp, sSC, sstep)
                            _w(gp, sDV, vstep)
                            gp.tensor_copy(out=b3i[:, :, 0:s], in_=a3i[:, :, 0:s])
                            gstep = sGP.inc(gp.tensor_tensor(out=b3i[:, :, s:Q],
                                                             in0=i3a[:, :, s:Q],
                                                             in1=b3i[:, :, s:Q],
                                                             op=ALU.add))
                            sstep_prev = sstep
                            cur = nxt
                            s *= 2
                        zfi = pairs[cur][1]
                        cv3 = convr_c.rearrange("p (b q) -> p b q", q=Q)
                        zi3 = zfi.rearrange("p (b q) -> p b q", q=Q)
                        _w(ve, sGP, gstep)
                        ve.tensor_copy(out=E2r[:, :], in_=cv3[:, :, Q - 1:Q])
                        vsync(ve.tensor_copy(out=E2i[:, :], in_=zi3[:, :, Q - 1:Q]))
                        srcs = [(E2r, E2i), (T2r, T2i)]
                        cu = 0
                        s = 1
                        while s < NB:
                            last = (s * 2 >= NB)
                            sr, si_ = srcs[cu]
                            dr, di_ = (S0r_c, S0i_c) if last else srcs[1 - cu]
                            ve.scalar_tensor_tensor(out=dr[:, s:NB], in0=sr[:, 0:NB - s],
                                                    scalar=PBr_c[:, s - 1:s],
                                                    in1=sr[:, s:NB],
                                                    op0=ALU.mult, op1=ALU.add)
                            ve.scalar_tensor_tensor(out=di_[:, s:NB], in0=sr[:, 0:NB - s],
                                                    scalar=PBi_c[:, s - 1:s],
                                                    in1=si_[:, s:NB],
                                                    op0=ALU.mult, op1=ALU.add)
                            vsync(ve.tensor_scalar(out=tA[:, 0:NB - s],
                                                   in0=si_[:, 0:NB - s],
                                                   scalar1=PBi_c[:, s - 1:s], scalar2=None,
                                                   op0=ALU.mult))
                            ve.tensor_tensor(out=dr[:, s:NB], in0=dr[:, s:NB],
                                             in1=tA[:, 0:NB - s], op=ALU.subtract)
                            ve.scalar_tensor_tensor(out=di_[:, s:NB], in0=si_[:, 0:NB - s],
                                                    scalar=PBr_c[:, s - 1:s],
                                                    in1=di_[:, s:NB],
                                                    op0=ALU.mult, op1=ALU.add)
                            ve.tensor_copy(out=dr[:, 0:s], in_=sr[:, 0:s])
                            vsync(ve.tensor_copy(out=di_[:, 0:s], in_=si_[:, 0:s]))
                            cu = 1 - cu
                            s *= 2
                        s0c = sDV.n
                        _w(te, sDV, s0c)
                        bank_wait(6)
                        te.transpose(pbank(6)[0:1, 0:128], S0r_c[:, NB - 1:NB], identf[:, :])
                        tpv = sMM.inc(te.transpose(pbank(6)[0:1, 128:256],
                                                   S0i_c[:, NB - 1:NB], identf[:, :]))
                        _w(ve, sMM, tpv)
                        ve.tensor_copy(out=ccrow[0:1, c * 128:(c + 1) * 128],
                                       in_=pbank(6)[0:1, 0:128])
                        ccv = sDV.inc(ve.tensor_copy(
                            out=ccrow[0:1, 512 + c * 128:512 + (c + 1) * 128],
                            in_=pbank(6)[0:1, 128:256]))
                        bank_free[6] = (sDV, ccv)

                    _w(gp, sDV, sDV.n)
                    _w(gp, sSC, sSC.n)
                    cc_wr = sCCD.inc(gp.dma_start(out=cc_in, in_=ccrow[0:1, :]), 16)
                    _w(gp, sCCD, cc_wr)
                    cc_done = sCC.inc(gp.collective_compute(
                        "AllGather", ALU.bypass, replica_groups=[list(range(NCORES))],
                        ins=[cc_in.opt()], outs=[cc_out.opt()]))
                    _w(gp, sCC, cc_done)
                    g_ld = sCCD.inc(gp.dma_start(out=Gsb[:, :], in_=cc_out), 16)
                    if l > 0:
                        _w(gp, sMM, w1_lastmm[l - 1])
                        _w(gp, sMM, w2_lastmm[l - 1])
                    w1_ready[l] = sW1.inc(gp.dma_start(out=w1t[:, :], in_=w1T[l],
                                                       max_dma_last_dim=8192), 16)
                    w2_ready[l] = sW2.inc(gp.dma_start(out=w2t[:, :], in_=w2T[l],
                                                       max_dma_last_dim=8192), 16)
                    NWDE = HG * IC * 128 * 1024
                    if l == 0:
                        wu_ready = conv_chunks(sCVu, wu_bf, wu_i, NGE, 5, 11)
                    elif l == 1:
                        wd_ready = conv_chunks(sCVd, wd_bf, wd_i, NWDE, 0, 6)
                    elif l == 2:
                        wd_ready = conv_chunks(sCVd, wd_bf, wd_i, NWDE, 6, 11)

                    _w(te, sCCD, g_ld)
                    bank_wait(7)
                    for c in range(C4):
                        te.matmul(pbank(7)[:, c:c + 1], Gsb[0:8, c * 128:(c + 1) * 128],
                                  selsb[:, :], start=True, stop=True)
                        instr = te.matmul(pbank(7)[:, 4 + c:5 + c],
                                          Gsb[0:8, 512 + c * 128:512 + (c + 1) * 128],
                                          selsb[:, :], start=True, stop=True)
                    selv = sMM.inc(instr)
                    _w(ve, sMM, selv)
                    sv2 = vsync(ve.tensor_copy(out=Ssel8[:, :], in_=pbank(7)[:, 0:8]))
                    bank_free[7] = (sDV, sv2)
                    L4 = slice(l * 4, l * 4 + 4)
                    P2r4 = PBr[:, :].rearrange("p (c n) -> p c n", n=NB)[:, :, NB - 1:NB]
                    P2i4 = PBi[:, :].rearrange("p (c n) -> p c n", n=NB)[:, :, NB - 1:NB]
                    I_r = Ic2[:, 0:4]; I_i = Ic2[:, 4:8]; nI_i = Ic2[:, 8:12]
                    t1r = Ic2[:, 12:16]; t1i = Ic2[:, 16:20]
                    m1 = Ic2[:, 20:24]; m2 = Ic2[:, 24:28]; m3 = Ic2[:, 28:32]
                    p2r_f = tA[:, 0:4]; p2i_f = tA[:, 4:8]
                    ve.tensor_copy(out=p2r_f.rearrange("p (c n) -> p c n", n=1), in_=P2r4)
                    vsync(ve.tensor_copy(out=p2i_f.rearrange("p (c n) -> p c n", n=1), in_=P2i4))
                    ve.tensor_tensor(out=m1, in0=p2r_f, in1=lcsr[:, L4], op=ALU.mult)
                    ve.tensor_tensor(out=m2, in0=p2i_f, in1=lcsi[:, L4], op=ALU.mult)
                    ve.tensor_tensor(out=m3, in0=p2r_f, in1=lcsi[:, L4], op=ALU.mult)
                    vsync(ve.tensor_tensor(out=t1i, in0=p2i_f, in1=lcsr[:, L4], op=ALU.mult))
                    ve.tensor_tensor(out=t1r, in0=m1, in1=m2, op=ALU.subtract)
                    vsync(ve.tensor_tensor(out=m1, in0=m3, in1=t1i, op=ALU.add))
                    ve.tensor_tensor(out=m2, in0=t1r, in1=Ssel8[:, 0:4], op=ALU.add)
                    vsync(ve.tensor_tensor(out=m3, in0=m1, in1=Ssel8[:, 4:8], op=ALU.add))
                    ve.tensor_scalar(out=t1r, in0=m2, scalar1=P["mv"][:, 0:1],
                                     scalar2=None, op0=ALU.mult)
                    vsync(ve.tensor_scalar(out=t1i, in0=m3, scalar1=P["mv"][:, 0:1],
                                           scalar2=None, op0=ALU.mult))
                    ve.tensor_scalar(out=m1, in0=lcsr[:, L4], scalar1=P["mb"][:, 0:1],
                                     scalar2=None, op0=ALU.mult)
                    vsync(ve.tensor_scalar(out=m2, in0=lcsi[:, L4], scalar1=P["mb"][:, 0:1],
                                           scalar2=None, op0=ALU.mult))
                    ve.tensor_tensor(out=I_r, in0=m1, in1=t1r, op=ALU.add)
                    vsync(ve.tensor_tensor(out=I_i, in0=m2, in1=t1i, op=ALU.add))
                    vsync(ve.tensor_scalar(out=nI_i, in0=I_i, scalar1=-1.0, scalar2=None,
                                           op0=ALU.mult))
                    last = None
                    for c in range(C4):
                        ve.tensor_copy(out=Sinr4[:, c * NB:c * NB + 1], in_=I_r[:, c:c + 1])
                        last = ve.tensor_copy(out=Sini4[:, c * NB:c * NB + 1],
                                              in_=I_i[:, c:c + 1])
                    vsync(last)
                    for c in range(C4):
                        last = ve.scalar_tensor_tensor(
                            out=Sinr4[:, c * NB + 1:(c + 1) * NB],
                            in0=PBr[:, c * NB:(c + 1) * NB - 1], scalar=I_r[:, c:c + 1],
                            in1=S0r[:, c * NB:(c + 1) * NB - 1], op0=ALU.mult, op1=ALU.add)
                    vsync(last)
                    for c in range(C4):
                        last = ve.scalar_tensor_tensor(
                            out=Sinr4[:, c * NB + 1:(c + 1) * NB],
                            in0=PBi[:, c * NB:(c + 1) * NB - 1], scalar=nI_i[:, c:c + 1],
                            in1=Sinr4[:, c * NB + 1:(c + 1) * NB], op0=ALU.mult, op1=ALU.add)
                    vsync(last)
                    for c in range(C4):
                        last = ve.scalar_tensor_tensor(
                            out=Sini4[:, c * NB + 1:(c + 1) * NB],
                            in0=PBr[:, c * NB:(c + 1) * NB - 1], scalar=I_i[:, c:c + 1],
                            in1=S0i[:, c * NB:(c + 1) * NB - 1], op0=ALU.mult, op1=ALU.add)
                    vsync(last)
                    for c in range(C4):
                        last = ve.scalar_tensor_tensor(
                            out=Sini4[:, c * NB + 1:(c + 1) * NB],
                            in0=PBi[:, c * NB:(c + 1) * NB - 1], scalar=I_r[:, c:c + 1],
                            in1=Sini4[:, c * NB + 1:(c + 1) * NB], op0=ALU.mult, op1=ALU.add)
                    vsync(last)
                    if "cv0" in dbg_aps and l == 0:
                        cdbg = sDV.inc(ve.nop())
                        _w(sy, sDV, cdbg)
                        tv = sDBG.inc(sy.dma_start(out=dbg_aps["cv0"], in_=convr[:, 0:TOK]), 16)
                        _w(ve, sDBG, tv)
                    for c in range(C4):
                        ve.tensor_tensor(out=h[:, c * TOK:(c + 1) * TOK],
                                         in0=h[:, c * TOK:(c + 1) * TOK],
                                         in1=convr[:, c * TOK:(c + 1) * TOK],
                                         op=ALU.add)
                    for c in range(C4):
                        h3 = h[:, c * TOK:(c + 1) * TOK].rearrange("p (b q) -> p b q", q=Q)
                        for j in range(NB):
                            ve.scalar_tensor_tensor(
                                out=h3[:, j, :], in0=P1r[:, c * Q:(c + 1) * Q],
                                scalar=Sinr4[:, c * NB + j:c * NB + j + 1],
                                in1=h3[:, j, :], op0=ALU.mult, op1=ALU.add)
                    last = None
                    for c in range(C4):
                        h3 = h[:, c * TOK:(c + 1) * TOK].rearrange("p (b q) -> p b q", q=Q)
                        for j in range(NB):
                            last = ve.scalar_tensor_tensor(
                                out=h3[:, j, :], in0=nP1i[:, c * Q:(c + 1) * Q],
                                scalar=Sini4[:, c * NB + j:c * NB + j + 1],
                                in1=h3[:, j, :], op0=ALU.mult, op1=ALU.add)
                    vsync(last)
                    if "hp3" in dbg_aps and l == 0:
                        _w(sy, sDV, sDV.n)
                        tv = sDBG.inc(sy.dma_start(out=dbg_aps["hp3"], in_=h[:, :]), 16)
                        _w(ve, sDBG, tv)
                        _w(sc, sDBG, tv)
                        _w(te, sDBG, tv)

                    rstd2 = ln_stats()
                    y2_done = 0
                    for c in range(C4):
                        _w(ve, sSC, rstd2)
                        ve.tensor_tensor(out=y_c[:, :], in0=hsl(c, 0, TOK),
                                         in1=mu_all[:, :], op=ALU.subtract)
                        ve.tensor_tensor(out=y_c[:, :], in0=y_c[:, :],
                                         in1=rstd_all[:, :], op=ALU.mult)
                        y2_done = sDV.inc(ve.tensor_scalar(
                            out=y2in[:, c * TOK:(c + 1) * TOK], in0=y_c[:, :],
                            scalar1=lc("l2w", c), scalar2=lc("l2b", c),
                            op0=ALU.mult, op1=ALU.add))

                    _w(te, sW1, w1_ready[l])
                    _w(te, sW2, w2_ready[l])
                    _w(te, sDV, y2_done)
                    for t4 in range(4):
                        sil_last = 0
                        for fc in range(FC16):
                            bk = fc % 2
                            bank_wait(bk)
                            for kc in range(C4):
                                instr = te.matmul(
                                    pbank(bk),
                                    w1t[:, kc * DFF + fc * 128:kc * DFF + fc * 128 + 128],
                                    y2in[:, kc * TOK + t4 * 512:kc * TOK + (t4 + 1) * 512],
                                    start=(kc == 0), stop=(kc == C4 - 1))
                            w1v = sMM.inc(instr)
                            w1_lastmm[l] = w1v
                            _w(sc, sMM, w1v)
                            sil_last = sSC.inc(sc.activation(
                                out=y2buf[:, fc * 512:(fc + 1) * 512], in_=pbank(bk),
                                func=AF.Silu,
                                bias=P["b1"][:, l * FC16 + fc:l * FC16 + fc + 1]))
                            bank_free[bk] = (sSC, sil_last)
                        _w(te, sSC, sil_last)
                        for c in range(C4):
                            bk = 2 + c % 2
                            bank_wait(bk)
                            for fc in range(FC16):
                                instr = te.matmul(
                                    pbank(bk),
                                    w2t[:, fc * DIN + c * 128:fc * DIN + c * 128 + 128],
                                    y2buf[:, fc * 512:(fc + 1) * 512],
                                    start=(fc == 0), stop=(fc == FC16 - 1))
                            w2v = sMM.inc(instr)
                            w2_lastmm[l] = w2v
                            _w(ve, sMM, w2v)
                            hv = sDV.inc(ve.scalar_tensor_tensor(
                                out=hsl(c, t4 * 512, (t4 + 1) * 512), in0=pbank(bk),
                                scalar=lc("b2", c),
                                in1=hsl(c, t4 * 512, (t4 + 1) * 512),
                                op0=ALU.add, op1=ALU.add))
                            bank_free[bk] = (sDV, hv)
                    if "h_0" in dbg_aps and l == 0:
                        _w(sy, sDV, sDV.n)
                        tv = sDBG.inc(sy.dma_start(out=dbg_aps["h_0"], in_=h[:, :]), 16)
                        _w(ve, sDBG, tv)
                        _w(sc, sDBG, tv)
                        _w(te, sDBG, tv)

                if "h_f" in dbg_aps:
                    _w(sy, sDV, sDV.n)
                    tv = sDBG.inc(sy.dma_start(out=dbg_aps["h_f"], in_=h[:, :]), 16)
                    _w(ve, sDBG, tv)

                rstdf = ln_stats()
                hf_done = 0
                for c in range(C4):
                    _w(ve, sSC, rstdf)
                    ve.tensor_tensor(out=y_c[:, :], in0=hsl(c, 0, TOK), in1=mu_all[:, :],
                                     op=ALU.subtract)
                    ve.tensor_tensor(out=y_c[:, :], in0=y_c[:, :], in1=rstd_all[:, :],
                                     op=ALU.mult)
                    hf_done = sDV.inc(ve.tensor_scalar(
                        out=hf[:, c * TOK:(c + 1) * TOK], in0=y_c[:, :],
                        scalar1=P["lnlw"][:, c:c + 1], scalar2=P["lnlb"][:, c:c + 1],
                        op0=ALU.mult, op1=ALU.add))

                for eng in (ve, sc, te, gp, sy):
                    _w(eng, sDV, sDV.n)
                    _w(eng, sSC, sSC.n)
                    _w(eng, sMM, sMM.n)
                    _w(eng, sW1, sW1.n)
                    _w(eng, sW2, sW2.n)
                    _w(eng, sCCD, sCCD.n)

        with ExitStack() as cctx:
            cctx.enter_context(nc.named_scope("stageC"))
            csb = lambda n, shp, dt=F32: cctx.enter_context(nc.sbuf_tensor(n, shp, dt))
            hh = csb("hh", [128, IC * 512], BF16)
            boutb = csb("boutb", [1, V], BF16)
            bout_ready = sONE.inc(gp.dma_start(out=boutb[:, :], in_=bout_in,
                                               max_dma_last_dim=8192), 16)
            OV = csb("OV", [128, 18432], F32)
            OVb = OV[:, :].bitcast(BF16)
            xnTs = OVb[:, 0:16384]
            wgbuf = [OVb[:, 16384 + i * 4096:16384 + (i + 1) * 4096] for i in range(2)]
            wubuf = [OVb[:, 24576 + i * 4096:24576 + (i + 1) * 4096] for i in range(2)]
            gsil = [OV[:, 16384 + i * 512:16384 + (i + 1) * 512] for i in range(2)]
            wdbuf = [OVb[:, i * 4096:(i + 1) * 4096] for i in range(2)]
            wobuf = [OVb[:, 8192 + i * 4096:8192 + (i + 1) * 4096] for i in range(2)]
            xstg = OV[:, 8192:12288]
            ostg = OV[:, 12288:16384]
            dsb = [OV[:, 16384 + i * 512:16384 + (i + 1) * 512] for i in range(2)]
            sumb = [OV[:, 17408 + i * 512:17408 + (i + 1) * 512] for i in range(2)]

            x4 = x_in.rearrange("(s b p) (g c) -> p s g b c", p=128, b=4, c=1024)
            o4 = out_t.rearrange("(s b p) (g c) -> p s g b c", p=128, b=4, c=1024)

            gbank_free = [(sDV, 0)] * 2
            ubank_free = [(sDV, 0)] * 2
            gsil_free = [(sDV, 0)] * 2
            wg_cons = [(sMM, 0)] * 2
            wu_cons = [(sMM, 0)] * 2
            wo_cons = [(sMM, 0)] * 2
            ostg_free = (sOS[0], 0)
            xstg_free = (sDV, 0)
            prev_down_reads = 0
            prev_store = 0

            for s in range(4):
                _w(sy, sDV, prev_down_reads)
                _w(sy, sOS[0], sOS[0].n)
                _w(sy, sOS[1], sOS[1].n)
                _w(sy, sCVg, wg_ready)
                _w(sc, sOS[0], sOS[0].n)
                _w(sc, sOS[1], sOS[1].n)
                xnT_ld = sXNT.inc(sy.dma_start(
                    out=xnTs.rearrange("p (v t) -> p v t", t=512),
                    in_=xnt[:, :, s * 512:(s + 1) * 512]), 16)

                gate_end = 0
                for i in range(IC):
                    bi = i % 2
                    s_, v_ = wg_cons[bi]
                    _w(sy, s_, v_)
                    wg_ld = sWG[bi].inc(sy.dma_start(out=wgbuf[bi], in_=wg_bf[i]), 16)
                    s_, v_ = wu_cons[bi]
                    _w(sy, s_, v_)
                    if i < 2:
                        _w(sy, sCVu, wu_ready)
                    wu_ld = sWU[bi].inc(sy.dma_start(out=wubuf[bi], in_=wu_bf[i]), 16)
                    _w(te, sWG[bi], wg_ld)
                    if i < 2:
                        _w(te, sXNT, xnT_ld)
                    s_, v_ = gbank_free[bi]
                    _w(te, s_, v_)
                    for hc in range(VC32):
                        instr = te.matmul(pbank(bi), wgbuf[bi][:, hc * 128:(hc + 1) * 128],
                                          xnTs[:, hc * 512:(hc + 1) * 512],
                                          start=(hc == 0), stop=(hc == VC32 - 1))
                    g_done = sMM.inc(instr)
                    wg_cons[bi] = (sMM, g_done)
                    _w(te, sWU[bi], wu_ld)
                    s_, v_ = ubank_free[bi]
                    _w(te, s_, v_)
                    for hc in range(VC32):
                        instr = te.matmul(pbank(2 + bi),
                                          wubuf[bi][:, hc * 128:(hc + 1) * 128],
                                          xnTs[:, hc * 512:(hc + 1) * 512],
                                          start=(hc == 0), stop=(hc == VC32 - 1))
                    u_done = sMM.inc(instr)
                    wu_cons[bi] = (sMM, u_done)
                    gate_end = u_done
                    s_, v_ = gsil_free[bi]
                    _w(sc, s_, v_)
                    _w(sc, sMM, g_done)
                    sil_v = sSC.inc(sc.activation(out=gsil[bi], in_=pbank(bi),
                                                  func=AF.Silu,
                                                  scale=P["sg"][:, i:i + 1]))
                    gbank_free[bi] = (sSC, sil_v)
                    _w(ve, sMM, u_done)
                    _w(ve, sSC, sil_v)
                    hh_v = sDV.inc(ve.scalar_tensor_tensor(
                        out=hh[:, i * 512:(i + 1) * 512], in0=pbank(2 + bi),
                        scalar=P["su"][:, i:i + 1], in1=gsil[bi],
                        op0=ALU.mult, op1=ALU.mult))
                    ubank_free[bi] = (sDV, hh_v)
                    gsil_free[bi] = (sDV, hh_v)
                hh_done = sDV.n

                bank_free[0] = gbank_free[0]
                bank_free[1] = gbank_free[1]
                bank_free[2] = ubank_free[0]
                bank_free[3] = ubank_free[1]

                wd_cons = [(sMM, 0)] * 2
                per_hc_add = 0
                NWD = (IC + 3) // 4
                for hg in range(HG):
                    s_, v_ = xstg_free
                    _w(sc, s_, v_)
                    if hg == 0:
                        _w(sc, sMM, gate_end)
                    xs_ld = sXS.inc(sc.dma_start(
                        out=xstg.rearrange("p (b c) -> p b c", c=1024),
                        in_=x4[:, s, hg]), 16)
                    s_, v_ = wo_cons[hg % 2]
                    _w(gp, s_, v_)
                    _w(gp, sMM, gate_end)
                    if s == 0 and hg == 0:
                        _w(gp, sCVd, wd_ready)
                        _w(gp, sONE, bout_ready)
                    wo_ld = sWO[hg % 2].inc(gp.dma_start(
                        out=wobuf[hg % 2].rearrange("p (f c) -> p f c", c=1024),
                        in_=woT[hg].rearrange("f p c -> p f c"),
                        max_dma_last_dim=8192), 16)
                    down_done = 0
                    for ib in range(NWD):
                        bi = ib % 2
                        i0 = ib * 4
                        ni = min(4, IC - i0)
                        s_, v_ = wd_cons[bi]
                        _w(sy, s_, v_)
                        if ib < 2:
                            _w(sy, sMM, gate_end)
                        wd_ld = sWD[bi].inc(sy.dma_start(
                            out=wdbuf[bi][:, 0:ni * 1024].rearrange("p (i c) -> p i c", c=1024),
                            in_=wd_bf[hg, i0:i0 + ni].rearrange("i p c -> p i c")), 16)
                        _w(te, sWD[bi], wd_ld)
                        if ib == 0:
                            _w(te, sDV, hh_done)
                        for ii in range(ni):
                            i = i0 + ii
                            for hc in range(8):
                                if i == 0:
                                    bank_wait(hc)
                                instr = te.matmul(
                                    pbank(hc),
                                    wdbuf[bi][:, ii * 1024 + hc * 128:ii * 1024 + (hc + 1) * 128],
                                    hh[:, i * 512:(i + 1) * 512],
                                    start=(i == 0), stop=(i == IC - 1))
                        grp_v = sMM.inc(instr)
                        wd_cons[bi] = (sMM, grp_v)
                        down_done = grp_v
                    dv = [0] * 8
                    wo_done = [0] * 8
                    sm = [0] * 8
                    tp = [0] * 8

                    def ve_dsb(hc):
                        _w(ve, sMM, down_done)
                        dv[hc] = sDV.inc(ve.tensor_scalar(
                            out=dsb[hc % 2], in0=pbank(hc),
                            scalar1=P["sd"][:, hg * 8 + hc:hg * 8 + hc + 1],
                            scalar2=None, op0=ALU.mult))

                    def te_wout(hc):
                        _w(te, sDV, dv[hc])
                        if hc == 0:
                            _w(te, sWO[hg % 2], wo_ld)
                        for fc in range(C4):
                            te.matmul(pbank(hc),
                                      wobuf[hg % 2][:, fc * 1024 + hc * 128:fc * 1024 + hc * 128 + 128],
                                      hf[:, fc * TOK + s * 512:fc * TOK + (s + 1) * 512],
                                      start=(fc == 0), stop=False)
                        wo_done[hc] = sMM.inc(te.matmul(
                            pbank(hc),
                            boutb[0:1, hg * 1024 + hc * 128:hg * 1024 + hc * 128 + 128],
                            onesb[0:1, :], start=False, stop=True))

                    def ve_sum(hc):
                        _w(ve, sMM, wo_done[hc])
                        if hc >= 2:
                            _w(ve, sMM, tp[hc - 2])
                        sm[hc] = sDV.inc(ve.tensor_tensor(out=sumb[hc % 2], in0=pbank(hc),
                                                          in1=dsb[hc % 2], op=ALU.add))

                    def te_tp(hc):
                        _w(te, sDV, sm[hc])
                        for tb in range(4):
                            instr = te.transpose(pbank(hc)[:, tb * 128:(tb + 1) * 128],
                                                 sumb[hc % 2][:, tb * 128:(tb + 1) * 128],
                                                 identf[:, :])
                        tp[hc] = sMM.inc(instr)

                    def ve_add(hc):
                        nonlocal per_hc_add
                        _w(ve, sMM, tp[hc])
                        _w(ve, sXS, xs_ld)
                        if hc == 0:
                            s_, v_ = ostg_free
                            _w(ve, s_, v_)
                        for tb in range(4):
                            per_hc_add = sDV.inc(ve.tensor_tensor(
                                out=ostg[:, tb * 1024 + hc * 128:tb * 1024 + hc * 128 + 128],
                                in0=pbank(hc)[:, tb * 128:(tb + 1) * 128],
                                in1=xstg[:, tb * 1024 + hc * 128:tb * 1024 + hc * 128 + 128],
                                op=ALU.add))
                        bank_free[hc] = (sDV, per_hc_add)

                    ve_dsb(0)
                    ve_dsb(1)
                    te_wout(0)
                    ve_sum(0)
                    te_tp(0)
                    for hc in range(1, 8):
                        te_wout(hc)
                        if hc + 1 < 8:
                            ve_dsb(hc + 1)
                        ve_sum(hc)
                        te_tp(hc)
                        ve_add(hc - 1)
                    ve_add(7)
                    wo_cons[hg % 2] = (sMM, tp[7])
                    _w(sc, sDV, per_hc_add)
                    prev_store = sOS[hg % 2].inc(sc.dma_start(
                        out=o4[:, s, hg],
                        in_=ostg.rearrange("p (b c) -> p b c", c=1024)), 16)
                    ostg_free = (sOS[hg % 2], prev_store)
                    xstg_free = (sDV, per_hc_add)
                prev_down_reads = per_hc_add
                gbank_free = [bank_free[0], bank_free[1]]
                ubank_free = [bank_free[2], bank_free[3]]

            _w(sy, sOS[0], sOS[0].n)
            _w(sy, sOS[1], sOS[1].n)
            _w(sy, sDBG, sDBG.n)

    return nc, dbg_aps


def host_inputs(L0_attn_x, L0_attn_Win, L0_attn_bin, L0_attn_ln1_w, L0_attn_ln1_b,
                L0_attn_phz_re, L0_attn_phz_im, L0_attn_lc_re, L0_attn_lc_im,
                L0_attn_sc_lin, L0_attn_ln2_w, L0_attn_ln2_b, L0_attn_W1, L0_attn_b1,
                L0_attn_W2, L0_attn_b2, L0_attn_lnl_w, L0_attn_lnl_b, L0_attn_Wout,
                L0_attn_bout, L0_ffn_w_gate, L0_ffn_s_gate, L0_ffn_w_up, L0_ffn_s_up,
                L0_ffn_w_down, L0_ffn_s_down):
    f32 = np.float32
    cc = np.ascontiguousarray

    def pc(v, k):
        return cc(np.asarray(v, f32).reshape(k, 128).T)

    def pc_l(v):
        return cc(np.asarray(v, f32).reshape(NLAY, 4, 128).transpose(2, 0, 1)
                  .reshape(128, 16))

    winT_h = cc(np.asarray(L0_attn_Win, f32).T.reshape(VC32, 128, DIN)
                .transpose(1, 0, 2).reshape(128, VC32 * DIN))
    w1T_h = cc(np.asarray(L0_attn_W1, f32).transpose(0, 2, 1)
               .reshape(NLAY, C4, 128, DFF).transpose(0, 2, 1, 3)
               .reshape(NLAY, 128, C4 * DFF))
    w2T_h = cc(np.asarray(L0_attn_W2, f32).transpose(0, 2, 1)
               .reshape(NLAY, FC16, 128, DIN).transpose(0, 2, 1, 3)
               .reshape(NLAY, 128, FC16 * DIN))
    woT_h = cc(np.asarray(L0_attn_Wout, f32).T.reshape(C4, 128, HG, 1024)
               .transpose(2, 0, 1, 3))
    wg_h = cc(np.asarray(L0_ffn_w_gate).T.reshape(VC32, 128, IC, 128)
              .transpose(2, 1, 0, 3).reshape(IC, 128, 4096).astype(np.int32))
    wu_h = cc(np.asarray(L0_ffn_w_up).T.reshape(VC32, 128, IC, 128)
              .transpose(2, 1, 0, 3).reshape(IC, 128, 4096).astype(np.int32))
    wd_h = cc(np.asarray(L0_ffn_w_down).T.reshape(IC, 128, HG, 1024)
              .transpose(2, 0, 1, 3).astype(np.int32))

    base = dict(
        winT=winT_h, w1T=w1T_h, w2T=w2T_h, woT=woT_h, wg_i=wg_h, wu_i=wu_h, wd_i=wd_h,
        p_bin=pc(L0_attn_bin, C4), p_l1w=pc_l(L0_attn_ln1_w), p_l1b=pc_l(L0_attn_ln1_b),
        p_l2w=pc_l(L0_attn_ln2_w), p_l2b=pc_l(L0_attn_ln2_b), p_scl=pc_l(L0_attn_sc_lin),
        p_phr=pc_l(L0_attn_phz_re), p_phi=pc_l(L0_attn_phz_im),
        p_lcr=pc_l(L0_attn_lc_re), p_lci=pc_l(L0_attn_lc_im),
        p_b2=pc_l(L0_attn_b2),
        p_lnlw=pc(L0_attn_lnl_w, C4), p_lnlb=pc(L0_attn_lnl_b, C4),
        p_b1=cc(np.asarray(L0_attn_b1, f32).reshape(NLAY, FC16, 128)
                .transpose(2, 0, 1).reshape(128, NLAY * FC16)),
        p_sg=pc(L0_ffn_s_gate, IC), p_su=pc(L0_ffn_s_up, IC), p_sd=pc(L0_ffn_s_down, 32),
        p_eps=np.full((128, 1), EPS, f32),
        ident_b=np.eye(128, dtype=ml_dtypes.bfloat16),
        ident_f=np.eye(128, dtype=f32),
        om=np.full((128, 128), 1.0 / DIN, f32),
        bout=cc(np.asarray(L0_attn_bout, f32).reshape(1, V)),
        onesr=np.ones((1, 512), ml_dtypes.bfloat16),
    )
    x = np.asarray(L0_attn_x, f32)
    in_maps = []
    for core in range(NCORES):
        b, half = core // 2, core % 2
        m = dict(base)
        m["x"] = cc(x[b, half * TOK:(half + 1) * TOK, :])
        m["p_mv"] = np.full((128, 1), float(half), f32)
        m["p_mb"] = np.full((128, 1), 1.0 - float(half), f32)
        selv = np.zeros((8, 1), f32)
        selv[core - 1 if half == 1 else core] = 1.0
        m["sel"] = selv
        in_maps.append(m)
    return in_maps


_NC_CACHE = {}


def kernel(**inputs):
    if "main" not in _NC_CACHE:
        _NC_CACHE["main"] = build()
    nc, _ = _NC_CACHE["main"]
    in_maps = host_inputs(**inputs)
    res = run_bass_kernel_spmd(nc, in_maps, core_ids=list(range(NCORES)))
    out = np.empty((B, L, V), np.float32)
    for core in range(NCORES):
        b, half = core // 2, core % 2
        out[b, half * TOK:(half + 1) * TOK, :] = res.results[core]["out"]
    return out


# version 16
# speedup vs baseline: 1.0196x; 1.0009x over previous
import numpy as np
import ml_dtypes
from contextlib import ExitStack

import concourse.bass as bass
import concourse.mybir as mybir
from concourse.bass_utils import run_bass_kernel_spmd

F32 = mybir.dt.float32
BF16 = mybir.dt.bfloat16
I32 = mybir.dt.int32
AF = mybir.ActivationFunctionType
ALU = mybir.AluOpType

B, L, V = 4, 4096, 4096
TOK = 2048
DIN, DFF, ITR = 512, 2048, 11008
NLAY = 4
Q, NB = 128, 16
C4 = DIN // 128
FC16 = DFF // 128
VC32 = V // 128
IC = ITR // 128
HG = 4
EPS = 1e-5
NCORES = 8


class Sem:
    def __init__(self, nc, ctx, name):
        self.h = ctx.enter_context(nc.semaphore(name))
        self.n = 0

    def inc(self, instr, k=1):
        instr.then_inc(self.h, k)
        self.n += k
        return self.n


def _w(eng, sem, val):
    if val > 0:
        eng.wait_ge(sem.h, val)


def build(dbg=()):
    nc = bass.Bass("TRN2", target_bir_lowering=False, debug=False)
    ap = lambda t: t.ap()

    x_in = ap(nc.dram_tensor("x", [TOK, V], F32, kind="ExternalInput"))
    winT = ap(nc.dram_tensor("winT", [128, VC32 * DIN], F32, kind="ExternalInput"))
    w1T = ap(nc.dram_tensor("w1T", [NLAY, 128, C4 * DFF], F32, kind="ExternalInput"))
    w2T = ap(nc.dram_tensor("w2T", [NLAY, 128, FC16 * DIN], F32, kind="ExternalInput"))
    woT = ap(nc.dram_tensor("woT", [HG, C4, 128, 1024], F32, kind="ExternalInput"))
    wg_i = ap(nc.dram_tensor("wg_i", [IC, 128, 4096], I32, kind="ExternalInput"))
    wu_i = ap(nc.dram_tensor("wu_i", [IC, 128, 4096], I32, kind="ExternalInput"))
    wd_i = ap(nc.dram_tensor("wd_i", [HG, IC, 128, 1024], I32, kind="ExternalInput"))
    pv = {}
    for name, k in [("bin", C4), ("l1w", 16), ("l1b", 16), ("l2w", 16), ("l2b", 16),
                    ("scl", 16), ("phr", 16), ("phi", 16), ("lcr", 16), ("lci", 16),
                    ("b2", 16), ("lnlw", C4), ("lnlb", C4), ("b1", NLAY * FC16),
                    ("sg", IC), ("su", IC), ("sd", 32), ("mv", 1), ("mb", 1), ("eps", 1)]:
        pv[name] = ap(nc.dram_tensor("p_" + name, [128, k], F32, kind="ExternalInput"))
    ident_b = ap(nc.dram_tensor("ident_b", [128, 128], BF16, kind="ExternalInput"))
    ident_f = ap(nc.dram_tensor("ident_f", [128, 128], F32, kind="ExternalInput"))
    om_in = ap(nc.dram_tensor("om", [128, 128], F32, kind="ExternalInput"))
    sel_in = ap(nc.dram_tensor("sel", [8, 1], F32, kind="ExternalInput"))
    bout_in = ap(nc.dram_tensor("bout", [1, V], F32, kind="ExternalInput"))
    ones_in = ap(nc.dram_tensor("onesr", [1, 512], BF16, kind="ExternalInput"))
    out_t = ap(nc.dram_tensor("out", [TOK, V], F32, kind="ExternalOutput"))

    wg_bf = ap(nc.dram_tensor("wg_bf", [IC, 128, 4096], BF16))
    wu_bf = ap(nc.dram_tensor("wu_bf", [IC, 128, 4096], BF16))
    wd_bf = ap(nc.dram_tensor("wd_bf", [HG, IC, 128, 1024], BF16))
    xnt = nc.dram_tensor("xnt", [VC32, 128, TOK], BF16).ap().rearrange("v p t -> p v t")
    cc_in = ap(nc.dram_tensor("cc_in", [1, 1024], F32))
    cc_out = ap(nc.dram_tensor("cc_out", [8, 1024], F32))

    dbg_aps = {}
    for name in dbg:
        shp = {"h_a": [128, C4 * TOK], "h_f": [128, C4 * TOK], "xn0": [128, V],
               "y0": [128, TOK], "cv0": [128, TOK], "pc": [128, 32],
               "st0": [128, 16], "hp3": [128, C4 * TOK],
               "h_0": [128, C4 * TOK]}[name]
        dt = {"xn0": BF16}.get(name, F32)
        dbg_aps[name] = ap(nc.dram_tensor("dbg_" + name, shp, dt, kind="ExternalOutput"))

    with ExitStack() as ctx:
        sCC = Sem(nc, ctx, "sCC")
        sMM = Sem(nc, ctx, "sMM")
        sDV = Sem(nc, ctx, "sDV")
        sSC = Sem(nc, ctx, "sSC")
        sPRM = Sem(nc, ctx, "sPRM")
        sCVg = Sem(nc, ctx, "sCVg")
        sCVu = Sem(nc, ctx, "sCVu")
        sCVd = Sem(nc, ctx, "sCVd")
        sONE = Sem(nc, ctx, "sONE")
        sW1 = Sem(nc, ctx, "sW1")
        sW2 = Sem(nc, ctx, "sW2")
        sCCD = Sem(nc, ctx, "sCCD")
        sX = [Sem(nc, ctx, "sX0"), Sem(nc, ctx, "sX1")]
        sSL = [Sem(nc, ctx, "sSL0"), Sem(nc, ctx, "sSL1")]
        sXNT = Sem(nc, ctx, "sXNT")
        sWG = [Sem(nc, ctx, "sWG0"), Sem(nc, ctx, "sWG1")]
        sWU = [Sem(nc, ctx, "sWU0"), Sem(nc, ctx, "sWU1")]
        sWD = [Sem(nc, ctx, "sWDa"), Sem(nc, ctx, "sWDb")]
        sWO = [Sem(nc, ctx, "sWO0"), Sem(nc, ctx, "sWO1")]
        sXS = Sem(nc, ctx, "sXS")
        sOS = [Sem(nc, ctx, "sOS0"), Sem(nc, ctx, "sOS1")]
        sDBG = Sem(nc, ctx, "sDBG")
        sGP = Sem(nc, ctx, "sGP")
        sy, sc, ve, te, gp = nc.sync, nc.scalar, nc.vector, nc.tensor, nc.gpsimd

        sb = lambda n, shp, dt=F32: ctx.enter_context(nc.sbuf_tensor(n, shp, dt))
        P = {k: sb("sp_" + k, [128, v.shape[-1]]) for k, v in pv.items()}
        identb = sb("identb", [128, 128], BF16)
        identf = sb("identf", [128, 128], F32)
        omt = sb("omt", [128, 128], F32)
        selsb = sb("selsb", [8, 1], F32)
        onesb = sb("onesb", [1, 512], BF16)
        hf = sb("hf", [128, C4 * TOK], BF16)
        lcsr = sb("lcsr", [128, 16], F32)
        lcsi = sb("lcsi", [128, 16], F32)
        pcr = sb("pcr", [128, 16], F32)
        pci = sb("pci", [128, 16], F32)
        tmp16a = sb("tmp16a", [128, 16], F32)
        tmp16b = sb("tmp16b", [128, 16], F32)
        tmp16c = sb("tmp16c", [128, 16], F32)
        tmp16d = sb("tmp16d", [128, 16], F32)
        tmp16e = sb("tmp16e", [128, 16], F32)

        ps = ctx.enter_context(nc.psum_tensor("ps", [128, 8 * 512], F32))
        pbank = lambda b: ps[:, b * 512:(b + 1) * 512]
        pbank_b16 = lambda b: ps[:, :].bitcast(BF16)[:, b * 1024:(b + 1) * 1024]
        bank_free = [(sDV, 0)] * 8

        def bank_wait(b):
            s_, v_ = bank_free[b]
            _w(te, s_, v_)

        for k in P:
            sPRM.inc(sy.dma_start(out=P[k][:, :], in_=pv[k]), 16)
        sPRM.inc(sy.dma_start(out=identb[:, :], in_=ident_b), 16)
        sPRM.inc(sy.dma_start(out=identf[:, :], in_=ident_f), 16)
        sPRM.inc(sy.dma_start(out=omt[:, :], in_=om_in), 16)
        sPRM.inc(sy.dma_start(out=selsb[:, :], in_=sel_in), 16)
        params_ready = sPRM.inc(sy.dma_start(out=onesb[:, :], in_=ones_in), 16)

        CHUNK = 4 * 1024 * 1024

        def conv_chunks(sem, dst, src, n_elem, c0=0, c1=10**9):
            dims = len(src.shape)
            names = " ".join(f"a{i}" for i in range(dims))
            sflat = src.rearrange(f"{names} -> ({names})")
            dflat = dst.rearrange(f"{names} -> ({names})")
            v = 0
            ncall = (n_elem + CHUNK - 1) // CHUNK
            for ci in range(max(0, c0), min(ncall, c1)):
                o = ci * CHUNK
                e = min(n_elem, o + CHUNK)
                v = sem.inc(gp.dma_start(out=dflat[o:e], in_=sflat[o:e],
                                         max_dma_last_dim=8192), 16)
            return v

        NGE = IC * 128 * 4096
        wg_ready = wu_ready = wd_ready = 0


        def vfence(instr):
            v = sDV.inc(instr)
            ve.wait_ge(sDV.h, v)
            return v

        def sfence(instr):
            v = sSC.inc(instr)
            sc.wait_ge(sSC.h, v)
            return v

        _w(ve, sPRM, params_ready)
        ve.tensor_tensor(out=tmp16a[:, :], in0=P["phr"][:, :], in1=P["phr"][:, :], op=ALU.mult)
        vfence(ve.tensor_tensor(out=tmp16b[:, :], in0=P["phi"][:, :], in1=P["phi"][:, :], op=ALU.mult))
        r2_done = sDV.inc(ve.tensor_tensor(out=tmp16c[:, :], in0=tmp16a[:, :],
                                           in1=tmp16b[:, :], op=ALU.add))
        _w(sc, sDV, r2_done)
        sfence(sc.activation(out=tmp16a[:, :], in_=tmp16c[:, :], func=AF.Ln))
        sfence(sc.activation(out=tmp16b[:, :], in_=tmp16a[:, :], func=AF.Exp, scale=0.5))
        sc.activation(out=tmp16d[:, :], in_=tmp16a[:, :], func=AF.Exp, scale=-0.5)
        mag_done = sSC.inc(sc.activation(out=tmp16e[:, :], in_=tmp16b[:, :], func=AF.Tanh))
        _w(ve, sSC, mag_done)
        vfence(ve.tensor_tensor(out=tmp16c[:, :], in0=tmp16e[:, :], in1=tmp16d[:, :], op=ALU.mult))
        ve.tensor_tensor(out=pcr[:, :], in0=P["phr"][:, :], in1=tmp16c[:, :], op=ALU.mult)
        ve.tensor_tensor(out=pci[:, :], in0=P["phi"][:, :], in1=tmp16c[:, :], op=ALU.mult)
        ve.tensor_tensor(out=lcsr[:, :], in0=P["lcr"][:, :], in1=P["scl"][:, :], op=ALU.mult)
        pcv = sDV.inc(ve.tensor_tensor(out=lcsi[:, :], in0=P["lci"][:, :], in1=P["scl"][:, :], op=ALU.mult))
        if "pc" in dbg_aps:
            _w(sy, sDV, pcv)
            sDBG.inc(sy.dma_start(out=dbg_aps["pc"][:, 0:16], in_=pcr[:, :]), 16)
            sDBG.inc(sy.dma_start(out=dbg_aps["pc"][:, 16:32], in_=pci[:, :]), 16)

        with ExitStack() as abctx:
            absb = lambda n, shp, dt=F32: abctx.enter_context(nc.sbuf_tensor(n, shp, dt))
            h = absb("h", [128, C4 * TOK], F32)
            hsl = lambda c, a, b_: h[:, c * TOK + a:c * TOK + b_]

            with ExitStack() as actx:
                actx.enter_context(nc.named_scope("stageA"))
                asb = lambda n, shp, dt=F32: actx.enter_context(nc.sbuf_tensor(n, shp, dt))
                xbuf = [asb("xa", [128, V]), asb("xb", [128, V])]
                xnb = [asb("xna", [128, V], BF16), asb("xnb2", [128, V], BF16)]
                stg = [asb("stga", [128, VC32 * 512], BF16),
                       asb("stgb", [128, VC32 * 512], BF16)]
                winTb = asb("winTb", [128, VC32 * DIN], BF16)
                stat = asb("stat", [128, 16], F32)
                junkf = asb("junkf", [128, V // 2], F32)
                winT_ready = sONE.inc(gp.dma_start(out=winTb[:, :], in_=winT,
                                                   max_dma_last_dim=8192), 16)
                wg_ready = conv_chunks(sCVg, wg_bf, wg_i, NGE)
                _w(te, sPRM, params_ready)

                xbuf_free = [(sDV, 0), (sDV, 0)]
                xnb_free = [(sMM, 0), (sMM, 0)]
                junk_free = (sDV, 0)
                stg_free = [(sMM, 0), (sMM, 0)]
                stg_wr = [0, 0]
                tp_bank_free = [(sDV, 0), (sDV, 0)]
                stg_evac = [0, 0]
                h_a_done = 0

                for tc in range(16):
                    bi = tc % 2
                    sl = tc // 4
                    sbl = sl % 2
                    st_off = (tc % 4) * 128
                    s_, v_ = xbuf_free[bi]
                    _w(sy, s_, v_)
                    x_ld = sX[bi].inc(sy.dma_start(out=xbuf[bi][:, :],
                                                   in_=x_in[tc * 128:(tc + 1) * 128, :]), 16)
                    _w(ve, sX[bi], x_ld)
                    _w(ve, sPRM, params_ready)
                    ve.tensor_reduce(out=stat[:, 8 * bi + 0:8 * bi + 1],
                                     in_=xbuf[bi][:, :], axis=mybir.AxisListType.X,
                                     op=ALU.add)
                    _w(sc, sX[bi], x_ld)
                    _w(sc, sPRM, params_ready)
                    s_, v_ = junk_free
                    _w(sc, s_, v_)
                    sqA = sSC.inc(sc.activation(out=junkf[:, :],
                                                in_=xbuf[bi][:, 0:V // 2],
                                                func=AF.Square))
                    _w(ve, sSC, sqA)
                    rA = sDV.inc(ve.tensor_reduce(
                        out=stat[:, 8 * bi + 1:8 * bi + 2], in_=junkf[:, :],
                        axis=mybir.AxisListType.X, op=ALU.add))
                    _w(sc, sDV, rA)
                    sqB = sSC.inc(sc.activation(out=junkf[:, :],
                                                in_=xbuf[bi][:, V // 2:V],
                                                func=AF.Square))
                    _w(ve, sSC, sqB)
                    red2 = sDV.inc(ve.tensor_reduce(
                        out=stat[:, 8 * bi + 3:8 * bi + 4], in_=junkf[:, :],
                        axis=mybir.AxisListType.X, op=ALU.add))
                    junk_free = (sDV, red2)
                    ve.wait_ge(sDV.h, red2)
                    ve.tensor_tensor(out=stat[:, 8 * bi + 6:8 * bi + 7],
                                     in0=stat[:, 8 * bi + 1:8 * bi + 2],
                                     in1=stat[:, 8 * bi + 3:8 * bi + 4], op=ALU.add)
                    w2_ = sDV.inc(ve.tensor_scalar(out=stat[:, 8 * bi + 2:8 * bi + 3],
                                                   in0=stat[:, 8 * bi + 0:8 * bi + 1],
                                                   scalar1=1.0 / V, scalar2=None,
                                                   op0=ALU.mult))
                    ve.wait_ge(sDV.h, w2_)
                    ve.tensor_scalar(out=stat[:, 8 * bi + 3:8 * bi + 4],
                                     in0=stat[:, 8 * bi + 6:8 * bi + 7],
                                     scalar1=1.0 / V, scalar2=None, op0=ALU.mult)
                    w4_ = sDV.inc(ve.tensor_tensor(out=stat[:, 8 * bi + 4:8 * bi + 5],
                                                   in0=stat[:, 8 * bi + 2:8 * bi + 3],
                                                   in1=stat[:, 8 * bi + 2:8 * bi + 3],
                                                   op=ALU.mult))
                    ve.wait_ge(sDV.h, w4_)
                    var_done = sDV.inc(ve.tensor_tensor(
                        out=stat[:, 8 * bi + 5:8 * bi + 6],
                        in0=stat[:, 8 * bi + 3:8 * bi + 4],
                        in1=stat[:, 8 * bi + 4:8 * bi + 5], op=ALU.subtract))
                    _w(sc, sDV, var_done)
                    lnv = sSC.inc(sc.activation(out=stat[:, 8 * bi + 6:8 * bi + 7],
                                                in_=stat[:, 8 * bi + 5:8 * bi + 6],
                                                func=AF.Ln, bias=P["eps"][:, 0:1]))
                    sc.wait_ge(sSC.h, lnv)
                    rstd_done = sSC.inc(sc.activation(
                        out=stat[:, 8 * bi + 7:8 * bi + 8],
                        in_=stat[:, 8 * bi + 6:8 * bi + 7], func=AF.Exp, scale=-0.5))
                    s_, v_ = xnb_free[bi]
                    _w(ve, s_, v_)
                    _w(ve, sSC, rstd_done)
                    xn_done = sDV.inc(ve.tensor_scalar(
                        out=xnb[bi][:, :], in0=xbuf[bi][:, :],
                        scalar1=stat[:, 8 * bi + 2:8 * bi + 3],
                        scalar2=stat[:, 8 * bi + 7:8 * bi + 8],
                        op0=ALU.subtract, op1=ALU.mult))
                    xbuf_free[bi] = (sDV, xn_done)
                    if "xn0" in dbg_aps and tc == 0:
                        _w(sy, sDV, xn_done)
                        tv = sDBG.inc(sy.dma_start(out=dbg_aps["xn0"], in_=xnb[bi][:, :]), 16)
                        _w(ve, sDBG, tv)
                        _w(sc, sDBG, tv)
                    if "st0" in dbg_aps and tc == 0:
                        _w(sy, sDV, xn_done)
                        tv = sDBG.inc(sy.dma_start(out=dbg_aps["st0"], in_=stat[:, :]), 16)
                        _w(ve, sDBG, tv)
                        _w(sc, sDBG, tv)
                    last_tp = 0
                    for g in range(4):
                        tb = g % 2
                        s_, v_ = tp_bank_free[tb]
                        _w(te, s_, v_)
                        _w(te, sDV, xn_done)
                        for k in range(8):
                            vc = g * 8 + k
                            instr = te.transpose(pbank_b16(4 + tb)[:, k * 128:(k + 1) * 128],
                                                 xnb[bi][:, vc * 128:(vc + 1) * 128],
                                                 identb[:, :])
                        tp_done = sMM.inc(instr)
                        last_tp = tp_done
                        if g == 0:
                            s_, v_ = stg_free[sbl]
                            _w(ve, s_, v_)
                            _w(ve, sSL[sbl], stg_wr[sbl])
                        _w(ve, sMM, tp_done)
                        dst = stg[sbl][:, :].rearrange("p (v t) -> p v t", t=512)
                        src = pbank_b16(4 + tb)[:, :].rearrange("p (k t) -> p k t", t=128)
                        ev = sDV.inc(ve.tensor_copy(
                            out=dst[:, g * 8:(g + 1) * 8, st_off:st_off + 128], in_=src))
                        tp_bank_free[tb] = (sDV, ev)
                        stg_evac[sbl] = ev
                    xnb_free[bi] = (sMM, last_tp)

                    if tc % 4 == 3:
                        _w(sy, sDV, stg_evac[sbl])
                        stg_wr[sbl] = sSL[sbl].inc(sy.dma_start(
                            out=xnt[:, :, sl * 512:(sl + 1) * 512],
                            in_=stg[sbl][:, :].rearrange("p (v t) -> p v t", t=512)), 16)
                        _w(te, sONE, winT_ready)
                        _w(te, sDV, stg_evac[sbl])
                        win_done = [0] * C4
                        for mc in range(C4):
                            bank_wait(mc)
                            for vc in range(VC32):
                                instr = te.matmul(
                                    pbank(mc),
                                    winTb[:, vc * DIN + mc * 128:vc * DIN + mc * 128 + 128],
                                    stg[sbl][:, vc * 512:(vc + 1) * 512],
                                    start=(vc == 0), stop=(vc == VC32 - 1))
                            win_done[mc] = sMM.inc(instr)
                        for mc in range(C4):
                            _w(ve, sMM, win_done[mc])
                            h_a_done = sDV.inc(ve.tensor_scalar(
                                out=h[:, mc * TOK + sl * 512:mc * TOK + (sl + 1) * 512],
                                in0=pbank(mc), scalar1=P["bin"][:, mc:mc + 1],
                                scalar2=None, op0=ALU.add))
                            bank_free[mc] = (sDV, h_a_done)
                        stg_free[sbl] = (sMM, win_done[C4 - 1])

                if "h_a" in dbg_aps:
                    _w(sy, sDV, h_a_done)
                    tv = sDBG.inc(sy.dma_start(out=dbg_aps["h_a"], in_=h[:, :]), 16)
                    _w(ve, sDBG, tv)
                for eng in (ve, sc, te, gp, sy):
                    _w(eng, sDV, sDV.n)
                    _w(eng, sSC, sSC.n)
                    _w(eng, sMM, sMM.n)
                    for fam in (sX[0], sX[1], sSL[0], sSL[1]):
                        _w(eng, fam, fam.n)
                    _w(eng, sPRM, sPRM.n)
                    _w(eng, sONE, winT_ready)

            with ExitStack() as bctx:
                bctx.enter_context(nc.named_scope("stageB"))
                bsb = lambda n, shp, dt=F32: bctx.enter_context(nc.sbuf_tensor(n, shp, dt))
                zA = bsb("zA", [128, 2 * TOK])
                zB = bsb("zB", [128, 2 * TOK])
                zAr, zAi = zA[:, 0:TOK], zA[:, TOK:2 * TOK]
                zBr, zBi = zB[:, 0:TOK], zB[:, TOK:2 * TOK]
                y2buf = zA[:, :].bitcast(BF16)
                convr = bsb("convr", [128, C4 * TOK])
                y2in = convr[:, :].bitcast(BF16)[:, 0:C4 * TOK]
                y_c = bsb("y_c", [128, TOK])
                mu_all = bsb("mu_all", [128, TOK])
                rstd_all = bsb("rstd_all", [128, TOK])
                sq4 = bsb("sq4", [128, 4 * 512])
                var4 = sq4
                m2b4 = bsb("m2b4", [128, 4 * 512])
                w1t = bsb("w1t", [128, C4 * DFF], BF16)
                w2t = bsb("w2t", [128, FC16 * DIN], BF16)
                P1r = bsb("P1r", [128, C4 * Q])
                P1i = bsb("P1i", [128, C4 * Q])
                nP1i = bsb("nP1i", [128, C4 * Q])
                PBr = bsb("PBr", [128, C4 * NB])
                PBi = bsb("PBi", [128, C4 * NB])
                S0r = bsb("S0r", [128, C4 * NB])
                S0i = bsb("S0i", [128, C4 * NB])
                E2r = bsb("E2r", [128, NB]); E2i = bsb("E2i", [128, NB])
                T2r = bsb("T2r", [128, NB]); T2i = bsb("T2i", [128, NB])
                tA = bsb("tA", [128, 512]); tB = bsb("tB", [128, 512])
                Sinr4 = bsb("Sinr4", [128, C4 * NB])
                Sini4 = bsb("Sini4", [128, C4 * NB])
                ccrow = m2b4[0:1, 0:1024]
                Gsb = m2b4[0:8, 1024:2048]
                Ssel8 = bsb("Ssel8", [128, 8])
                Ic2 = bsb("Ic2", [128, 32])

                w1_ready = [0] * NLAY
                w2_ready = [0] * NLAY
                w1_lastmm = [0] * NLAY
                w2_lastmm = [0] * NLAY
                tI1 = w1t[:, :].bitcast(F32)[:, 0:TOK]
                tI2 = w1t[:, :].bitcast(F32)[:, TOK:2 * TOK]
                wu_ready = conv_chunks(sCVu, wu_bf, wu_i, NGE, 0, 5)

                def ln_stats():
                    _w(te, sDV, sDV.n)
                    _w(sc, sDV, sDV.n)
                    sq_done = [0] * C4
                    prev_grp = 0
                    for c in range(C4):
                        _w(sc, sMM, prev_grp)
                        for t4 in range(4):
                            sq_done[c] = sSC.inc(sc.activation(
                                out=sq4[:, t4 * 512:(t4 + 1) * 512],
                                in_=hsl(c, t4 * 512, (t4 + 1) * 512), func=AF.Square))
                        for t4 in range(4):
                            if c == 0:
                                bank_wait(t4)
                            te.matmul(pbank(t4), omt[:, :],
                                      hsl(c, t4 * 512, (t4 + 1) * 512),
                                      start=(c == 0), stop=(c == C4 - 1))
                        _w(te, sSC, sq_done[c])
                        for t4 in range(4):
                            if c == 0:
                                bank_wait(4 + t4)
                            instr = te.matmul(pbank(4 + t4), omt[:, :],
                                              sq4[:, t4 * 512:(t4 + 1) * 512],
                                              start=(c == 0), stop=(c == C4 - 1))
                        prev_grp = sMM.inc(instr)
                    mm_done = prev_grp
                    _w(sc, sMM, mm_done)
                    muc = 0
                    for t4 in range(4):
                        muc = sSC.inc(sc.copy(out=mu_all[:, t4 * 512:(t4 + 1) * 512],
                                              in_=pbank(t4)))
                    _w(ve, sSC, muc)
                    _w(ve, sMM, mm_done)
                    var_v = 0
                    for t4 in range(4):
                        ve.tensor_tensor(out=m2b4[:, t4 * 512:(t4 + 1) * 512],
                                         in0=mu_all[:, t4 * 512:(t4 + 1) * 512],
                                         in1=mu_all[:, t4 * 512:(t4 + 1) * 512],
                                         op=ALU.mult)
                        var_v = sDV.inc(ve.tensor_tensor(
                            out=var4[:, t4 * 512:(t4 + 1) * 512], in0=pbank(4 + t4),
                            in1=m2b4[:, t4 * 512:(t4 + 1) * 512], op=ALU.subtract))
                        bank_free[t4] = (sSC, muc)
                        bank_free[4 + t4] = (sDV, var_v)
                    _w(sc, sDV, var_v)
                    rstd_v = 0
                    for t4 in range(4):
                        sc.activation(out=m2b4[:, t4 * 512:(t4 + 1) * 512],
                                      in_=var4[:, t4 * 512:(t4 + 1) * 512], func=AF.Ln,
                                      bias=P["eps"][:, 0:1])
                    for t4 in range(4):
                        rstd_v = sSC.inc(sc.activation(
                            out=rstd_all[:, t4 * 512:(t4 + 1) * 512],
                            in_=m2b4[:, t4 * 512:(t4 + 1) * 512],
                            func=AF.Exp, scale=-0.5))
                    return rstd_v

                def vsync(instr):
                    v = sDV.inc(instr)
                    ve.wait_ge(sDV.h, v)
                    return v

                def ssync(instr):
                    v = sSC.inc(instr)
                    sc.wait_ge(sSC.h, v)
                    return v

                def ctab_all(Tr, Ti, width):
                    k = 1
                    while k < width:
                        n = min(k, width - k)
                        for c in range(C4):
                            o = c * width
                            cr, ci = Tr[:, o + k - 1:o + k], Ti[:, o + k - 1:o + k]
                            ve.tensor_scalar(out=tA[:, c * 64:c * 64 + n],
                                             in0=Tr[:, o:o + n], scalar1=cr,
                                             scalar2=None, op0=ALU.mult)
                            ve.tensor_scalar(out=tB[:, c * 64:c * 64 + n],
                                             in0=Ti[:, o:o + n], scalar1=ci,
                                             scalar2=None, op0=ALU.mult)
                            ve.tensor_scalar(out=tA[:, 256 + c * 64:256 + c * 64 + n],
                                             in0=Tr[:, o:o + n], scalar1=ci,
                                             scalar2=None, op0=ALU.mult)
                            last = ve.tensor_scalar(out=tB[:, 256 + c * 64:256 + c * 64 + n],
                                                    in0=Ti[:, o:o + n], scalar1=cr,
                                                    scalar2=None, op0=ALU.mult)
                        vsync(last)
                        for c in range(C4):
                            o = c * width
                            ve.tensor_tensor(out=Tr[:, o + k:o + k + n],
                                             in0=tA[:, c * 64:c * 64 + n],
                                             in1=tB[:, c * 64:c * 64 + n], op=ALU.subtract)
                            last = ve.tensor_tensor(out=Ti[:, o + k:o + k + n],
                                                    in0=tA[:, 256 + c * 64:256 + c * 64 + n],
                                                    in1=tB[:, 256 + c * 64:256 + c * 64 + n],
                                                    op=ALU.add)
                        vsync(last)
                        k *= 2

                for l in range(NLAY):
                    lc = lambda t, c: P[t][:, l * 4 + c:l * 4 + c + 1]
                    _w(ve, sMM, sMM.n)
                    _w(sc, sMM, sMM.n)

                    rstd1 = ln_stats()
                    last = None
                    for c in range(C4):
                        ve.tensor_copy(out=P1r[:, c * Q:c * Q + 1],
                                       in_=pcr[:, l * 4 + c:l * 4 + c + 1])
                        last = ve.tensor_copy(out=P1i[:, c * Q:c * Q + 1],
                                              in_=pci[:, l * 4 + c:l * 4 + c + 1])
                    vsync(last)
                    ctab_all(P1r, P1i, Q)
                    ve.tensor_scalar(out=nP1i[:, :], in0=P1i[:, :], scalar1=-1.0,
                                     scalar2=None, op0=ALU.mult)
                    last = None
                    for c in range(C4):
                        ve.tensor_copy(out=PBr[:, c * NB:c * NB + 1],
                                       in_=P1r[:, c * Q + Q - 1:c * Q + Q])
                        last = ve.tensor_copy(out=PBi[:, c * NB:c * NB + 1],
                                              in_=P1i[:, c * Q + Q - 1:c * Q + Q])
                    vsync(last)
                    ctab_all(PBr, PBi, NB)
                    for c in range(C4):
                        P1r_c = P1r[:, c * Q:(c + 1) * Q]
                        P1i_c = P1i[:, c * Q:(c + 1) * Q]
                        nP1i_c = nP1i[:, c * Q:(c + 1) * Q]
                        PBr_c = PBr[:, c * NB:(c + 1) * NB]
                        PBi_c = PBi[:, c * NB:(c + 1) * NB]
                        S0r_c = S0r[:, c * NB:(c + 1) * NB]
                        S0i_c = S0i[:, c * NB:(c + 1) * NB]
                        convr_c = convr[:, c * TOK:(c + 1) * TOK]
                        _w(ve, sSC, rstd1)
                        ve.tensor_tensor(out=y_c[:, :], in0=hsl(c, 0, TOK),
                                         in1=mu_all[:, :], op=ALU.subtract)
                        ve.tensor_tensor(out=y_c[:, :], in0=y_c[:, :],
                                         in1=rstd_all[:, :], op=ALU.mult)
                        yv = sDV.inc(ve.tensor_scalar(out=y_c[:, :], in0=y_c[:, :],
                                                      scalar1=lc("l1w", c),
                                                      scalar2=lc("l1b", c),
                                                      op0=ALU.mult, op1=ALU.add))
                        _w(sc, sDV, yv)
                        sil_v = sSC.inc(sc.activation(out=y_c[:, :], in_=y_c[:, :],
                                                      func=AF.Silu))
                        _w(ve, sSC, sil_v)
                        yscl = sDV.inc(ve.tensor_scalar(out=y_c[:, :], in0=y_c[:, :],
                                                        scalar1=lc("scl", c), scalar2=None,
                                                        op0=ALU.mult))
                        if "y0" in dbg_aps and l == 0 and c == 0:
                            ydbg = sDV.inc(ve.nop())
                            _w(sy, sDV, ydbg)
                            tv = sDBG.inc(sy.dma_start(out=dbg_aps["y0"], in_=y_c[:, :]), 16)
                            _w(ve, sDBG, tv)
                        y3 = y_c[:, :].rearrange("p (b q) -> p b q", q=Q)
                        c3r = zBr.rearrange("p (b q) -> p b q", q=Q)
                        c3i = zBi.rearrange("p (b q) -> p b q", q=Q)
                        i3a = tI1[:, :].rearrange("p (b q) -> p b q", q=Q)
                        ve.scalar_tensor_tensor(out=c3r[:, :, 1:Q], in0=y3[:, :, 0:Q - 1],
                                                scalar=P1r_c[:, 0:1], in1=y3[:, :, 1:Q],
                                                op0=ALU.mult, op1=ALU.add)
                        vstep = sDV.inc(ve.tensor_copy(out=c3r[:, :, 0:1], in_=y3[:, :, 0:1]))
                        ve.wait_ge(sDV.h, vstep)
                        _w(sc, sDV, yscl)
                        sstep = sSC.inc(sc.activation(out=c3i[:, :, 1:Q],
                                                      in_=y3[:, :, 0:Q - 1], func=AF.Copy,
                                                      scale=P1i_c[:, 0:1]))
                        _w(gp, sSC, sstep)
                        gstep = sGP.inc(gp.memset(c3i[:, :, 0:1], 0.0))
                        sstep_prev = sstep
                        pairs = [(zAr, zAi), (zBr, zBi)]
                        cur = 1
                        s = 2
                        while s < Q:
                            last = (s * 2 >= Q)
                            nxt = 1 - cur
                            cr_, ci_ = pairs[cur]
                            nr_ = convr_c if last else pairs[nxt][0]
                            ni_ = pairs[nxt][1]
                            a3r = cr_.rearrange("p (b q) -> p b q", q=Q)
                            a3i = ci_.rearrange("p (b q) -> p b q", q=Q)
                            b3r = nr_.rearrange("p (b q) -> p b q", q=Q)
                            b3i = ni_.rearrange("p (b q) -> p b q", q=Q)
                            _w(sc, sDV, vstep)
                            _w(sc, sGP, gstep)
                            sstep = sSC.inc(sc.activation(out=i3a[:, :, s:Q],
                                                          in_=a3r[:, :, 0:Q - s],
                                                          func=AF.Copy,
                                                          scale=P1i_c[:, s - 1:s]))
                            _w(ve, sGP, gstep)
                            _w(ve, sSC, sstep_prev)
                            ve.scalar_tensor_tensor(out=b3r[:, :, s:Q],
                                                    in0=a3r[:, :, 0:Q - s],
                                                    scalar=P1r_c[:, s - 1:s],
                                                    in1=a3r[:, :, s:Q],
                                                    op0=ALU.mult, op1=ALU.add)
                            ve.scalar_tensor_tensor(out=b3r[:, :, s:Q],
                                                    in0=a3i[:, :, 0:Q - s],
                                                    scalar=nP1i_c[:, s - 1:s],
                                                    in1=b3r[:, :, s:Q],
                                                    op0=ALU.mult, op1=ALU.add)
                            ve.scalar_tensor_tensor(out=b3i[:, :, s:Q],
                                                    in0=a3i[:, :, 0:Q - s],
                                                    scalar=P1r_c[:, s - 1:s],
                                                    in1=a3i[:, :, s:Q],
                                                    op0=ALU.mult, op1=ALU.add)
                            vstep = sDV.inc(ve.tensor_copy(out=b3r[:, :, 0:s],
                                                           in_=a3r[:, :, 0:s]))
                            ve.wait_ge(sDV.h, vstep)
                            _w(gp, sSC, sstep)
                            _w(gp, sDV, vstep)
                            gp.tensor_copy(out=b3i[:, :, 0:s], in_=a3i[:, :, 0:s])
                            gstep = sGP.inc(gp.tensor_tensor(out=b3i[:, :, s:Q],
                                                             in0=i3a[:, :, s:Q],
                                                             in1=b3i[:, :, s:Q],
                                                             op=ALU.add))
                            sstep_prev = sstep
                            cur = nxt
                            s *= 2
                        zfi = pairs[cur][1]
                        cv3 = convr_c.rearrange("p (b q) -> p b q", q=Q)
                        zi3 = zfi.rearrange("p (b q) -> p b q", q=Q)
                        _w(ve, sGP, gstep)
                        ve.tensor_copy(out=E2r[:, :], in_=cv3[:, :, Q - 1:Q])
                        vsync(ve.tensor_copy(out=E2i[:, :], in_=zi3[:, :, Q - 1:Q]))
                        srcs = [(E2r, E2i), (T2r, T2i)]
                        cu = 0
                        s = 1
                        while s < NB:
                            last = (s * 2 >= NB)
                            sr, si_ = srcs[cu]
                            dr, di_ = (S0r_c, S0i_c) if last else srcs[1 - cu]
                            ve.scalar_tensor_tensor(out=dr[:, s:NB], in0=sr[:, 0:NB - s],
                                                    scalar=PBr_c[:, s - 1:s],
                                                    in1=sr[:, s:NB],
                                                    op0=ALU.mult, op1=ALU.add)
                            ve.scalar_tensor_tensor(out=di_[:, s:NB], in0=sr[:, 0:NB - s],
                                                    scalar=PBi_c[:, s - 1:s],
                                                    in1=si_[:, s:NB],
                                                    op0=ALU.mult, op1=ALU.add)
                            vsync(ve.tensor_scalar(out=tA[:, 0:NB - s],
                                                   in0=si_[:, 0:NB - s],
                                                   scalar1=PBi_c[:, s - 1:s], scalar2=None,
                                                   op0=ALU.mult))
                            ve.tensor_tensor(out=dr[:, s:NB], in0=dr[:, s:NB],
                                             in1=tA[:, 0:NB - s], op=ALU.subtract)
                            ve.scalar_tensor_tensor(out=di_[:, s:NB], in0=si_[:, 0:NB - s],
                                                    scalar=PBr_c[:, s - 1:s],
                                                    in1=di_[:, s:NB],
                                                    op0=ALU.mult, op1=ALU.add)
                            ve.tensor_copy(out=dr[:, 0:s], in_=sr[:, 0:s])
                            vsync(ve.tensor_copy(out=di_[:, 0:s], in_=si_[:, 0:s]))
                            cu = 1 - cu
                            s *= 2
                        s0c = sDV.n
                        _w(te, sDV, s0c)
                        bank_wait(6)
                        te.transpose(pbank(6)[0:1, 0:128], S0r_c[:, NB - 1:NB], identf[:, :])
                        tpv = sMM.inc(te.transpose(pbank(6)[0:1, 128:256],
                                                   S0i_c[:, NB - 1:NB], identf[:, :]))
                        _w(ve, sMM, tpv)
                        ve.tensor_copy(out=ccrow[0:1, c * 128:(c + 1) * 128],
                                       in_=pbank(6)[0:1, 0:128])
                        ccv = sDV.inc(ve.tensor_copy(
                            out=ccrow[0:1, 512 + c * 128:512 + (c + 1) * 128],
                            in_=pbank(6)[0:1, 128:256]))
                        bank_free[6] = (sDV, ccv)

                    _w(gp, sDV, sDV.n)
                    _w(gp, sSC, sSC.n)
                    cc_wr = sCCD.inc(gp.dma_start(out=cc_in, in_=ccrow[0:1, :]), 16)
                    _w(gp, sCCD, cc_wr)
                    cc_done = sCC.inc(gp.collective_compute(
                        "AllGather", ALU.bypass, replica_groups=[list(range(NCORES))],
                        ins=[cc_in.opt()], outs=[cc_out.opt()]))
                    _w(gp, sCC, cc_done)
                    g_ld = sCCD.inc(gp.dma_start(out=Gsb[:, :], in_=cc_out), 16)
                    if l > 0:
                        _w(gp, sMM, w1_lastmm[l - 1])
                        _w(gp, sMM, w2_lastmm[l - 1])
                    w1_ready[l] = sW1.inc(gp.dma_start(out=w1t[:, :], in_=w1T[l],
                                                       max_dma_last_dim=8192), 16)
                    w2_ready[l] = sW2.inc(gp.dma_start(out=w2t[:, :], in_=w2T[l],
                                                       max_dma_last_dim=8192), 16)
                    NWDE = HG * IC * 128 * 1024
                    if l == 0:
                        wu_ready = conv_chunks(sCVu, wu_bf, wu_i, NGE, 5, 11)
                    elif l == 1:
                        wd_ready = conv_chunks(sCVd, wd_bf, wd_i, NWDE, 0, 6)
                    elif l == 2:
                        wd_ready = conv_chunks(sCVd, wd_bf, wd_i, NWDE, 6, 11)

                    _w(te, sCCD, g_ld)
                    bank_wait(7)
                    for c in range(C4):
                        te.matmul(pbank(7)[:, c:c + 1], Gsb[0:8, c * 128:(c + 1) * 128],
                                  selsb[:, :], start=True, stop=True)
                        instr = te.matmul(pbank(7)[:, 4 + c:5 + c],
                                          Gsb[0:8, 512 + c * 128:512 + (c + 1) * 128],
                                          selsb[:, :], start=True, stop=True)
                    selv = sMM.inc(instr)
                    _w(ve, sMM, selv)
                    sv2 = vsync(ve.tensor_copy(out=Ssel8[:, :], in_=pbank(7)[:, 0:8]))
                    bank_free[7] = (sDV, sv2)
                    L4 = slice(l * 4, l * 4 + 4)
                    P2r4 = PBr[:, :].rearrange("p (c n) -> p c n", n=NB)[:, :, NB - 1:NB]
                    P2i4 = PBi[:, :].rearrange("p (c n) -> p c n", n=NB)[:, :, NB - 1:NB]
                    I_r = Ic2[:, 0:4]; I_i = Ic2[:, 4:8]; nI_i = Ic2[:, 8:12]
                    t1r = Ic2[:, 12:16]; t1i = Ic2[:, 16:20]
                    m1 = Ic2[:, 20:24]; m2 = Ic2[:, 24:28]; m3 = Ic2[:, 28:32]
                    p2r_f = tA[:, 0:4]; p2i_f = tA[:, 4:8]
                    ve.tensor_copy(out=p2r_f.rearrange("p (c n) -> p c n", n=1), in_=P2r4)
                    vsync(ve.tensor_copy(out=p2i_f.rearrange("p (c n) -> p c n", n=1), in_=P2i4))
                    ve.tensor_tensor(out=m1, in0=p2r_f, in1=lcsr[:, L4], op=ALU.mult)
                    ve.tensor_tensor(out=m2, in0=p2i_f, in1=lcsi[:, L4], op=ALU.mult)
                    ve.tensor_tensor(out=m3, in0=p2r_f, in1=lcsi[:, L4], op=ALU.mult)
                    vsync(ve.tensor_tensor(out=t1i, in0=p2i_f, in1=lcsr[:, L4], op=ALU.mult))
                    ve.tensor_tensor(out=t1r, in0=m1, in1=m2, op=ALU.subtract)
                    vsync(ve.tensor_tensor(out=m1, in0=m3, in1=t1i, op=ALU.add))
                    ve.tensor_tensor(out=m2, in0=t1r, in1=Ssel8[:, 0:4], op=ALU.add)
                    vsync(ve.tensor_tensor(out=m3, in0=m1, in1=Ssel8[:, 4:8], op=ALU.add))
                    ve.tensor_scalar(out=t1r, in0=m2, scalar1=P["mv"][:, 0:1],
                                     scalar2=None, op0=ALU.mult)
                    vsync(ve.tensor_scalar(out=t1i, in0=m3, scalar1=P["mv"][:, 0:1],
                                           scalar2=None, op0=ALU.mult))
                    ve.tensor_scalar(out=m1, in0=lcsr[:, L4], scalar1=P["mb"][:, 0:1],
                                     scalar2=None, op0=ALU.mult)
                    vsync(ve.tensor_scalar(out=m2, in0=lcsi[:, L4], scalar1=P["mb"][:, 0:1],
                                           scalar2=None, op0=ALU.mult))
                    ve.tensor_tensor(out=I_r, in0=m1, in1=t1r, op=ALU.add)
                    vsync(ve.tensor_tensor(out=I_i, in0=m2, in1=t1i, op=ALU.add))
                    vsync(ve.tensor_scalar(out=nI_i, in0=I_i, scalar1=-1.0, scalar2=None,
                                           op0=ALU.mult))
                    last = None
                    for c in range(C4):
                        ve.tensor_copy(out=Sinr4[:, c * NB:c * NB + 1], in_=I_r[:, c:c + 1])
                        last = ve.tensor_copy(out=Sini4[:, c * NB:c * NB + 1],
                                              in_=I_i[:, c:c + 1])
                    vsync(last)
                    for c in range(C4):
                        last = ve.scalar_tensor_tensor(
                            out=Sinr4[:, c * NB + 1:(c + 1) * NB],
                            in0=PBr[:, c * NB:(c + 1) * NB - 1], scalar=I_r[:, c:c + 1],
                            in1=S0r[:, c * NB:(c + 1) * NB - 1], op0=ALU.mult, op1=ALU.add)
                    vsync(last)
                    for c in range(C4):
                        last = ve.scalar_tensor_tensor(
                            out=Sinr4[:, c * NB + 1:(c + 1) * NB],
                            in0=PBi[:, c * NB:(c + 1) * NB - 1], scalar=nI_i[:, c:c + 1],
                            in1=Sinr4[:, c * NB + 1:(c + 1) * NB], op0=ALU.mult, op1=ALU.add)
                    vsync(last)
                    for c in range(C4):
                        last = ve.scalar_tensor_tensor(
                            out=Sini4[:, c * NB + 1:(c + 1) * NB],
                            in0=PBr[:, c * NB:(c + 1) * NB - 1], scalar=I_i[:, c:c + 1],
                            in1=S0i[:, c * NB:(c + 1) * NB - 1], op0=ALU.mult, op1=ALU.add)
                    vsync(last)
                    for c in range(C4):
                        last = ve.scalar_tensor_tensor(
                            out=Sini4[:, c * NB + 1:(c + 1) * NB],
                            in0=PBi[:, c * NB:(c + 1) * NB - 1], scalar=I_r[:, c:c + 1],
                            in1=Sini4[:, c * NB + 1:(c + 1) * NB], op0=ALU.mult, op1=ALU.add)
                    vsync(last)
                    if "cv0" in dbg_aps and l == 0:
                        cdbg = sDV.inc(ve.nop())
                        _w(sy, sDV, cdbg)
                        tv = sDBG.inc(sy.dma_start(out=dbg_aps["cv0"], in_=convr[:, 0:TOK]), 16)
                        _w(ve, sDBG, tv)
                    for c in range(C4):
                        ve.tensor_tensor(out=h[:, c * TOK:(c + 1) * TOK],
                                         in0=h[:, c * TOK:(c + 1) * TOK],
                                         in1=convr[:, c * TOK:(c + 1) * TOK],
                                         op=ALU.add)
                    for c in range(C4):
                        h3 = h[:, c * TOK:(c + 1) * TOK].rearrange("p (b q) -> p b q", q=Q)
                        for j in range(NB):
                            ve.scalar_tensor_tensor(
                                out=h3[:, j, :], in0=P1r[:, c * Q:(c + 1) * Q],
                                scalar=Sinr4[:, c * NB + j:c * NB + j + 1],
                                in1=h3[:, j, :], op0=ALU.mult, op1=ALU.add)
                    last = None
                    for c in range(C4):
                        h3 = h[:, c * TOK:(c + 1) * TOK].rearrange("p (b q) -> p b q", q=Q)
                        for j in range(NB):
                            last = ve.scalar_tensor_tensor(
                                out=h3[:, j, :], in0=nP1i[:, c * Q:(c + 1) * Q],
                                scalar=Sini4[:, c * NB + j:c * NB + j + 1],
                                in1=h3[:, j, :], op0=ALU.mult, op1=ALU.add)
                    vsync(last)
                    if "hp3" in dbg_aps and l == 0:
                        _w(sy, sDV, sDV.n)
                        tv = sDBG.inc(sy.dma_start(out=dbg_aps["hp3"], in_=h[:, :]), 16)
                        _w(ve, sDBG, tv)
                        _w(sc, sDBG, tv)
                        _w(te, sDBG, tv)

                    rstd2 = ln_stats()
                    y2_done = 0
                    for c in range(C4):
                        _w(ve, sSC, rstd2)
                        ve.tensor_tensor(out=y_c[:, :], in0=hsl(c, 0, TOK),
                                         in1=mu_all[:, :], op=ALU.subtract)
                        ve.tensor_tensor(out=y_c[:, :], in0=y_c[:, :],
                                         in1=rstd_all[:, :], op=ALU.mult)
                        y2_done = sDV.inc(ve.tensor_scalar(
                            out=y2in[:, c * TOK:(c + 1) * TOK], in0=y_c[:, :],
                            scalar1=lc("l2w", c), scalar2=lc("l2b", c),
                            op0=ALU.mult, op1=ALU.add))

                    _w(te, sW1, w1_ready[l])
                    _w(te, sW2, w2_ready[l])
                    _w(te, sDV, y2_done)
                    for t4 in range(4):
                        sil_last = 0
                        for fc in range(FC16):
                            bk = fc % 2
                            bank_wait(bk)
                            for kc in range(C4):
                                instr = te.matmul(
                                    pbank(bk),
                                    w1t[:, kc * DFF + fc * 128:kc * DFF + fc * 128 + 128],
                                    y2in[:, kc * TOK + t4 * 512:kc * TOK + (t4 + 1) * 512],
                                    start=(kc == 0), stop=(kc == C4 - 1))
                            w1v = sMM.inc(instr)
                            w1_lastmm[l] = w1v
                            _w(sc, sMM, w1v)
                            sil_last = sSC.inc(sc.activation(
                                out=y2buf[:, fc * 512:(fc + 1) * 512], in_=pbank(bk),
                                func=AF.Silu,
                                bias=P["b1"][:, l * FC16 + fc:l * FC16 + fc + 1]))
                            bank_free[bk] = (sSC, sil_last)
                        _w(te, sSC, sil_last)
                        for c in range(C4):
                            bk = 2 + c % 2
                            bank_wait(bk)
                            for fc in range(FC16):
                                instr = te.matmul(
                                    pbank(bk),
                                    w2t[:, fc * DIN + c * 128:fc * DIN + c * 128 + 128],
                                    y2buf[:, fc * 512:(fc + 1) * 512],
                                    start=(fc == 0), stop=(fc == FC16 - 1))
                            w2v = sMM.inc(instr)
                            w2_lastmm[l] = w2v
                            _w(ve, sMM, w2v)
                            hv = sDV.inc(ve.scalar_tensor_tensor(
                                out=hsl(c, t4 * 512, (t4 + 1) * 512), in0=pbank(bk),
                                scalar=lc("b2", c),
                                in1=hsl(c, t4 * 512, (t4 + 1) * 512),
                                op0=ALU.add, op1=ALU.add))
                            bank_free[bk] = (sDV, hv)
                    if "h_0" in dbg_aps and l == 0:
                        _w(sy, sDV, sDV.n)
                        tv = sDBG.inc(sy.dma_start(out=dbg_aps["h_0"], in_=h[:, :]), 16)
                        _w(ve, sDBG, tv)
                        _w(sc, sDBG, tv)
                        _w(te, sDBG, tv)

                if "h_f" in dbg_aps:
                    _w(sy, sDV, sDV.n)
                    tv = sDBG.inc(sy.dma_start(out=dbg_aps["h_f"], in_=h[:, :]), 16)
                    _w(ve, sDBG, tv)

                rstdf = ln_stats()
                hf_done = 0
                for c in range(C4):
                    _w(ve, sSC, rstdf)
                    ve.tensor_tensor(out=y_c[:, :], in0=hsl(c, 0, TOK), in1=mu_all[:, :],
                                     op=ALU.subtract)
                    ve.tensor_tensor(out=y_c[:, :], in0=y_c[:, :], in1=rstd_all[:, :],
                                     op=ALU.mult)
                    hf_done = sDV.inc(ve.tensor_scalar(
                        out=hf[:, c * TOK:(c + 1) * TOK], in0=y_c[:, :],
                        scalar1=P["lnlw"][:, c:c + 1], scalar2=P["lnlb"][:, c:c + 1],
                        op0=ALU.mult, op1=ALU.add))

                for eng in (ve, sc, te, gp, sy):
                    _w(eng, sDV, sDV.n)
                    _w(eng, sSC, sSC.n)
                    _w(eng, sMM, sMM.n)
                    _w(eng, sW1, sW1.n)
                    _w(eng, sW2, sW2.n)
                    _w(eng, sCCD, sCCD.n)

        with ExitStack() as cctx:
            cctx.enter_context(nc.named_scope("stageC"))
            csb = lambda n, shp, dt=F32: cctx.enter_context(nc.sbuf_tensor(n, shp, dt))
            hh = csb("hh", [128, IC * 512], BF16)
            boutb = csb("boutb", [1, V], BF16)
            bout_ready = sONE.inc(gp.dma_start(out=boutb[:, :], in_=bout_in,
                                               max_dma_last_dim=8192), 16)
            OV = csb("OV", [128, 18432], F32)
            OVb = OV[:, :].bitcast(BF16)
            xnTs = OVb[:, 0:16384]
            wgbuf = [OVb[:, 16384 + i * 4096:16384 + (i + 1) * 4096] for i in range(2)]
            wubuf = [OVb[:, 24576 + i * 4096:24576 + (i + 1) * 4096] for i in range(2)]
            gsil = [OV[:, 16384 + i * 512:16384 + (i + 1) * 512] for i in range(2)]
            wdbuf = [OVb[:, i * 4096:(i + 1) * 4096] for i in range(2)]
            wobuf = [OVb[:, 8192 + i * 4096:8192 + (i + 1) * 4096] for i in range(2)]
            xstg = OV[:, 8192:12288]
            ostg = OV[:, 12288:16384]
            dsb = [OV[:, 16384 + i * 512:16384 + (i + 1) * 512] for i in range(2)]
            sumb = [OV[:, 17408 + i * 512:17408 + (i + 1) * 512] for i in range(2)]

            x4 = x_in.rearrange("(s b p) (g c) -> p s g b c", p=128, b=4, c=1024)
            o4 = out_t.rearrange("(s b p) (g c) -> p s g b c", p=128, b=4, c=1024)

            gbank_free = [(sDV, 0)] * 2
            ubank_free = [(sDV, 0)] * 2
            gsil_free = [(sDV, 0)] * 2
            wg_cons = [(sMM, 0)] * 2
            wu_cons = [(sMM, 0)] * 2
            wo_cons = [(sMM, 0)] * 2
            ostg_free = (sOS[0], 0)
            xstg_free = (sDV, 0)
            prev_down_reads = 0
            prev_store = 0

            for s in range(4):
                _w(sy, sDV, prev_down_reads)
                _w(sy, sOS[0], sOS[0].n)
                _w(sy, sOS[1], sOS[1].n)
                _w(sy, sCVg, wg_ready)
                _w(sc, sOS[0], sOS[0].n)
                _w(sc, sOS[1], sOS[1].n)
                xnT_ld = sXNT.inc(sy.dma_start(
                    out=xnTs.rearrange("p (v t) -> p v t", t=512),
                    in_=xnt[:, :, s * 512:(s + 1) * 512]), 16)

                gate_end = 0
                for i in range(IC):
                    bi = i % 2
                    s_, v_ = wg_cons[bi]
                    _w(sy, s_, v_)
                    wg_ld = sWG[bi].inc(sy.dma_start(out=wgbuf[bi], in_=wg_bf[i]), 16)
                    s_, v_ = wu_cons[bi]
                    _w(sc, s_, v_)
                    if i < 2:
                        _w(sc, sCVu, wu_ready)
                    wu_ld = sWU[bi].inc(sc.dma_start(out=wubuf[bi], in_=wu_bf[i]), 16)
                    _w(te, sWG[bi], wg_ld)
                    if i < 2:
                        _w(te, sXNT, xnT_ld)
                    s_, v_ = gbank_free[bi]
                    _w(te, s_, v_)
                    for hc in range(VC32):
                        instr = te.matmul(pbank(bi), wgbuf[bi][:, hc * 128:(hc + 1) * 128],
                                          xnTs[:, hc * 512:(hc + 1) * 512],
                                          start=(hc == 0), stop=(hc == VC32 - 1))
                    g_done = sMM.inc(instr)
                    wg_cons[bi] = (sMM, g_done)
                    _w(te, sWU[bi], wu_ld)
                    s_, v_ = ubank_free[bi]
                    _w(te, s_, v_)
                    for hc in range(VC32):
                        instr = te.matmul(pbank(2 + bi),
                                          wubuf[bi][:, hc * 128:(hc + 1) * 128],
                                          xnTs[:, hc * 512:(hc + 1) * 512],
                                          start=(hc == 0), stop=(hc == VC32 - 1))
                    u_done = sMM.inc(instr)
                    wu_cons[bi] = (sMM, u_done)
                    gate_end = u_done
                    s_, v_ = gsil_free[bi]
                    _w(sc, s_, v_)
                    _w(sc, sMM, g_done)
                    sil_v = sSC.inc(sc.activation(out=gsil[bi], in_=pbank(bi),
                                                  func=AF.Silu,
                                                  scale=P["sg"][:, i:i + 1]))
                    gbank_free[bi] = (sSC, sil_v)
                    _w(ve, sMM, u_done)
                    _w(ve, sSC, sil_v)
                    hh_v = sDV.inc(ve.scalar_tensor_tensor(
                        out=hh[:, i * 512:(i + 1) * 512], in0=pbank(2 + bi),
                        scalar=P["su"][:, i:i + 1], in1=gsil[bi],
                        op0=ALU.mult, op1=ALU.mult))
                    ubank_free[bi] = (sDV, hh_v)
                    gsil_free[bi] = (sDV, hh_v)
                hh_done = sDV.n

                bank_free[0] = gbank_free[0]
                bank_free[1] = gbank_free[1]
                bank_free[2] = ubank_free[0]
                bank_free[3] = ubank_free[1]

                wd_cons = [(sMM, 0)] * 2
                per_hc_add = 0
                NWD = (IC + 3) // 4
                for hg in range(HG):
                    s_, v_ = xstg_free
                    _w(sc, s_, v_)
                    if hg == 0:
                        _w(sc, sMM, gate_end)
                    xs_ld = sXS.inc(sc.dma_start(
                        out=xstg.rearrange("p (b c) -> p b c", c=1024),
                        in_=x4[:, s, hg]), 16)
                    s_, v_ = wo_cons[hg % 2]
                    _w(gp, s_, v_)
                    _w(gp, sMM, gate_end)
                    if s == 0 and hg == 0:
                        _w(gp, sCVd, wd_ready)
                        _w(gp, sONE, bout_ready)
                    wo_ld = sWO[hg % 2].inc(gp.dma_start(
                        out=wobuf[hg % 2].rearrange("p (f c) -> p f c", c=1024),
                        in_=woT[hg].rearrange("f p c -> p f c"),
                        max_dma_last_dim=8192), 16)
                    down_done = 0
                    for ib in range(NWD):
                        bi = ib % 2
                        qe = sy if bi == 0 else gp
                        i0 = ib * 4
                        ni = min(4, IC - i0)
                        s_, v_ = wd_cons[bi]
                        _w(qe, s_, v_)
                        if ib < 2:
                            _w(qe, sMM, gate_end)
                        wd_ld = sWD[bi].inc(qe.dma_start(
                            out=wdbuf[bi][:, 0:ni * 1024].rearrange("p (i c) -> p i c", c=1024),
                            in_=wd_bf[hg, i0:i0 + ni].rearrange("i p c -> p i c")), 16)
                        _w(te, sWD[bi], wd_ld)
                        if ib == 0:
                            _w(te, sDV, hh_done)
                        for ii in range(ni):
                            i = i0 + ii
                            for hc in range(8):
                                if i == 0:
                                    bank_wait(hc)
                                instr = te.matmul(
                                    pbank(hc),
                                    wdbuf[bi][:, ii * 1024 + hc * 128:ii * 1024 + (hc + 1) * 128],
                                    hh[:, i * 512:(i + 1) * 512],
                                    start=(i == 0), stop=(i == IC - 1))
                        grp_v = sMM.inc(instr)
                        wd_cons[bi] = (sMM, grp_v)
                        down_done = grp_v
                    dv = [0] * 8
                    wo_done = [0] * 8
                    sm = [0] * 8
                    tp = [0] * 8

                    def ve_dsb(hc):
                        _w(ve, sMM, down_done)
                        dv[hc] = sDV.inc(ve.tensor_scalar(
                            out=dsb[hc % 2], in0=pbank(hc),
                            scalar1=P["sd"][:, hg * 8 + hc:hg * 8 + hc + 1],
                            scalar2=None, op0=ALU.mult))

                    def te_wout(hc):
                        _w(te, sDV, dv[hc])
                        if hc == 0:
                            _w(te, sWO[hg % 2], wo_ld)
                        for fc in range(C4):
                            te.matmul(pbank(hc),
                                      wobuf[hg % 2][:, fc * 1024 + hc * 128:fc * 1024 + hc * 128 + 128],
                                      hf[:, fc * TOK + s * 512:fc * TOK + (s + 1) * 512],
                                      start=(fc == 0), stop=False)
                        wo_done[hc] = sMM.inc(te.matmul(
                            pbank(hc),
                            boutb[0:1, hg * 1024 + hc * 128:hg * 1024 + hc * 128 + 128],
                            onesb[0:1, :], start=False, stop=True))

                    def ve_sum(hc):
                        _w(ve, sMM, wo_done[hc])
                        if hc >= 2:
                            _w(ve, sMM, tp[hc - 2])
                        sm[hc] = sDV.inc(ve.tensor_tensor(out=sumb[hc % 2], in0=pbank(hc),
                                                          in1=dsb[hc % 2], op=ALU.add))

                    def te_tp(hc):
                        _w(te, sDV, sm[hc])
                        for tb in range(4):
                            instr = te.transpose(pbank(hc)[:, tb * 128:(tb + 1) * 128],
                                                 sumb[hc % 2][:, tb * 128:(tb + 1) * 128],
                                                 identf[:, :])
                        tp[hc] = sMM.inc(instr)

                    def ve_add(hc):
                        nonlocal per_hc_add
                        _w(ve, sMM, tp[hc])
                        _w(ve, sXS, xs_ld)
                        if hc == 0:
                            s_, v_ = ostg_free
                            _w(ve, s_, v_)
                        for tb in range(4):
                            per_hc_add = sDV.inc(ve.tensor_tensor(
                                out=ostg[:, tb * 1024 + hc * 128:tb * 1024 + hc * 128 + 128],
                                in0=pbank(hc)[:, tb * 128:(tb + 1) * 128],
                                in1=xstg[:, tb * 1024 + hc * 128:tb * 1024 + hc * 128 + 128],
                                op=ALU.add))
                        bank_free[hc] = (sDV, per_hc_add)

                    ve_dsb(0)
                    ve_dsb(1)
                    te_wout(0)
                    ve_sum(0)
                    te_tp(0)
                    for hc in range(1, 8):
                        te_wout(hc)
                        if hc + 1 < 8:
                            ve_dsb(hc + 1)
                        ve_sum(hc)
                        te_tp(hc)
                        ve_add(hc - 1)
                    ve_add(7)
                    wo_cons[hg % 2] = (sMM, tp[7])
                    _w(sc, sDV, per_hc_add)
                    prev_store = sOS[hg % 2].inc(sc.dma_start(
                        out=o4[:, s, hg],
                        in_=ostg.rearrange("p (b c) -> p b c", c=1024)), 16)
                    ostg_free = (sOS[hg % 2], prev_store)
                    xstg_free = (sDV, per_hc_add)
                prev_down_reads = per_hc_add
                gbank_free = [bank_free[0], bank_free[1]]
                ubank_free = [bank_free[2], bank_free[3]]

            _w(sy, sOS[0], sOS[0].n)
            _w(sy, sOS[1], sOS[1].n)
            _w(sy, sDBG, sDBG.n)

    return nc, dbg_aps


def host_inputs(L0_attn_x, L0_attn_Win, L0_attn_bin, L0_attn_ln1_w, L0_attn_ln1_b,
                L0_attn_phz_re, L0_attn_phz_im, L0_attn_lc_re, L0_attn_lc_im,
                L0_attn_sc_lin, L0_attn_ln2_w, L0_attn_ln2_b, L0_attn_W1, L0_attn_b1,
                L0_attn_W2, L0_attn_b2, L0_attn_lnl_w, L0_attn_lnl_b, L0_attn_Wout,
                L0_attn_bout, L0_ffn_w_gate, L0_ffn_s_gate, L0_ffn_w_up, L0_ffn_s_up,
                L0_ffn_w_down, L0_ffn_s_down):
    f32 = np.float32
    cc = np.ascontiguousarray

    def pc(v, k):
        return cc(np.asarray(v, f32).reshape(k, 128).T)

    def pc_l(v):
        return cc(np.asarray(v, f32).reshape(NLAY, 4, 128).transpose(2, 0, 1)
                  .reshape(128, 16))

    winT_h = cc(np.asarray(L0_attn_Win, f32).T.reshape(VC32, 128, DIN)
                .transpose(1, 0, 2).reshape(128, VC32 * DIN))
    w1T_h = cc(np.asarray(L0_attn_W1, f32).transpose(0, 2, 1)
               .reshape(NLAY, C4, 128, DFF).transpose(0, 2, 1, 3)
               .reshape(NLAY, 128, C4 * DFF))
    w2T_h = cc(np.asarray(L0_attn_W2, f32).transpose(0, 2, 1)
               .reshape(NLAY, FC16, 128, DIN).transpose(0, 2, 1, 3)
               .reshape(NLAY, 128, FC16 * DIN))
    woT_h = cc(np.asarray(L0_attn_Wout, f32).T.reshape(C4, 128, HG, 1024)
               .transpose(2, 0, 1, 3))
    wg_h = cc(np.asarray(L0_ffn_w_gate).T.reshape(VC32, 128, IC, 128)
              .transpose(2, 1, 0, 3).reshape(IC, 128, 4096).astype(np.int32))
    wu_h = cc(np.asarray(L0_ffn_w_up).T.reshape(VC32, 128, IC, 128)
              .transpose(2, 1, 0, 3).reshape(IC, 128, 4096).astype(np.int32))
    wd_h = cc(np.asarray(L0_ffn_w_down).T.reshape(IC, 128, HG, 1024)
              .transpose(2, 0, 1, 3).astype(np.int32))

    base = dict(
        winT=winT_h, w1T=w1T_h, w2T=w2T_h, woT=woT_h, wg_i=wg_h, wu_i=wu_h, wd_i=wd_h,
        p_bin=pc(L0_attn_bin, C4), p_l1w=pc_l(L0_attn_ln1_w), p_l1b=pc_l(L0_attn_ln1_b),
        p_l2w=pc_l(L0_attn_ln2_w), p_l2b=pc_l(L0_attn_ln2_b), p_scl=pc_l(L0_attn_sc_lin),
        p_phr=pc_l(L0_attn_phz_re), p_phi=pc_l(L0_attn_phz_im),
        p_lcr=pc_l(L0_attn_lc_re), p_lci=pc_l(L0_attn_lc_im),
        p_b2=pc_l(L0_attn_b2),
        p_lnlw=pc(L0_attn_lnl_w, C4), p_lnlb=pc(L0_attn_lnl_b, C4),
        p_b1=cc(np.asarray(L0_attn_b1, f32).reshape(NLAY, FC16, 128)
                .transpose(2, 0, 1).reshape(128, NLAY * FC16)),
        p_sg=pc(L0_ffn_s_gate, IC), p_su=pc(L0_ffn_s_up, IC), p_sd=pc(L0_ffn_s_down, 32),
        p_eps=np.full((128, 1), EPS, f32),
        ident_b=np.eye(128, dtype=ml_dtypes.bfloat16),
        ident_f=np.eye(128, dtype=f32),
        om=np.full((128, 128), 1.0 / DIN, f32),
        bout=cc(np.asarray(L0_attn_bout, f32).reshape(1, V)),
        onesr=np.ones((1, 512), ml_dtypes.bfloat16),
    )
    x = np.asarray(L0_attn_x, f32)
    in_maps = []
    for core in range(NCORES):
        b, half = core // 2, core % 2
        m = dict(base)
        m["x"] = cc(x[b, half * TOK:(half + 1) * TOK, :])
        m["p_mv"] = np.full((128, 1), float(half), f32)
        m["p_mb"] = np.full((128, 1), 1.0 - float(half), f32)
        selv = np.zeros((8, 1), f32)
        selv[core - 1 if half == 1 else core] = 1.0
        m["sel"] = selv
        in_maps.append(m)
    return in_maps


_NC_CACHE = {}


def kernel(**inputs):
    if "main" not in _NC_CACHE:
        _NC_CACHE["main"] = build()
    nc, _ = _NC_CACHE["main"]
    in_maps = host_inputs(**inputs)
    res = run_bass_kernel_spmd(nc, in_maps, core_ids=list(range(NCORES)))
    out = np.empty((B, L, V), np.float32)
    for core in range(NCORES):
        b, half = core // 2, core % 2
        out[b, half * TOK:(half + 1) * TOK, :] = res.results[core]["out"]
    return out


# version 17 (final)
# speedup vs baseline: 1.0383x; 1.0183x over previous
import numpy as np
import ml_dtypes
from contextlib import ExitStack

import concourse.bass as bass
import concourse.mybir as mybir
from concourse.bass_utils import run_bass_kernel_spmd

F32 = mybir.dt.float32
BF16 = mybir.dt.bfloat16
I32 = mybir.dt.int32
AF = mybir.ActivationFunctionType
ALU = mybir.AluOpType

B, L, V = 4, 4096, 4096
TOK = 2048
DIN, DFF, ITR = 512, 2048, 11008
NLAY = 4
Q, NB = 128, 16
C4 = DIN // 128
FC16 = DFF // 128
VC32 = V // 128
IC = ITR // 128
HG = 4
EPS = 1e-5
NCORES = 8


class Sem:
    def __init__(self, nc, ctx, name):
        self.h = ctx.enter_context(nc.semaphore(name))
        self.n = 0

    def inc(self, instr, k=1):
        instr.then_inc(self.h, k)
        self.n += k
        return self.n


def _w(eng, sem, val):
    if val > 0:
        eng.wait_ge(sem.h, val)


def build(dbg=()):
    nc = bass.Bass("TRN2", target_bir_lowering=False, debug=False)
    ap = lambda t: t.ap()

    x_in = ap(nc.dram_tensor("x", [TOK, V], F32, kind="ExternalInput"))
    winT = ap(nc.dram_tensor("winT", [128, VC32 * DIN], F32, kind="ExternalInput"))
    w1T = ap(nc.dram_tensor("w1T", [NLAY, 128, C4 * DFF], F32, kind="ExternalInput"))
    w2T = ap(nc.dram_tensor("w2T", [NLAY, 128, FC16 * DIN], F32, kind="ExternalInput"))
    woT = ap(nc.dram_tensor("woT", [HG, C4, 128, 1024], F32, kind="ExternalInput"))
    wg_i = ap(nc.dram_tensor("wg_i", [IC, 128, 4096], I32, kind="ExternalInput"))
    wu_i = ap(nc.dram_tensor("wu_i", [IC, 128, 4096], I32, kind="ExternalInput"))
    wd_i = ap(nc.dram_tensor("wd_i", [HG, IC, 128, 1024], I32, kind="ExternalInput"))
    pv = {}
    for name, k in [("bin", C4), ("l1w", 16), ("l1b", 16), ("l2w", 16), ("l2b", 16),
                    ("scl", 16), ("phr", 16), ("phi", 16), ("lcr", 16), ("lci", 16),
                    ("b2", 16), ("lnlw", C4), ("lnlb", C4), ("b1", NLAY * FC16),
                    ("sg", IC), ("su", IC), ("sd", 32), ("mv", 1), ("mb", 1), ("eps", 1)]:
        pv[name] = ap(nc.dram_tensor("p_" + name, [128, k], F32, kind="ExternalInput"))
    ident_b = ap(nc.dram_tensor("ident_b", [128, 128], BF16, kind="ExternalInput"))
    ident_f = ap(nc.dram_tensor("ident_f", [128, 128], F32, kind="ExternalInput"))
    om_in = ap(nc.dram_tensor("om", [128, 128], F32, kind="ExternalInput"))
    sel_in = ap(nc.dram_tensor("sel", [8, 1], F32, kind="ExternalInput"))
    bout_in = ap(nc.dram_tensor("bout", [1, V], F32, kind="ExternalInput"))
    ones_in = ap(nc.dram_tensor("onesr", [1, 512], BF16, kind="ExternalInput"))
    out_t = ap(nc.dram_tensor("out", [TOK, V], F32, kind="ExternalOutput"))

    wg_bf = ap(nc.dram_tensor("wg_bf", [IC, 128, 4096], BF16))
    wu_bf = ap(nc.dram_tensor("wu_bf", [IC, 128, 4096], BF16))
    wd_bf = ap(nc.dram_tensor("wd_bf", [HG, IC, 128, 1024], BF16))
    xnt = nc.dram_tensor("xnt", [VC32, 128, TOK], BF16).ap().rearrange("v p t -> p v t")
    cc_in = ap(nc.dram_tensor("cc_in", [1, 1024], F32))
    cc_out = ap(nc.dram_tensor("cc_out", [8, 1024], F32))

    dbg_aps = {}
    for name in dbg:
        shp = {"h_a": [128, C4 * TOK], "h_f": [128, C4 * TOK], "xn0": [128, V],
               "y0": [128, TOK], "cv0": [128, TOK], "pc": [128, 32],
               "st0": [128, 16], "hp3": [128, C4 * TOK],
               "h_0": [128, C4 * TOK]}[name]
        dt = {"xn0": BF16}.get(name, F32)
        dbg_aps[name] = ap(nc.dram_tensor("dbg_" + name, shp, dt, kind="ExternalOutput"))

    with ExitStack() as ctx:
        sCC = Sem(nc, ctx, "sCC")
        sMM = Sem(nc, ctx, "sMM")
        sDV = Sem(nc, ctx, "sDV")
        sSC = Sem(nc, ctx, "sSC")
        sPRM = Sem(nc, ctx, "sPRM")
        sCVg = Sem(nc, ctx, "sCVg")
        sCVu = Sem(nc, ctx, "sCVu")
        sCVd = Sem(nc, ctx, "sCVd")
        sONE = Sem(nc, ctx, "sONE")
        sW1 = Sem(nc, ctx, "sW1")
        sW2 = Sem(nc, ctx, "sW2")
        sCCD = Sem(nc, ctx, "sCCD")
        sX = [Sem(nc, ctx, "sX0"), Sem(nc, ctx, "sX1")]
        sSL = [Sem(nc, ctx, "sSL0"), Sem(nc, ctx, "sSL1")]
        sXNT = Sem(nc, ctx, "sXNT")
        sWG = [Sem(nc, ctx, "sWG0"), Sem(nc, ctx, "sWG1")]
        sWU = [Sem(nc, ctx, "sWU0"), Sem(nc, ctx, "sWU1")]
        sWD = [Sem(nc, ctx, "sWDa"), Sem(nc, ctx, "sWDb")]
        sWO = [Sem(nc, ctx, "sWO0"), Sem(nc, ctx, "sWO1")]
        sXS = Sem(nc, ctx, "sXS")
        sOS = [Sem(nc, ctx, "sOS0"), Sem(nc, ctx, "sOS1")]
        sDBG = Sem(nc, ctx, "sDBG")
        sGP = Sem(nc, ctx, "sGP")
        sy, sc, ve, te, gp = nc.sync, nc.scalar, nc.vector, nc.tensor, nc.gpsimd

        sb = lambda n, shp, dt=F32: ctx.enter_context(nc.sbuf_tensor(n, shp, dt))
        P = {k: sb("sp_" + k, [128, v.shape[-1]]) for k, v in pv.items()}
        identb = sb("identb", [128, 128], BF16)
        identf = sb("identf", [128, 128], F32)
        omt = sb("omt", [128, 128], F32)
        selsb = sb("selsb", [8, 1], F32)
        onesb = sb("onesb", [1, 512], BF16)
        hf = sb("hf", [128, C4 * TOK], BF16)
        lcsr = sb("lcsr", [128, 16], F32)
        lcsi = sb("lcsi", [128, 16], F32)
        pcr = sb("pcr", [128, 16], F32)
        pci = sb("pci", [128, 16], F32)
        tmp16a = sb("tmp16a", [128, 16], F32)
        tmp16b = sb("tmp16b", [128, 16], F32)
        tmp16c = sb("tmp16c", [128, 16], F32)
        tmp16d = sb("tmp16d", [128, 16], F32)
        tmp16e = sb("tmp16e", [128, 16], F32)

        ps = ctx.enter_context(nc.psum_tensor("ps", [128, 8 * 512], F32))
        pbank = lambda b: ps[:, b * 512:(b + 1) * 512]
        pbank_b16 = lambda b: ps[:, :].bitcast(BF16)[:, b * 1024:(b + 1) * 1024]
        bank_free = [(sDV, 0)] * 8

        def bank_wait(b):
            s_, v_ = bank_free[b]
            _w(te, s_, v_)

        for k in P:
            sPRM.inc(sy.dma_start(out=P[k][:, :], in_=pv[k]), 16)
        sPRM.inc(sy.dma_start(out=identb[:, :], in_=ident_b), 16)
        sPRM.inc(sy.dma_start(out=identf[:, :], in_=ident_f), 16)
        sPRM.inc(sy.dma_start(out=omt[:, :], in_=om_in), 16)
        sPRM.inc(sy.dma_start(out=selsb[:, :], in_=sel_in), 16)
        params_ready = sPRM.inc(sy.dma_start(out=onesb[:, :], in_=ones_in), 16)

        CHUNK = 4 * 1024 * 1024

        def conv_chunks(sem, dst, src, n_elem, c0=0, c1=10**9):
            dims = len(src.shape)
            names = " ".join(f"a{i}" for i in range(dims))
            sflat = src.rearrange(f"{names} -> ({names})")
            dflat = dst.rearrange(f"{names} -> ({names})")
            v = 0
            ncall = (n_elem + CHUNK - 1) // CHUNK
            for ci in range(max(0, c0), min(ncall, c1)):
                o = ci * CHUNK
                e = min(n_elem, o + CHUNK)
                v = sem.inc(gp.dma_start(out=dflat[o:e], in_=sflat[o:e],
                                         max_dma_last_dim=8192), 16)
            return v

        NGE = IC * 128 * 4096
        wg_ready = wu_ready = wd_ready = 0


        def vfence(instr):
            v = sDV.inc(instr)
            ve.wait_ge(sDV.h, v)
            return v

        def sfence(instr):
            v = sSC.inc(instr)
            sc.wait_ge(sSC.h, v)
            return v

        _w(ve, sPRM, params_ready)
        ve.tensor_tensor(out=tmp16a[:, :], in0=P["phr"][:, :], in1=P["phr"][:, :], op=ALU.mult)
        vfence(ve.tensor_tensor(out=tmp16b[:, :], in0=P["phi"][:, :], in1=P["phi"][:, :], op=ALU.mult))
        r2_done = sDV.inc(ve.tensor_tensor(out=tmp16c[:, :], in0=tmp16a[:, :],
                                           in1=tmp16b[:, :], op=ALU.add))
        _w(sc, sDV, r2_done)
        sfence(sc.activation(out=tmp16a[:, :], in_=tmp16c[:, :], func=AF.Ln))
        sfence(sc.activation(out=tmp16b[:, :], in_=tmp16a[:, :], func=AF.Exp, scale=0.5))
        sc.activation(out=tmp16d[:, :], in_=tmp16a[:, :], func=AF.Exp, scale=-0.5)
        mag_done = sSC.inc(sc.activation(out=tmp16e[:, :], in_=tmp16b[:, :], func=AF.Tanh))
        _w(ve, sSC, mag_done)
        vfence(ve.tensor_tensor(out=tmp16c[:, :], in0=tmp16e[:, :], in1=tmp16d[:, :], op=ALU.mult))
        ve.tensor_tensor(out=pcr[:, :], in0=P["phr"][:, :], in1=tmp16c[:, :], op=ALU.mult)
        ve.tensor_tensor(out=pci[:, :], in0=P["phi"][:, :], in1=tmp16c[:, :], op=ALU.mult)
        ve.tensor_tensor(out=lcsr[:, :], in0=P["lcr"][:, :], in1=P["scl"][:, :], op=ALU.mult)
        pcv = sDV.inc(ve.tensor_tensor(out=lcsi[:, :], in0=P["lci"][:, :], in1=P["scl"][:, :], op=ALU.mult))
        if "pc" in dbg_aps:
            _w(sy, sDV, pcv)
            sDBG.inc(sy.dma_start(out=dbg_aps["pc"][:, 0:16], in_=pcr[:, :]), 16)
            sDBG.inc(sy.dma_start(out=dbg_aps["pc"][:, 16:32], in_=pci[:, :]), 16)

        with ExitStack() as abctx:
            absb = lambda n, shp, dt=F32: abctx.enter_context(nc.sbuf_tensor(n, shp, dt))
            h = absb("h", [128, C4 * TOK], F32)
            hsl = lambda c, a, b_: h[:, c * TOK + a:c * TOK + b_]

            with ExitStack() as actx:
                actx.enter_context(nc.named_scope("stageA"))
                asb = lambda n, shp, dt=F32: actx.enter_context(nc.sbuf_tensor(n, shp, dt))
                xbuf = [asb("xa", [128, V]), asb("xb", [128, V])]
                xnb = [asb("xna", [128, V], BF16), asb("xnb2", [128, V], BF16)]
                stg = [asb("stga", [128, VC32 * 512], BF16),
                       asb("stgb", [128, VC32 * 512], BF16)]
                winTb = asb("winTb", [128, VC32 * DIN], BF16)
                stat = asb("stat", [128, 16], F32)
                junkf = asb("junkf", [128, V // 2], F32)
                winT_ready = sONE.inc(gp.dma_start(out=winTb[:, :], in_=winT,
                                                   max_dma_last_dim=8192), 16)
                wg_ready = conv_chunks(sCVg, wg_bf, wg_i, NGE)
                _w(te, sPRM, params_ready)

                xbuf_free = [(sDV, 0), (sDV, 0)]
                xnb_free = [(sMM, 0), (sMM, 0)]
                junk_free = (sDV, 0)
                stg_free = [(sMM, 0), (sMM, 0)]
                stg_wr = [0, 0]
                tp_bank_free = [(sDV, 0), (sDV, 0)]
                stg_evac = [0, 0]
                h_a_done = 0

                for tc in range(16):
                    bi = tc % 2
                    sl = tc // 4
                    sbl = sl % 2
                    st_off = (tc % 4) * 128
                    s_, v_ = xbuf_free[bi]
                    _w(sy, s_, v_)
                    x_ld = sX[bi].inc(sy.dma_start(out=xbuf[bi][:, :],
                                                   in_=x_in[tc * 128:(tc + 1) * 128, :]), 16)
                    _w(ve, sX[bi], x_ld)
                    _w(ve, sPRM, params_ready)
                    ve.tensor_reduce(out=stat[:, 8 * bi + 0:8 * bi + 1],
                                     in_=xbuf[bi][:, :], axis=mybir.AxisListType.X,
                                     op=ALU.add)
                    _w(sc, sX[bi], x_ld)
                    _w(sc, sPRM, params_ready)
                    s_, v_ = junk_free
                    _w(sc, s_, v_)
                    sqA = sSC.inc(sc.activation(out=junkf[:, :],
                                                in_=xbuf[bi][:, 0:V // 2],
                                                func=AF.Square))
                    _w(ve, sSC, sqA)
                    rA = sDV.inc(ve.tensor_reduce(
                        out=stat[:, 8 * bi + 1:8 * bi + 2], in_=junkf[:, :],
                        axis=mybir.AxisListType.X, op=ALU.add))
                    _w(sc, sDV, rA)
                    sqB = sSC.inc(sc.activation(out=junkf[:, :],
                                                in_=xbuf[bi][:, V // 2:V],
                                                func=AF.Square))
                    _w(ve, sSC, sqB)
                    red2 = sDV.inc(ve.tensor_reduce(
                        out=stat[:, 8 * bi + 3:8 * bi + 4], in_=junkf[:, :],
                        axis=mybir.AxisListType.X, op=ALU.add))
                    junk_free = (sDV, red2)
                    ve.wait_ge(sDV.h, red2)
                    ve.tensor_tensor(out=stat[:, 8 * bi + 6:8 * bi + 7],
                                     in0=stat[:, 8 * bi + 1:8 * bi + 2],
                                     in1=stat[:, 8 * bi + 3:8 * bi + 4], op=ALU.add)
                    w2_ = sDV.inc(ve.tensor_scalar(out=stat[:, 8 * bi + 2:8 * bi + 3],
                                                   in0=stat[:, 8 * bi + 0:8 * bi + 1],
                                                   scalar1=1.0 / V, scalar2=None,
                                                   op0=ALU.mult))
                    ve.wait_ge(sDV.h, w2_)
                    ve.tensor_scalar(out=stat[:, 8 * bi + 3:8 * bi + 4],
                                     in0=stat[:, 8 * bi + 6:8 * bi + 7],
                                     scalar1=1.0 / V, scalar2=None, op0=ALU.mult)
                    w4_ = sDV.inc(ve.tensor_tensor(out=stat[:, 8 * bi + 4:8 * bi + 5],
                                                   in0=stat[:, 8 * bi + 2:8 * bi + 3],
                                                   in1=stat[:, 8 * bi + 2:8 * bi + 3],
                                                   op=ALU.mult))
                    ve.wait_ge(sDV.h, w4_)
                    var_done = sDV.inc(ve.tensor_tensor(
                        out=stat[:, 8 * bi + 5:8 * bi + 6],
                        in0=stat[:, 8 * bi + 3:8 * bi + 4],
                        in1=stat[:, 8 * bi + 4:8 * bi + 5], op=ALU.subtract))
                    _w(sc, sDV, var_done)
                    lnv = sSC.inc(sc.activation(out=stat[:, 8 * bi + 6:8 * bi + 7],
                                                in_=stat[:, 8 * bi + 5:8 * bi + 6],
                                                func=AF.Ln, bias=P["eps"][:, 0:1]))
                    sc.wait_ge(sSC.h, lnv)
                    rstd_done = sSC.inc(sc.activation(
                        out=stat[:, 8 * bi + 7:8 * bi + 8],
                        in_=stat[:, 8 * bi + 6:8 * bi + 7], func=AF.Exp, scale=-0.5))
                    s_, v_ = xnb_free[bi]
                    _w(ve, s_, v_)
                    _w(ve, sSC, rstd_done)
                    xn_done = sDV.inc(ve.tensor_scalar(
                        out=xnb[bi][:, :], in0=xbuf[bi][:, :],
                        scalar1=stat[:, 8 * bi + 2:8 * bi + 3],
                        scalar2=stat[:, 8 * bi + 7:8 * bi + 8],
                        op0=ALU.subtract, op1=ALU.mult))
                    xbuf_free[bi] = (sDV, xn_done)
                    if "xn0" in dbg_aps and tc == 0:
                        _w(sy, sDV, xn_done)
                        tv = sDBG.inc(sy.dma_start(out=dbg_aps["xn0"], in_=xnb[bi][:, :]), 16)
                        _w(ve, sDBG, tv)
                        _w(sc, sDBG, tv)
                    if "st0" in dbg_aps and tc == 0:
                        _w(sy, sDV, xn_done)
                        tv = sDBG.inc(sy.dma_start(out=dbg_aps["st0"], in_=stat[:, :]), 16)
                        _w(ve, sDBG, tv)
                        _w(sc, sDBG, tv)
                    last_tp = 0
                    for g in range(4):
                        tb = g % 2
                        s_, v_ = tp_bank_free[tb]
                        _w(te, s_, v_)
                        _w(te, sDV, xn_done)
                        for k in range(8):
                            vc = g * 8 + k
                            instr = te.transpose(pbank_b16(4 + tb)[:, k * 128:(k + 1) * 128],
                                                 xnb[bi][:, vc * 128:(vc + 1) * 128],
                                                 identb[:, :])
                        tp_done = sMM.inc(instr)
                        last_tp = tp_done
                        if g == 0:
                            s_, v_ = stg_free[sbl]
                            _w(ve, s_, v_)
                            _w(ve, sSL[sbl], stg_wr[sbl])
                        _w(ve, sMM, tp_done)
                        dst = stg[sbl][:, :].rearrange("p (v t) -> p v t", t=512)
                        src = pbank_b16(4 + tb)[:, :].rearrange("p (k t) -> p k t", t=128)
                        ev = sDV.inc(ve.tensor_copy(
                            out=dst[:, g * 8:(g + 1) * 8, st_off:st_off + 128], in_=src))
                        tp_bank_free[tb] = (sDV, ev)
                        stg_evac[sbl] = ev
                    xnb_free[bi] = (sMM, last_tp)

                    if tc % 4 == 3:
                        _w(sy, sDV, stg_evac[sbl])
                        stg_wr[sbl] = sSL[sbl].inc(sy.dma_start(
                            out=xnt[:, :, sl * 512:(sl + 1) * 512],
                            in_=stg[sbl][:, :].rearrange("p (v t) -> p v t", t=512)), 16)
                        _w(te, sONE, winT_ready)
                        _w(te, sDV, stg_evac[sbl])
                        win_done = [0] * C4
                        for mc in range(C4):
                            bank_wait(mc)
                            for vc in range(VC32):
                                instr = te.matmul(
                                    pbank(mc),
                                    winTb[:, vc * DIN + mc * 128:vc * DIN + mc * 128 + 128],
                                    stg[sbl][:, vc * 512:(vc + 1) * 512],
                                    start=(vc == 0), stop=(vc == VC32 - 1))
                            win_done[mc] = sMM.inc(instr)
                        for mc in range(C4):
                            _w(ve, sMM, win_done[mc])
                            h_a_done = sDV.inc(ve.tensor_scalar(
                                out=h[:, mc * TOK + sl * 512:mc * TOK + (sl + 1) * 512],
                                in0=pbank(mc), scalar1=P["bin"][:, mc:mc + 1],
                                scalar2=None, op0=ALU.add))
                            bank_free[mc] = (sDV, h_a_done)
                        stg_free[sbl] = (sMM, win_done[C4 - 1])

                if "h_a" in dbg_aps:
                    _w(sy, sDV, h_a_done)
                    tv = sDBG.inc(sy.dma_start(out=dbg_aps["h_a"], in_=h[:, :]), 16)
                    _w(ve, sDBG, tv)
                for eng in (ve, sc, te, gp, sy):
                    _w(eng, sDV, sDV.n)
                    _w(eng, sSC, sSC.n)
                    _w(eng, sMM, sMM.n)
                    for fam in (sX[0], sX[1], sSL[0], sSL[1]):
                        _w(eng, fam, fam.n)
                    _w(eng, sPRM, sPRM.n)
                    _w(eng, sONE, winT_ready)

            with ExitStack() as bctx:
                bctx.enter_context(nc.named_scope("stageB"))
                bsb = lambda n, shp, dt=F32: bctx.enter_context(nc.sbuf_tensor(n, shp, dt))
                zA = bsb("zA", [128, 2 * TOK])
                zB = bsb("zB", [128, 2 * TOK])
                zAr, zAi = zA[:, 0:TOK], zA[:, TOK:2 * TOK]
                zBr, zBi = zB[:, 0:TOK], zB[:, TOK:2 * TOK]
                y2buf = zA[:, :].bitcast(BF16)
                convr = bsb("convr", [128, C4 * TOK])
                y2in = convr[:, :].bitcast(BF16)[:, 0:C4 * TOK]
                y_c = bsb("y_c", [128, TOK])
                mu_all = bsb("mu_all", [128, TOK])
                rstd_all = bsb("rstd_all", [128, TOK])
                sq4 = bsb("sq4", [128, 4 * 512])
                var4 = sq4
                m2b4 = bsb("m2b4", [128, 4 * 512])
                w1t = bsb("w1t", [128, C4 * DFF], BF16)
                w2t = bsb("w2t", [128, FC16 * DIN], BF16)
                P1r = bsb("P1r", [128, C4 * Q])
                P1i = bsb("P1i", [128, C4 * Q])
                nP1i = bsb("nP1i", [128, C4 * Q])
                PBr = bsb("PBr", [128, C4 * NB])
                PBi = bsb("PBi", [128, C4 * NB])
                S0r = bsb("S0r", [128, C4 * NB])
                S0i = bsb("S0i", [128, C4 * NB])
                E2r = bsb("E2r", [128, NB]); E2i = bsb("E2i", [128, NB])
                T2r = bsb("T2r", [128, NB]); T2i = bsb("T2i", [128, NB])
                tA = bsb("tA", [128, 512]); tB = bsb("tB", [128, 512])
                Sinr4 = bsb("Sinr4", [128, C4 * NB])
                Sini4 = bsb("Sini4", [128, C4 * NB])
                ccrow = m2b4[0:1, 0:1024]
                Gsb = m2b4[0:8, 1024:2048]
                Ssel8 = bsb("Ssel8", [128, 8])
                Ic2 = bsb("Ic2", [128, 32])

                w1_ready = [0] * NLAY
                w2_ready = [0] * NLAY
                w1_lastmm = [0] * NLAY
                w2_lastmm = [0] * NLAY
                tI1 = w1t[:, :].bitcast(F32)[:, 0:TOK]
                tI2 = w1t[:, :].bitcast(F32)[:, TOK:2 * TOK]
                wu_ready = conv_chunks(sCVu, wu_bf, wu_i, NGE, 0, 5)

                def ln_stats():
                    _w(te, sDV, sDV.n)
                    _w(sc, sDV, sDV.n)
                    sq_done = [0] * C4
                    prev_grp = 0
                    for c in range(C4):
                        _w(sc, sMM, prev_grp)
                        for t4 in range(4):
                            sq_done[c] = sSC.inc(sc.activation(
                                out=sq4[:, t4 * 512:(t4 + 1) * 512],
                                in_=hsl(c, t4 * 512, (t4 + 1) * 512), func=AF.Square))
                        for t4 in range(4):
                            if c == 0:
                                bank_wait(t4)
                            te.matmul(pbank(t4), omt[:, :],
                                      hsl(c, t4 * 512, (t4 + 1) * 512),
                                      start=(c == 0), stop=(c == C4 - 1))
                        _w(te, sSC, sq_done[c])
                        for t4 in range(4):
                            if c == 0:
                                bank_wait(4 + t4)
                            instr = te.matmul(pbank(4 + t4), omt[:, :],
                                              sq4[:, t4 * 512:(t4 + 1) * 512],
                                              start=(c == 0), stop=(c == C4 - 1))
                        prev_grp = sMM.inc(instr)
                    mm_done = prev_grp
                    _w(sc, sMM, mm_done)
                    muc = 0
                    for t4 in range(4):
                        muc = sSC.inc(sc.copy(out=mu_all[:, t4 * 512:(t4 + 1) * 512],
                                              in_=pbank(t4)))
                    _w(ve, sSC, muc)
                    _w(ve, sMM, mm_done)
                    var_v = 0
                    for t4 in range(4):
                        ve.tensor_tensor(out=m2b4[:, t4 * 512:(t4 + 1) * 512],
                                         in0=mu_all[:, t4 * 512:(t4 + 1) * 512],
                                         in1=mu_all[:, t4 * 512:(t4 + 1) * 512],
                                         op=ALU.mult)
                        var_v = sDV.inc(ve.tensor_tensor(
                            out=var4[:, t4 * 512:(t4 + 1) * 512], in0=pbank(4 + t4),
                            in1=m2b4[:, t4 * 512:(t4 + 1) * 512], op=ALU.subtract))
                        bank_free[t4] = (sSC, muc)
                        bank_free[4 + t4] = (sDV, var_v)
                    _w(sc, sDV, var_v)
                    rstd_v = 0
                    for t4 in range(4):
                        sc.activation(out=m2b4[:, t4 * 512:(t4 + 1) * 512],
                                      in_=var4[:, t4 * 512:(t4 + 1) * 512], func=AF.Ln,
                                      bias=P["eps"][:, 0:1])
                    for t4 in range(4):
                        rstd_v = sSC.inc(sc.activation(
                            out=rstd_all[:, t4 * 512:(t4 + 1) * 512],
                            in_=m2b4[:, t4 * 512:(t4 + 1) * 512],
                            func=AF.Exp, scale=-0.5))
                    return rstd_v

                def vsync(instr):
                    v = sDV.inc(instr)
                    ve.wait_ge(sDV.h, v)
                    return v

                def ssync(instr):
                    v = sSC.inc(instr)
                    sc.wait_ge(sSC.h, v)
                    return v

                def ctab_all(Tr, Ti, width):
                    k = 1
                    while k < width:
                        n = min(k, width - k)
                        for c in range(C4):
                            o = c * width
                            cr, ci = Tr[:, o + k - 1:o + k], Ti[:, o + k - 1:o + k]
                            ve.tensor_scalar(out=tA[:, c * 64:c * 64 + n],
                                             in0=Tr[:, o:o + n], scalar1=cr,
                                             scalar2=None, op0=ALU.mult)
                            ve.tensor_scalar(out=tB[:, c * 64:c * 64 + n],
                                             in0=Ti[:, o:o + n], scalar1=ci,
                                             scalar2=None, op0=ALU.mult)
                            ve.tensor_scalar(out=tA[:, 256 + c * 64:256 + c * 64 + n],
                                             in0=Tr[:, o:o + n], scalar1=ci,
                                             scalar2=None, op0=ALU.mult)
                            last = ve.tensor_scalar(out=tB[:, 256 + c * 64:256 + c * 64 + n],
                                                    in0=Ti[:, o:o + n], scalar1=cr,
                                                    scalar2=None, op0=ALU.mult)
                        vsync(last)
                        for c in range(C4):
                            o = c * width
                            ve.tensor_tensor(out=Tr[:, o + k:o + k + n],
                                             in0=tA[:, c * 64:c * 64 + n],
                                             in1=tB[:, c * 64:c * 64 + n], op=ALU.subtract)
                            last = ve.tensor_tensor(out=Ti[:, o + k:o + k + n],
                                                    in0=tA[:, 256 + c * 64:256 + c * 64 + n],
                                                    in1=tB[:, 256 + c * 64:256 + c * 64 + n],
                                                    op=ALU.add)
                        vsync(last)
                        k *= 2

                for l in range(NLAY):
                    lc = lambda t, c: P[t][:, l * 4 + c:l * 4 + c + 1]
                    _w(ve, sMM, sMM.n)
                    _w(sc, sMM, sMM.n)

                    rstd1 = ln_stats()
                    last = None
                    for c in range(C4):
                        ve.tensor_copy(out=P1r[:, c * Q:c * Q + 1],
                                       in_=pcr[:, l * 4 + c:l * 4 + c + 1])
                        last = ve.tensor_copy(out=P1i[:, c * Q:c * Q + 1],
                                              in_=pci[:, l * 4 + c:l * 4 + c + 1])
                    vsync(last)
                    ctab_all(P1r, P1i, Q)
                    ve.tensor_scalar(out=nP1i[:, :], in0=P1i[:, :], scalar1=-1.0,
                                     scalar2=None, op0=ALU.mult)
                    last = None
                    for c in range(C4):
                        ve.tensor_copy(out=PBr[:, c * NB:c * NB + 1],
                                       in_=P1r[:, c * Q + Q - 1:c * Q + Q])
                        last = ve.tensor_copy(out=PBi[:, c * NB:c * NB + 1],
                                              in_=P1i[:, c * Q + Q - 1:c * Q + Q])
                    vsync(last)
                    ctab_all(PBr, PBi, NB)
                    for c in range(C4):
                        P1r_c = P1r[:, c * Q:(c + 1) * Q]
                        P1i_c = P1i[:, c * Q:(c + 1) * Q]
                        nP1i_c = nP1i[:, c * Q:(c + 1) * Q]
                        PBr_c = PBr[:, c * NB:(c + 1) * NB]
                        PBi_c = PBi[:, c * NB:(c + 1) * NB]
                        S0r_c = S0r[:, c * NB:(c + 1) * NB]
                        S0i_c = S0i[:, c * NB:(c + 1) * NB]
                        convr_c = convr[:, c * TOK:(c + 1) * TOK]
                        _w(ve, sSC, rstd1)
                        ve.tensor_tensor(out=y_c[:, :], in0=hsl(c, 0, TOK),
                                         in1=mu_all[:, :], op=ALU.subtract)
                        ve.tensor_tensor(out=y_c[:, :], in0=y_c[:, :],
                                         in1=rstd_all[:, :], op=ALU.mult)
                        yv = sDV.inc(ve.tensor_scalar(out=y_c[:, :], in0=y_c[:, :],
                                                      scalar1=lc("l1w", c),
                                                      scalar2=lc("l1b", c),
                                                      op0=ALU.mult, op1=ALU.add))
                        _w(sc, sDV, yv)
                        sil_v = sSC.inc(sc.activation(out=y_c[:, :], in_=y_c[:, :],
                                                      func=AF.Silu))
                        _w(ve, sSC, sil_v)
                        yscl = sDV.inc(ve.tensor_scalar(out=y_c[:, :], in0=y_c[:, :],
                                                        scalar1=lc("scl", c), scalar2=None,
                                                        op0=ALU.mult))
                        if "y0" in dbg_aps and l == 0 and c == 0:
                            ydbg = sDV.inc(ve.nop())
                            _w(sy, sDV, ydbg)
                            tv = sDBG.inc(sy.dma_start(out=dbg_aps["y0"], in_=y_c[:, :]), 16)
                            _w(ve, sDBG, tv)
                        y3 = y_c[:, :].rearrange("p (b q) -> p b q", q=Q)
                        c3r = zBr.rearrange("p (b q) -> p b q", q=Q)
                        c3i = zBi.rearrange("p (b q) -> p b q", q=Q)
                        i3a = tI1[:, :].rearrange("p (b q) -> p b q", q=Q)
                        ve.scalar_tensor_tensor(out=c3r[:, :, 1:Q], in0=y3[:, :, 0:Q - 1],
                                                scalar=P1r_c[:, 0:1], in1=y3[:, :, 1:Q],
                                                op0=ALU.mult, op1=ALU.add)
                        vstep = sDV.inc(ve.tensor_copy(out=c3r[:, :, 0:1], in_=y3[:, :, 0:1]))
                        ve.wait_ge(sDV.h, vstep)
                        _w(sc, sDV, yscl)
                        sstep = sSC.inc(sc.activation(out=c3i[:, :, 1:Q],
                                                      in_=y3[:, :, 0:Q - 1], func=AF.Copy,
                                                      scale=P1i_c[:, 0:1]))
                        _w(gp, sSC, sstep)
                        gstep = sGP.inc(gp.memset(c3i[:, :, 0:1], 0.0))
                        sstep_prev = sstep
                        pairs = [(zAr, zAi), (zBr, zBi)]
                        cur = 1
                        s = 2
                        while s < Q:
                            last = (s * 2 >= Q)
                            nxt = 1 - cur
                            cr_, ci_ = pairs[cur]
                            nr_ = convr_c if last else pairs[nxt][0]
                            ni_ = pairs[nxt][1]
                            a3r = cr_.rearrange("p (b q) -> p b q", q=Q)
                            a3i = ci_.rearrange("p (b q) -> p b q", q=Q)
                            b3r = nr_.rearrange("p (b q) -> p b q", q=Q)
                            b3i = ni_.rearrange("p (b q) -> p b q", q=Q)
                            _w(sc, sDV, vstep)
                            sstep = sSC.inc(sc.activation(out=i3a[:, :, s:Q],
                                                          in_=a3r[:, :, 0:Q - s],
                                                          func=AF.Copy,
                                                          scale=P1i_c[:, s - 1:s]))
                            _w(ve, sGP, gstep)
                            _w(ve, sSC, sstep_prev)
                            ve.scalar_tensor_tensor(out=b3r[:, :, s:Q],
                                                    in0=a3r[:, :, 0:Q - s],
                                                    scalar=P1r_c[:, s - 1:s],
                                                    in1=a3r[:, :, s:Q],
                                                    op0=ALU.mult, op1=ALU.add)
                            ve.scalar_tensor_tensor(out=b3r[:, :, s:Q],
                                                    in0=a3i[:, :, 0:Q - s],
                                                    scalar=nP1i_c[:, s - 1:s],
                                                    in1=b3r[:, :, s:Q],
                                                    op0=ALU.mult, op1=ALU.add)
                            ve.scalar_tensor_tensor(out=b3i[:, :, s:Q],
                                                    in0=a3i[:, :, 0:Q - s],
                                                    scalar=P1r_c[:, s - 1:s],
                                                    in1=a3i[:, :, s:Q],
                                                    op0=ALU.mult, op1=ALU.add)
                            _w(ve, sSC, sstep)
                            ve.tensor_tensor(out=b3i[:, :, s:Q], in0=i3a[:, :, s:Q],
                                             in1=b3i[:, :, s:Q], op=ALU.add)
                            vstep = sDV.inc(ve.tensor_copy(out=b3r[:, :, 0:s],
                                                           in_=a3r[:, :, 0:s]))
                            ve.wait_ge(sDV.h, vstep)
                            if s * 2 < Q:
                                _w(gp, sDV, vstep)
                                gstep = sGP.inc(gp.tensor_copy(out=b3i[:, :, 0:s],
                                                               in_=a3i[:, :, 0:s]))
                            sstep_prev = sstep
                            cur = nxt
                            s *= 2
                        zfi = pairs[cur][1]
                        cv3 = convr_c.rearrange("p (b q) -> p b q", q=Q)
                        zi3 = zfi.rearrange("p (b q) -> p b q", q=Q)
                        _w(ve, sGP, gstep)
                        ve.tensor_copy(out=E2r[:, :], in_=cv3[:, :, Q - 1:Q])
                        vsync(ve.tensor_copy(out=E2i[:, :], in_=zi3[:, :, Q - 1:Q]))
                        srcs = [(E2r, E2i), (T2r, T2i)]
                        cu = 0
                        s = 1
                        while s < NB:
                            last = (s * 2 >= NB)
                            sr, si_ = srcs[cu]
                            dr, di_ = (S0r_c, S0i_c) if last else srcs[1 - cu]
                            ve.scalar_tensor_tensor(out=dr[:, s:NB], in0=sr[:, 0:NB - s],
                                                    scalar=PBr_c[:, s - 1:s],
                                                    in1=sr[:, s:NB],
                                                    op0=ALU.mult, op1=ALU.add)
                            ve.scalar_tensor_tensor(out=di_[:, s:NB], in0=sr[:, 0:NB - s],
                                                    scalar=PBi_c[:, s - 1:s],
                                                    in1=si_[:, s:NB],
                                                    op0=ALU.mult, op1=ALU.add)
                            vsync(ve.tensor_scalar(out=tA[:, 0:NB - s],
                                                   in0=si_[:, 0:NB - s],
                                                   scalar1=PBi_c[:, s - 1:s], scalar2=None,
                                                   op0=ALU.mult))
                            ve.tensor_tensor(out=dr[:, s:NB], in0=dr[:, s:NB],
                                             in1=tA[:, 0:NB - s], op=ALU.subtract)
                            ve.scalar_tensor_tensor(out=di_[:, s:NB], in0=si_[:, 0:NB - s],
                                                    scalar=PBr_c[:, s - 1:s],
                                                    in1=di_[:, s:NB],
                                                    op0=ALU.mult, op1=ALU.add)
                            ve.tensor_copy(out=dr[:, 0:s], in_=sr[:, 0:s])
                            vsync(ve.tensor_copy(out=di_[:, 0:s], in_=si_[:, 0:s]))
                            cu = 1 - cu
                            s *= 2
                        s0c = sDV.n
                        _w(te, sDV, s0c)
                        bank_wait(6)
                        te.transpose(pbank(6)[0:1, 0:128], S0r_c[:, NB - 1:NB], identf[:, :])
                        tpv = sMM.inc(te.transpose(pbank(6)[0:1, 128:256],
                                                   S0i_c[:, NB - 1:NB], identf[:, :]))
                        _w(ve, sMM, tpv)
                        ve.tensor_copy(out=ccrow[0:1, c * 128:(c + 1) * 128],
                                       in_=pbank(6)[0:1, 0:128])
                        ccv = sDV.inc(ve.tensor_copy(
                            out=ccrow[0:1, 512 + c * 128:512 + (c + 1) * 128],
                            in_=pbank(6)[0:1, 128:256]))
                        bank_free[6] = (sDV, ccv)

                    _w(gp, sDV, sDV.n)
                    _w(gp, sSC, sSC.n)
                    cc_wr = sCCD.inc(gp.dma_start(out=cc_in, in_=ccrow[0:1, :]), 16)
                    _w(gp, sCCD, cc_wr)
                    cc_done = sCC.inc(gp.collective_compute(
                        "AllGather", ALU.bypass, replica_groups=[list(range(NCORES))],
                        ins=[cc_in.opt()], outs=[cc_out.opt()]))
                    _w(gp, sCC, cc_done)
                    g_ld = sCCD.inc(gp.dma_start(out=Gsb[:, :], in_=cc_out), 16)
                    if l > 0:
                        _w(gp, sMM, w1_lastmm[l - 1])
                        _w(gp, sMM, w2_lastmm[l - 1])
                    w1_ready[l] = sW1.inc(gp.dma_start(out=w1t[:, :], in_=w1T[l],
                                                       max_dma_last_dim=8192), 16)
                    w2_ready[l] = sW2.inc(gp.dma_start(out=w2t[:, :], in_=w2T[l],
                                                       max_dma_last_dim=8192), 16)
                    NWDE = HG * IC * 128 * 1024
                    if l == 0:
                        wu_ready = conv_chunks(sCVu, wu_bf, wu_i, NGE, 5, 11)
                    elif l == 1:
                        wd_ready = conv_chunks(sCVd, wd_bf, wd_i, NWDE, 0, 6)
                    elif l == 2:
                        wd_ready = conv_chunks(sCVd, wd_bf, wd_i, NWDE, 6, 11)

                    _w(te, sCCD, g_ld)
                    bank_wait(7)
                    for c in range(C4):
                        te.matmul(pbank(7)[:, c:c + 1], Gsb[0:8, c * 128:(c + 1) * 128],
                                  selsb[:, :], start=True, stop=True)
                        instr = te.matmul(pbank(7)[:, 4 + c:5 + c],
                                          Gsb[0:8, 512 + c * 128:512 + (c + 1) * 128],
                                          selsb[:, :], start=True, stop=True)
                    selv = sMM.inc(instr)
                    _w(ve, sMM, selv)
                    sv2 = vsync(ve.tensor_copy(out=Ssel8[:, :], in_=pbank(7)[:, 0:8]))
                    bank_free[7] = (sDV, sv2)
                    L4 = slice(l * 4, l * 4 + 4)
                    P2r4 = PBr[:, :].rearrange("p (c n) -> p c n", n=NB)[:, :, NB - 1:NB]
                    P2i4 = PBi[:, :].rearrange("p (c n) -> p c n", n=NB)[:, :, NB - 1:NB]
                    I_r = Ic2[:, 0:4]; I_i = Ic2[:, 4:8]; nI_i = Ic2[:, 8:12]
                    t1r = Ic2[:, 12:16]; t1i = Ic2[:, 16:20]
                    m1 = Ic2[:, 20:24]; m2 = Ic2[:, 24:28]; m3 = Ic2[:, 28:32]
                    p2r_f = tA[:, 0:4]; p2i_f = tA[:, 4:8]
                    ve.tensor_copy(out=p2r_f.rearrange("p (c n) -> p c n", n=1), in_=P2r4)
                    vsync(ve.tensor_copy(out=p2i_f.rearrange("p (c n) -> p c n", n=1), in_=P2i4))
                    ve.tensor_tensor(out=m1, in0=p2r_f, in1=lcsr[:, L4], op=ALU.mult)
                    ve.tensor_tensor(out=m2, in0=p2i_f, in1=lcsi[:, L4], op=ALU.mult)
                    ve.tensor_tensor(out=m3, in0=p2r_f, in1=lcsi[:, L4], op=ALU.mult)
                    vsync(ve.tensor_tensor(out=t1i, in0=p2i_f, in1=lcsr[:, L4], op=ALU.mult))
                    ve.tensor_tensor(out=t1r, in0=m1, in1=m2, op=ALU.subtract)
                    vsync(ve.tensor_tensor(out=m1, in0=m3, in1=t1i, op=ALU.add))
                    ve.tensor_tensor(out=m2, in0=t1r, in1=Ssel8[:, 0:4], op=ALU.add)
                    vsync(ve.tensor_tensor(out=m3, in0=m1, in1=Ssel8[:, 4:8], op=ALU.add))
                    ve.tensor_scalar(out=t1r, in0=m2, scalar1=P["mv"][:, 0:1],
                                     scalar2=None, op0=ALU.mult)
                    vsync(ve.tensor_scalar(out=t1i, in0=m3, scalar1=P["mv"][:, 0:1],
                                           scalar2=None, op0=ALU.mult))
                    ve.tensor_scalar(out=m1, in0=lcsr[:, L4], scalar1=P["mb"][:, 0:1],
                                     scalar2=None, op0=ALU.mult)
                    vsync(ve.tensor_scalar(out=m2, in0=lcsi[:, L4], scalar1=P["mb"][:, 0:1],
                                           scalar2=None, op0=ALU.mult))
                    ve.tensor_tensor(out=I_r, in0=m1, in1=t1r, op=ALU.add)
                    vsync(ve.tensor_tensor(out=I_i, in0=m2, in1=t1i, op=ALU.add))
                    vsync(ve.tensor_scalar(out=nI_i, in0=I_i, scalar1=-1.0, scalar2=None,
                                           op0=ALU.mult))
                    last = None
                    for c in range(C4):
                        ve.tensor_copy(out=Sinr4[:, c * NB:c * NB + 1], in_=I_r[:, c:c + 1])
                        last = ve.tensor_copy(out=Sini4[:, c * NB:c * NB + 1],
                                              in_=I_i[:, c:c + 1])
                    vsync(last)
                    for c in range(C4):
                        last = ve.scalar_tensor_tensor(
                            out=Sinr4[:, c * NB + 1:(c + 1) * NB],
                            in0=PBr[:, c * NB:(c + 1) * NB - 1], scalar=I_r[:, c:c + 1],
                            in1=S0r[:, c * NB:(c + 1) * NB - 1], op0=ALU.mult, op1=ALU.add)
                    vsync(last)
                    for c in range(C4):
                        last = ve.scalar_tensor_tensor(
                            out=Sinr4[:, c * NB + 1:(c + 1) * NB],
                            in0=PBi[:, c * NB:(c + 1) * NB - 1], scalar=nI_i[:, c:c + 1],
                            in1=Sinr4[:, c * NB + 1:(c + 1) * NB], op0=ALU.mult, op1=ALU.add)
                    vsync(last)
                    for c in range(C4):
                        last = ve.scalar_tensor_tensor(
                            out=Sini4[:, c * NB + 1:(c + 1) * NB],
                            in0=PBr[:, c * NB:(c + 1) * NB - 1], scalar=I_i[:, c:c + 1],
                            in1=S0i[:, c * NB:(c + 1) * NB - 1], op0=ALU.mult, op1=ALU.add)
                    vsync(last)
                    for c in range(C4):
                        last = ve.scalar_tensor_tensor(
                            out=Sini4[:, c * NB + 1:(c + 1) * NB],
                            in0=PBi[:, c * NB:(c + 1) * NB - 1], scalar=I_r[:, c:c + 1],
                            in1=Sini4[:, c * NB + 1:(c + 1) * NB], op0=ALU.mult, op1=ALU.add)
                    vsync(last)
                    if "cv0" in dbg_aps and l == 0:
                        cdbg = sDV.inc(ve.nop())
                        _w(sy, sDV, cdbg)
                        tv = sDBG.inc(sy.dma_start(out=dbg_aps["cv0"], in_=convr[:, 0:TOK]), 16)
                        _w(ve, sDBG, tv)
                    for c in range(C4):
                        ve.tensor_tensor(out=h[:, c * TOK:(c + 1) * TOK],
                                         in0=h[:, c * TOK:(c + 1) * TOK],
                                         in1=convr[:, c * TOK:(c + 1) * TOK],
                                         op=ALU.add)
                    for c in range(C4):
                        h3 = h[:, c * TOK:(c + 1) * TOK].rearrange("p (b q) -> p b q", q=Q)
                        for j in range(NB):
                            ve.scalar_tensor_tensor(
                                out=h3[:, j, :], in0=P1r[:, c * Q:(c + 1) * Q],
                                scalar=Sinr4[:, c * NB + j:c * NB + j + 1],
                                in1=h3[:, j, :], op0=ALU.mult, op1=ALU.add)
                    last = None
                    for c in range(C4):
                        h3 = h[:, c * TOK:(c + 1) * TOK].rearrange("p (b q) -> p b q", q=Q)
                        for j in range(NB):
                            last = ve.scalar_tensor_tensor(
                                out=h3[:, j, :], in0=nP1i[:, c * Q:(c + 1) * Q],
                                scalar=Sini4[:, c * NB + j:c * NB + j + 1],
                                in1=h3[:, j, :], op0=ALU.mult, op1=ALU.add)
                    vsync(last)
                    if "hp3" in dbg_aps and l == 0:
                        _w(sy, sDV, sDV.n)
                        tv = sDBG.inc(sy.dma_start(out=dbg_aps["hp3"], in_=h[:, :]), 16)
                        _w(ve, sDBG, tv)
                        _w(sc, sDBG, tv)
                        _w(te, sDBG, tv)

                    rstd2 = ln_stats()
                    y2_done = 0
                    for c in range(C4):
                        _w(ve, sSC, rstd2)
                        ve.tensor_tensor(out=y_c[:, :], in0=hsl(c, 0, TOK),
                                         in1=mu_all[:, :], op=ALU.subtract)
                        ve.tensor_tensor(out=y_c[:, :], in0=y_c[:, :],
                                         in1=rstd_all[:, :], op=ALU.mult)
                        y2_done = sDV.inc(ve.tensor_scalar(
                            out=y2in[:, c * TOK:(c + 1) * TOK], in0=y_c[:, :],
                            scalar1=lc("l2w", c), scalar2=lc("l2b", c),
                            op0=ALU.mult, op1=ALU.add))

                    _w(te, sW1, w1_ready[l])
                    _w(te, sW2, w2_ready[l])
                    _w(te, sDV, y2_done)
                    for t4 in range(4):
                        sil_last = 0
                        for fc in range(FC16):
                            bk = fc % 2
                            bank_wait(bk)
                            for kc in range(C4):
                                instr = te.matmul(
                                    pbank(bk),
                                    w1t[:, kc * DFF + fc * 128:kc * DFF + fc * 128 + 128],
                                    y2in[:, kc * TOK + t4 * 512:kc * TOK + (t4 + 1) * 512],
                                    start=(kc == 0), stop=(kc == C4 - 1))
                            w1v = sMM.inc(instr)
                            w1_lastmm[l] = w1v
                            _w(sc, sMM, w1v)
                            sil_last = sSC.inc(sc.activation(
                                out=y2buf[:, fc * 512:(fc + 1) * 512], in_=pbank(bk),
                                func=AF.Silu,
                                bias=P["b1"][:, l * FC16 + fc:l * FC16 + fc + 1]))
                            bank_free[bk] = (sSC, sil_last)
                        _w(te, sSC, sil_last)
                        for c in range(C4):
                            bk = 2 + c % 2
                            bank_wait(bk)
                            for fc in range(FC16):
                                instr = te.matmul(
                                    pbank(bk),
                                    w2t[:, fc * DIN + c * 128:fc * DIN + c * 128 + 128],
                                    y2buf[:, fc * 512:(fc + 1) * 512],
                                    start=(fc == 0), stop=(fc == FC16 - 1))
                            w2v = sMM.inc(instr)
                            w2_lastmm[l] = w2v
                            _w(ve, sMM, w2v)
                            hv = sDV.inc(ve.scalar_tensor_tensor(
                                out=hsl(c, t4 * 512, (t4 + 1) * 512), in0=pbank(bk),
                                scalar=lc("b2", c),
                                in1=hsl(c, t4 * 512, (t4 + 1) * 512),
                                op0=ALU.add, op1=ALU.add))
                            bank_free[bk] = (sDV, hv)
                    if "h_0" in dbg_aps and l == 0:
                        _w(sy, sDV, sDV.n)
                        tv = sDBG.inc(sy.dma_start(out=dbg_aps["h_0"], in_=h[:, :]), 16)
                        _w(ve, sDBG, tv)
                        _w(sc, sDBG, tv)
                        _w(te, sDBG, tv)

                if "h_f" in dbg_aps:
                    _w(sy, sDV, sDV.n)
                    tv = sDBG.inc(sy.dma_start(out=dbg_aps["h_f"], in_=h[:, :]), 16)
                    _w(ve, sDBG, tv)

                rstdf = ln_stats()
                hf_done = 0
                for c in range(C4):
                    _w(ve, sSC, rstdf)
                    ve.tensor_tensor(out=y_c[:, :], in0=hsl(c, 0, TOK), in1=mu_all[:, :],
                                     op=ALU.subtract)
                    ve.tensor_tensor(out=y_c[:, :], in0=y_c[:, :], in1=rstd_all[:, :],
                                     op=ALU.mult)
                    hf_done = sDV.inc(ve.tensor_scalar(
                        out=hf[:, c * TOK:(c + 1) * TOK], in0=y_c[:, :],
                        scalar1=P["lnlw"][:, c:c + 1], scalar2=P["lnlb"][:, c:c + 1],
                        op0=ALU.mult, op1=ALU.add))

                for eng in (ve, sc, te, gp, sy):
                    _w(eng, sDV, sDV.n)
                    _w(eng, sSC, sSC.n)
                    _w(eng, sMM, sMM.n)
                    _w(eng, sW1, sW1.n)
                    _w(eng, sW2, sW2.n)
                    _w(eng, sCCD, sCCD.n)

        with ExitStack() as cctx:
            cctx.enter_context(nc.named_scope("stageC"))
            csb = lambda n, shp, dt=F32: cctx.enter_context(nc.sbuf_tensor(n, shp, dt))
            hh = csb("hh", [128, IC * 512], BF16)
            boutb = csb("boutb", [1, V], BF16)
            bout_ready = sONE.inc(gp.dma_start(out=boutb[:, :], in_=bout_in,
                                               max_dma_last_dim=8192), 16)
            OV = csb("OV", [128, 18432], F32)
            OVb = OV[:, :].bitcast(BF16)
            xnTs = OVb[:, 0:16384]
            wgbuf = [OVb[:, 16384 + i * 4096:16384 + (i + 1) * 4096] for i in range(2)]
            wubuf = [OVb[:, 24576 + i * 4096:24576 + (i + 1) * 4096] for i in range(2)]
            gsil = [OV[:, 16384 + i * 512:16384 + (i + 1) * 512] for i in range(2)]
            wdbuf = [OVb[:, i * 4096:(i + 1) * 4096] for i in range(2)]
            wobuf = [OVb[:, 8192 + i * 4096:8192 + (i + 1) * 4096] for i in range(2)]
            xstg = OV[:, 8192:12288]
            ostg = OV[:, 12288:16384]
            dsb = [OV[:, 16384 + i * 512:16384 + (i + 1) * 512] for i in range(2)]
            sumb = [OV[:, 17408 + i * 512:17408 + (i + 1) * 512] for i in range(2)]

            x4 = x_in.rearrange("(s b p) (g c) -> p s g b c", p=128, b=4, c=1024)
            o4 = out_t.rearrange("(s b p) (g c) -> p s g b c", p=128, b=4, c=1024)

            gbank_free = [(sDV, 0)] * 2
            ubank_free = [(sDV, 0)] * 2
            gsil_free = [(sDV, 0)] * 2
            wg_cons = [(sMM, 0)] * 2
            wu_cons = [(sMM, 0)] * 2
            wo_cons = [(sMM, 0)] * 2
            ostg_free = (sOS[0], 0)
            xstg_free = (sDV, 0)
            prev_down_reads = 0
            prev_store = 0

            for s in range(4):
                _w(sy, sDV, prev_down_reads)
                _w(sy, sOS[0], sOS[0].n)
                _w(sy, sOS[1], sOS[1].n)
                _w(sy, sCVg, wg_ready)
                _w(sc, sOS[0], sOS[0].n)
                _w(sc, sOS[1], sOS[1].n)
                xnT_ld = sXNT.inc(sy.dma_start(
                    out=xnTs.rearrange("p (v t) -> p v t", t=512),
                    in_=xnt[:, :, s * 512:(s + 1) * 512]), 16)

                gate_end = 0
                for i in range(IC):
                    bi = i % 2
                    s_, v_ = wg_cons[bi]
                    _w(sy, s_, v_)
                    wg_ld = sWG[bi].inc(sy.dma_start(out=wgbuf[bi], in_=wg_bf[i]), 16)
                    s_, v_ = wu_cons[bi]
                    _w(sc, s_, v_)
                    if i < 2:
                        _w(sc, sCVu, wu_ready)
                    wu_ld = sWU[bi].inc(sc.dma_start(out=wubuf[bi], in_=wu_bf[i]), 16)
                    _w(te, sWG[bi], wg_ld)
                    if i < 2:
                        _w(te, sXNT, xnT_ld)
                    s_, v_ = gbank_free[bi]
                    _w(te, s_, v_)
                    for hc in range(VC32):
                        instr = te.matmul(pbank(bi), wgbuf[bi][:, hc * 128:(hc + 1) * 128],
                                          xnTs[:, hc * 512:(hc + 1) * 512],
                                          start=(hc == 0), stop=(hc == VC32 - 1))
                    g_done = sMM.inc(instr)
                    wg_cons[bi] = (sMM, g_done)
                    _w(te, sWU[bi], wu_ld)
                    s_, v_ = ubank_free[bi]
                    _w(te, s_, v_)
                    for hc in range(VC32):
                        instr = te.matmul(pbank(2 + bi),
                                          wubuf[bi][:, hc * 128:(hc + 1) * 128],
                                          xnTs[:, hc * 512:(hc + 1) * 512],
                                          start=(hc == 0), stop=(hc == VC32 - 1))
                    u_done = sMM.inc(instr)
                    wu_cons[bi] = (sMM, u_done)
                    gate_end = u_done
                    s_, v_ = gsil_free[bi]
                    _w(sc, s_, v_)
                    _w(sc, sMM, g_done)
                    sil_v = sSC.inc(sc.activation(out=gsil[bi], in_=pbank(bi),
                                                  func=AF.Silu,
                                                  scale=P["sg"][:, i:i + 1]))
                    gbank_free[bi] = (sSC, sil_v)
                    _w(ve, sMM, u_done)
                    _w(ve, sSC, sil_v)
                    hh_v = sDV.inc(ve.scalar_tensor_tensor(
                        out=hh[:, i * 512:(i + 1) * 512], in0=pbank(2 + bi),
                        scalar=P["su"][:, i:i + 1], in1=gsil[bi],
                        op0=ALU.mult, op1=ALU.mult))
                    ubank_free[bi] = (sDV, hh_v)
                    gsil_free[bi] = (sDV, hh_v)
                hh_done = sDV.n

                bank_free[0] = gbank_free[0]
                bank_free[1] = gbank_free[1]
                bank_free[2] = ubank_free[0]
                bank_free[3] = ubank_free[1]

                wd_cons = [(sMM, 0)] * 2
                per_hc_add = 0
                NWD = (IC + 3) // 4
                for hg in range(HG):
                    s_, v_ = xstg_free
                    _w(sc, s_, v_)
                    if hg == 0:
                        _w(sc, sMM, gate_end)
                    xs_ld = sXS.inc(sc.dma_start(
                        out=xstg.rearrange("p (b c) -> p b c", c=1024),
                        in_=x4[:, s, hg]), 16)
                    s_, v_ = wo_cons[hg % 2]
                    _w(gp, s_, v_)
                    _w(gp, sMM, gate_end)
                    if s == 0 and hg == 0:
                        _w(gp, sCVd, wd_ready)
                        _w(gp, sONE, bout_ready)
                    wo_ld = sWO[hg % 2].inc(gp.dma_start(
                        out=wobuf[hg % 2].rearrange("p (f c) -> p f c", c=1024),
                        in_=woT[hg].rearrange("f p c -> p f c"),
                        max_dma_last_dim=8192), 16)
                    down_done = 0
                    for ib in range(NWD):
                        bi = ib % 2
                        qe = sy if bi == 0 else gp
                        i0 = ib * 4
                        ni = min(4, IC - i0)
                        s_, v_ = wd_cons[bi]
                        _w(qe, s_, v_)
                        if ib < 2:
                            _w(qe, sMM, gate_end)
                        wd_ld = sWD[bi].inc(qe.dma_start(
                            out=wdbuf[bi][:, 0:ni * 1024].rearrange("p (i c) -> p i c", c=1024),
                            in_=wd_bf[hg, i0:i0 + ni].rearrange("i p c -> p i c")), 16)
                        _w(te, sWD[bi], wd_ld)
                        if ib == 0:
                            _w(te, sDV, hh_done)
                        for ii in range(ni):
                            i = i0 + ii
                            for hc in range(8):
                                if i == 0:
                                    bank_wait(hc)
                                instr = te.matmul(
                                    pbank(hc),
                                    wdbuf[bi][:, ii * 1024 + hc * 128:ii * 1024 + (hc + 1) * 128],
                                    hh[:, i * 512:(i + 1) * 512],
                                    start=(i == 0), stop=(i == IC - 1))
                        grp_v = sMM.inc(instr)
                        wd_cons[bi] = (sMM, grp_v)
                        down_done = grp_v
                    dv = [0] * 8
                    wo_done = [0] * 8
                    sm = [0] * 8
                    tp = [0] * 8

                    def ve_dsb(hc):
                        _w(ve, sMM, down_done)
                        dv[hc] = sDV.inc(ve.tensor_scalar(
                            out=dsb[hc % 2], in0=pbank(hc),
                            scalar1=P["sd"][:, hg * 8 + hc:hg * 8 + hc + 1],
                            scalar2=None, op0=ALU.mult))

                    def te_wout(hc):
                        _w(te, sDV, dv[hc])
                        if hc == 0:
                            _w(te, sWO[hg % 2], wo_ld)
                        for fc in range(C4):
                            te.matmul(pbank(hc),
                                      wobuf[hg % 2][:, fc * 1024 + hc * 128:fc * 1024 + hc * 128 + 128],
                                      hf[:, fc * TOK + s * 512:fc * TOK + (s + 1) * 512],
                                      start=(fc == 0), stop=False)
                        wo_done[hc] = sMM.inc(te.matmul(
                            pbank(hc),
                            boutb[0:1, hg * 1024 + hc * 128:hg * 1024 + hc * 128 + 128],
                            onesb[0:1, :], start=False, stop=True))

                    def ve_sum(hc):
                        _w(ve, sMM, wo_done[hc])
                        if hc >= 2:
                            _w(ve, sMM, tp[hc - 2])
                        sm[hc] = sDV.inc(ve.tensor_tensor(out=sumb[hc % 2], in0=pbank(hc),
                                                          in1=dsb[hc % 2], op=ALU.add))

                    def te_tp(hc):
                        _w(te, sDV, sm[hc])
                        for tb in range(4):
                            instr = te.transpose(pbank(hc)[:, tb * 128:(tb + 1) * 128],
                                                 sumb[hc % 2][:, tb * 128:(tb + 1) * 128],
                                                 identf[:, :])
                        tp[hc] = sMM.inc(instr)

                    def ve_add(hc):
                        nonlocal per_hc_add
                        _w(ve, sMM, tp[hc])
                        _w(ve, sXS, xs_ld)
                        if hc == 0:
                            s_, v_ = ostg_free
                            _w(ve, s_, v_)
                        for tb in range(4):
                            per_hc_add = sDV.inc(ve.tensor_tensor(
                                out=ostg[:, tb * 1024 + hc * 128:tb * 1024 + hc * 128 + 128],
                                in0=pbank(hc)[:, tb * 128:(tb + 1) * 128],
                                in1=xstg[:, tb * 1024 + hc * 128:tb * 1024 + hc * 128 + 128],
                                op=ALU.add))
                        bank_free[hc] = (sDV, per_hc_add)

                    ve_dsb(0)
                    ve_dsb(1)
                    te_wout(0)
                    ve_sum(0)
                    te_tp(0)
                    for hc in range(1, 8):
                        te_wout(hc)
                        if hc + 1 < 8:
                            ve_dsb(hc + 1)
                        ve_sum(hc)
                        te_tp(hc)
                        ve_add(hc - 1)
                    ve_add(7)
                    wo_cons[hg % 2] = (sMM, tp[7])
                    _w(sc, sDV, per_hc_add)
                    prev_store = sOS[hg % 2].inc(sc.dma_start(
                        out=o4[:, s, hg],
                        in_=ostg.rearrange("p (b c) -> p b c", c=1024)), 16)
                    ostg_free = (sOS[hg % 2], prev_store)
                    xstg_free = (sDV, per_hc_add)
                prev_down_reads = per_hc_add
                gbank_free = [bank_free[0], bank_free[1]]
                ubank_free = [bank_free[2], bank_free[3]]

            _w(sy, sOS[0], sOS[0].n)
            _w(sy, sOS[1], sOS[1].n)
            _w(sy, sDBG, sDBG.n)

    return nc, dbg_aps


def host_inputs(L0_attn_x, L0_attn_Win, L0_attn_bin, L0_attn_ln1_w, L0_attn_ln1_b,
                L0_attn_phz_re, L0_attn_phz_im, L0_attn_lc_re, L0_attn_lc_im,
                L0_attn_sc_lin, L0_attn_ln2_w, L0_attn_ln2_b, L0_attn_W1, L0_attn_b1,
                L0_attn_W2, L0_attn_b2, L0_attn_lnl_w, L0_attn_lnl_b, L0_attn_Wout,
                L0_attn_bout, L0_ffn_w_gate, L0_ffn_s_gate, L0_ffn_w_up, L0_ffn_s_up,
                L0_ffn_w_down, L0_ffn_s_down):
    f32 = np.float32
    cc = np.ascontiguousarray

    def pc(v, k):
        return cc(np.asarray(v, f32).reshape(k, 128).T)

    def pc_l(v):
        return cc(np.asarray(v, f32).reshape(NLAY, 4, 128).transpose(2, 0, 1)
                  .reshape(128, 16))

    winT_h = cc(np.asarray(L0_attn_Win, f32).T.reshape(VC32, 128, DIN)
                .transpose(1, 0, 2).reshape(128, VC32 * DIN))
    w1T_h = cc(np.asarray(L0_attn_W1, f32).transpose(0, 2, 1)
               .reshape(NLAY, C4, 128, DFF).transpose(0, 2, 1, 3)
               .reshape(NLAY, 128, C4 * DFF))
    w2T_h = cc(np.asarray(L0_attn_W2, f32).transpose(0, 2, 1)
               .reshape(NLAY, FC16, 128, DIN).transpose(0, 2, 1, 3)
               .reshape(NLAY, 128, FC16 * DIN))
    woT_h = cc(np.asarray(L0_attn_Wout, f32).T.reshape(C4, 128, HG, 1024)
               .transpose(2, 0, 1, 3))
    wg_h = cc(np.asarray(L0_ffn_w_gate).T.reshape(VC32, 128, IC, 128)
              .transpose(2, 1, 0, 3).reshape(IC, 128, 4096).astype(np.int32))
    wu_h = cc(np.asarray(L0_ffn_w_up).T.reshape(VC32, 128, IC, 128)
              .transpose(2, 1, 0, 3).reshape(IC, 128, 4096).astype(np.int32))
    wd_h = cc(np.asarray(L0_ffn_w_down).T.reshape(IC, 128, HG, 1024)
              .transpose(2, 0, 1, 3).astype(np.int32))

    base = dict(
        winT=winT_h, w1T=w1T_h, w2T=w2T_h, woT=woT_h, wg_i=wg_h, wu_i=wu_h, wd_i=wd_h,
        p_bin=pc(L0_attn_bin, C4), p_l1w=pc_l(L0_attn_ln1_w), p_l1b=pc_l(L0_attn_ln1_b),
        p_l2w=pc_l(L0_attn_ln2_w), p_l2b=pc_l(L0_attn_ln2_b), p_scl=pc_l(L0_attn_sc_lin),
        p_phr=pc_l(L0_attn_phz_re), p_phi=pc_l(L0_attn_phz_im),
        p_lcr=pc_l(L0_attn_lc_re), p_lci=pc_l(L0_attn_lc_im),
        p_b2=pc_l(L0_attn_b2),
        p_lnlw=pc(L0_attn_lnl_w, C4), p_lnlb=pc(L0_attn_lnl_b, C4),
        p_b1=cc(np.asarray(L0_attn_b1, f32).reshape(NLAY, FC16, 128)
                .transpose(2, 0, 1).reshape(128, NLAY * FC16)),
        p_sg=pc(L0_ffn_s_gate, IC), p_su=pc(L0_ffn_s_up, IC), p_sd=pc(L0_ffn_s_down, 32),
        p_eps=np.full((128, 1), EPS, f32),
        ident_b=np.eye(128, dtype=ml_dtypes.bfloat16),
        ident_f=np.eye(128, dtype=f32),
        om=np.full((128, 128), 1.0 / DIN, f32),
        bout=cc(np.asarray(L0_attn_bout, f32).reshape(1, V)),
        onesr=np.ones((1, 512), ml_dtypes.bfloat16),
    )
    x = np.asarray(L0_attn_x, f32)
    in_maps = []
    for core in range(NCORES):
        b, half = core // 2, core % 2
        m = dict(base)
        m["x"] = cc(x[b, half * TOK:(half + 1) * TOK, :])
        m["p_mv"] = np.full((128, 1), float(half), f32)
        m["p_mb"] = np.full((128, 1), 1.0 - float(half), f32)
        selv = np.zeros((8, 1), f32)
        selv[core - 1 if half == 1 else core] = 1.0
        m["sel"] = selv
        in_maps.append(m)
    return in_maps


_NC_CACHE = {}


def kernel(**inputs):
    if "main" not in _NC_CACHE:
        _NC_CACHE["main"] = build()
    nc, _ = _NC_CACHE["main"]
    in_maps = host_inputs(**inputs)
    res = run_bass_kernel_spmd(nc, in_maps, core_ids=list(range(NCORES)))
    out = np.empty((B, L, V), np.float32)
    for core in range(NCORES):
        b, half = core // 2, core % 2
        out[b, half * TOK:(half + 1) * TOK, :] = res.results[core]["out"]
    return out
